# version 18
# baseline (speedup 1.0000x reference)
import numpy as np
import concourse.bass as bass
import concourse.mybir as mybir
from concourse import dve_ops as _dve_ops
from concourse.bass_utils import run_bass_kernel_spmd
from contextlib import ExitStack

F32 = mybir.dt.float32
BF16 = mybir.dt.bfloat16
AF = mybir.ActivationFunctionType
OP = mybir.AluOpType

B, T, D = 4, 4096, 1024
NCORES = 8
TOK = 2048
NT = 16
NCH = 8
M, INNER = 64, 16
EPS_LN = 1e-5
EPS_Z = 1e-6


DMA_SEMS = {"LD", "GD", "ST", "ST2", "G0", "G1", "SD", "TD", "VD"}


class Op:
    __slots__ = ("eng", "fn", "deps", "sem", "inc", "done", "tag", "new_group",
                 "self_wait")
    def __init__(self, eng, fn, deps, sem, inc, tag="", new_group=False):
        self.eng, self.fn, self.deps, self.sem, self.inc = eng, fn, deps, sem, inc
        self.done = None
        self.tag = tag
        self.new_group = new_group
        self.self_wait = None


class Sched:
    def __init__(self):
        self.ops = []
        self.counts = {}
        self.group_end = {}
        self.last_done = {}

    def add(self, eng, fn, deps=(), sem=None, inc=0, tag="", new_group=False):
        deps = [d for d in deps if d is not None]
        if eng in ("vector", "scalar", "tensor") and eng in self.last_done:
            deps.append(self.last_done[eng])
        op = Op(eng, fn, deps, sem, inc, tag, new_group)
        self.ops.append(op)
        if sem is not None and inc:
            v = self.counts.get(sem, 0) + inc
            self.counts[sem] = v
            op.done = (sem, v)
            if eng in ("vector", "scalar", "tensor") and sem not in DMA_SEMS:
                self.last_done[eng] = op.done
        return op.done

    def finalize(self):
        for sem in DMA_SEMS:
            sem_ops = [op for op in self.ops if op.sem == sem]
            if not sem_ops:
                continue
            groups = []
            for i, op in enumerate(sem_ops):
                if i == 0 or op.new_group:
                    groups.append([])
                groups[-1].append(op)
            running = 0
            for g in groups:
                start = running
                for op in g:
                    running += op.inc
                for op in g:
                    self.group_end[(sem, op.done[1])] = running
                if start > 0:
                    g[0].self_wait = start

    def emit(self, eng_name, proxy, sems):
        if eng_name == "tensor":
            real = proxy

            class _Shim:
                def matmul(self, *a, **k):
                    TE_TRACE.append(self._tag)
                    return real.matmul(*a, **k)

                def transpose(self, *a, **k):
                    TE_TRACE.append(self._tag)
                    return real.transpose(*a, **k)

                def __getattr__(self, n):
                    return getattr(real, n)

            shim = _Shim()
        waited = {}
        for op in self.ops:
            if op.eng != eng_name:
                continue
            if eng_name == "tensor":
                shim._tag = op.tag
                proxy = shim
            if op.self_wait is not None:
                if waited.get(op.sem, -1) < op.self_wait:
                    proxy.wait_ge(sems[op.sem], op.self_wait)
                    waited[op.sem] = op.self_wait
            for (s, v) in op.deps:
                if s in DMA_SEMS:
                    v = self.group_end[(s, v)]
                if waited.get(s, -1) >= v:
                    continue
                proxy.wait_ge(sems[s], v)
                waited[s] = v
            ins = op.fn(proxy)
            if op.done is not None:
                assert ins is not None, op.tag
                ins.then_inc(sems[op.sem], op.inc)


_DEBUG_TAPS = False
TE_TRACE = []


def build_nc():
    nc = bass.Bass("TRN2", target_bir_lowering=False, debug=False,
                   dynamic_dma_scratch_size=16384)

    def din(name, shape):
        return nc.dram_tensor(name, list(shape), F32, kind="ExternalInput").ap()

    x_tm = din("x_tm", (TOK, D))
    x_fm = din("x_fm", (D, TOK))
    wqT = din("wqT", (D, D)); wkT = din("wkT", (D, D))
    wvT = din("wvT", (D, D)); wcT = din("wcT", (D, D))
    wc_nat = din("wc_nat", (D, D))
    win = din("win", (D, M)); wout = din("wout", (M, D))
    bdw2 = din("bdw2", (NCH, 128, M))
    selS = din("selS", (NCH, M, 128))
    rep_bk = din("rep_bk", (128, D)); rep_bv = din("rep_bv", (128, D))
    bq_col = din("bq_col", (128, NCH)); bc_col = din("bc_col", (128, NCH))
    brows = din("brows", (1, 2 * D))
    bin_col = din("bin_col", (M, 1)); b2_col = din("b2_col", (M, 1))
    w1_col = din("w1_col", (128, NCH)); b1_col = din("b1_col", (128, NCH))
    rsel = din("rsel", (32, 32, M))
    blockmask = din("blockmask", (128, 128))
    ident = din("ident", (128, 128))

    out_tm = nc.dram_tensor("out_tm", [TOK, D], F32, kind="ExternalOutput").ap()

    cc_in = nc.dram_tensor("cc_in", [128, 1032], F32, kind="Internal").ap()
    cc_out = nc.dram_tensor("cc_out", [128, 1032], F32, kind="Internal").ap()

    ctx = ExitStack()
    e = ctx.enter_context

    A1 = e(nc.sbuf_tensor("A1", [128, 16384], BF16))
    A2 = e(nc.sbuf_tensor("A2", [128, 16384], BF16))
    A3 = e(nc.sbuf_tensor("A3", [128, 16384], BF16))
    A4 = e(nc.sbuf_tensor("A4", [128, 16384], BF16))
    W0 = e(nc.sbuf_tensor("W0", [128, 8192], BF16))
    W1 = e(nc.sbuf_tensor("W1", [128, 8192], BF16))
    SCR = e(nc.sbuf_tensor("SCR", [128, 2048], F32))
    RB0 = e(nc.sbuf_tensor("RB0", [128, 1024], F32))
    RB1 = e(nc.sbuf_tensor("RB1", [128, 1024], F32))
    s_x = e(nc.sbuf_tensor("s_x", [64, 2048], F32))
    kv_b = e(nc.sbuf_tensor("kv_b", [128, 1024], BF16))
    rsel_f = e(nc.sbuf_tensor("rsel_f", [32, 2048], F32))
    bcb = e(nc.sbuf_tensor("bcb", [128, 8], BF16))
    wout_b = e(nc.sbuf_tensor("wout_b", [64, 1024], BF16))
    win_b = e(nc.sbuf_tensor("win_b", [128, 512], BF16))
    bdw2_b = e(nc.sbuf_tensor("bdw2_b", [128, 512], BF16))
    wcw_b = e(nc.sbuf_tensor("wcw_b", [128, 512], BF16))
    selS_b = e(nc.sbuf_tensor("selS_b", [64, 1024], BF16))
    bmask_s = e(nc.sbuf_tensor("bmask_s", [128, 128], BF16))
    ident_s = e(nc.sbuf_tensor("ident_s", [128, 128], F32))
    brows_b = e(nc.sbuf_tensor("brows_b", [1, 2048], BF16))
    ones_r = e(nc.sbuf_tensor("ones_r", [1, 128], BF16))
    ones_c = e(nc.sbuf_tensor("ones_c", [128, 1], BF16))
    cols = e(nc.sbuf_tensor("cols", [128, 64], F32))
    colsB = e(nc.sbuf_tensor("colsB", [128, 96], F32))
    LW = e(nc.sbuf_tensor("LW", [128, 96], F32))
    LW2 = e(nc.sbuf_tensor("LW2", [128, 96], F32))
    mcols = e(nc.sbuf_tensor("mcols", [64, 8], F32))
    zmz = e(nc.sbuf_tensor("zmz", [128, 32], F32))

    P = [e(nc.psum_tensor(f"P{i}", [128, 512], F32)) for i in range(8)]

    sems = {}
    for s in ["LD", "GD", "TE", "VE", "SE", "CC", "ST", "ST2", "G0", "G1",
              "SD", "TD", "VD"]:
        sems[s] = e(nc.semaphore(s))

    S = Sched()

    def f32_tile(arena, slot):
        return arena[:, 2048 * slot:2048 * (slot + 1)].bitcast(F32)

    def r1_t(t):  return f32_tile(A3, t) if t < 8 else f32_tile(A4, t - 8)
    def x2_t(t):  return f32_tile(A1, t) if t < 8 else f32_tile(A3, t - 8)
    def r2_t(t):  return f32_tile(A4, t) if t < 8 else f32_tile(A1, t - 8)

    den_v = A2[0:16, 0:4096].bitcast(F32)
    zrec_v = A2[0:16, 4096:8192].bitcast(F32)
    exch_v = A2[:, 8192:10240].bitcast(F32)
    exks_v = A2[:, 10240:10256].bitcast(F32)
    z1rep = W0[0:64, 0:4096].bitcast(F32)
    mz1rep = W0[0:64, 4096:8192].bitcast(F32)
    xin = [RB0[:, :], RB1[:, :]]
    ktb = SCR[:, 0:1024]
    e2 = SCR[:, 1024:2048]
    sqscr = SCR[:, 0:1024].bitcast(BF16)
    t1 = SCR[0:64, 1024:1536]
    t2 = SCR[0:64, 1536:2048]
    h_b = [SCR[:, 1024 + 256 * i:1024 + 256 * (i + 1)].bitcast(BF16) for i in range(4)]
    outst = [A2[:, 4096:6144].bitcast(F32), A2[:, 6144:8192].bitcast(F32),
             A2[:, 12288:14336].bitcast(F32), A2[:, 14336:16384].bitcast(F32)]
    s_b = A2[0:64, 0:2048]
    g_b = A2[0:64, 2048:4096]
    zwh_b = A2[0:32, 4096:4224]
    zwlo_b = A2[0:32, 4224:4352]
    zwf = A2[0:32, 4352:4608].bitcast(F32)
    A1F = A1[:, :].bitcast(F32)
    zrh_b = A1[0:16, 8192:10240]
    zrl_b = A1[0:16, 10240:12288]
    zhi_f = A1F[0:16, 2048:4096]
    zrep = [A1F[:, 4096 + 512 * s:4096 + 512 * (s + 1)] for s in range(8)]

    C_bq = cols[:, 0:8]; C_negbq = cols[:, 8:16]; C_bq1 = cols[:, 16:24]
    C_bc = cols[:, 24:32]; C_w1 = cols[:, 32:40]; C_b1 = cols[:, 40:48]
    C_sum2b = cols[:, 48:64]
    C_sum1a = colsB[:, 0:16]; C_sum1b = colsB[:, 16:32]; C_ssq1 = colsB[:, 32:48]
    C_m1 = colsB[:, 48:64]; C_sum2a = colsB[:, 64:80]; C_ssq2 = colsB[:, 80:96]
    L_msq = LW[:, 0:16]; L_var = LW[:, 16:32]; L_rinv = LW[:, 32:48]
    L_rstd1 = LW[:, 48:64]; L_mrs1 = LW[:, 64:80]; L_sum = LW[:, 80:96]
    L_rstd2 = LW2[:, 0:16]; L_mrs2 = LW2[:, 16:32]; L_m2 = LW2[:, 32:48]
    L_w2 = LW2[:, 48:64]; L_mrs2n = LW2[:, 64:80]
    M_bin = mcols[:, 0:1]; M_b2 = mcols[:, 1:2]; M_bcw = mcols[:, 2:3]; M_cs = mcols[:, 3:4]

    ld = {}
    def q_load(name, dst, src, eng="sync", sem="LD", new_group=False, deps=()):
        ld[name] = S.add(eng, lambda p, d=dst, s=src: p.dma_start(out=d, in_=s),
                         deps=deps, sem=sem, inc=16, tag=f"ld.{name}",
                         new_group=new_group)

    stA = A4[:, 8192:16384].bitcast(F32)
    stB = A2[:, 8192:16384].bitcast(F32)
    xstage = A3[:, :].bitcast(F32)
    scr_f = SCR[:, :]
    xb_done = []
    wk_done = []
    wv_done = []

    q_load("ident", ident_s[:, :], ident, eng="gpsimd", sem="GD")
    q_load("bq", C_bq, bq_col, eng="gpsimd", sem="GD")
    q_load("bc", C_bc, bc_col, eng="gpsimd", sem="GD")
    q_load("w1", C_w1, w1_col, eng="gpsimd", sem="GD")
    q_load("b1", C_b1, b1_col, eng="gpsimd", sem="GD")
    q_load("bin", M_bin, bin_col, eng="gpsimd", sem="GD")
    q_load("b2", M_b2, b2_col, eng="gpsimd", sem="GD")

    xlds = [None] * NCH
    xb_done = [None] * NCH

    def emit_xld(c, eng, sem, deps, new_group):
        j = c % 4
        xlds[c] = S.add(eng, lambda p, c=c, j=j: p.dma_start(
            out=xstage[:, 2048 * j:2048 * (j + 1)],
            in_=x_fm[128 * c:128 * (c + 1), :]),
            deps=deps, sem=sem, inc=16, tag=f"xbld{c}", new_group=new_group)

    for c in range(4):
        emit_xld(c, "sync", "LD", [], c % 2 == 0)
    for c in range(4):
        xb_done[c] = S.add("vector", lambda p, c=c: p.tensor_copy(
            A1[:, 2048 * c:2048 * (c + 1)],
            xstage[:, 2048 * c:2048 * (c + 1)]),
            deps=[xlds[c]], sem="VE", inc=1, tag=f"xbcast{c}")
    emit_xld(4, "sync", "LD", [xb_done[0]], True)
    emit_xld(5, "sync", "LD", [xb_done[1]], False)

    wk_slots = [stA[:, 1024 * i:1024 * (i + 1)] for i in range(4)] + \
               [scr_f[:, 0:1024], scr_f[:, 1024:2048], RB0[:, :], RB1[:, :]]
    wk_lds = []
    for c in range(NCH):
        wk_lds.append(S.add("scalar", lambda p, c=c: p.dma_start(
            out=wk_slots[c], in_=wkT[128 * c:128 * (c + 1), :]),
            sem="SD", inc=16, tag=f"wkld{c}", new_group=(c == 0)))
    for c in range(NCH):
        wk_done.append(S.add("scalar", lambda p, c=c: p.copy(
            W0[:, 1024 * c:1024 * (c + 1)], wk_slots[c]),
            deps=[wk_lds[c]], sem="SE", inc=1, tag=f"wkcast{c}"))

    emit_xld(6, "scalar", "SD", [xb_done[2]], True)
    emit_xld(7, "scalar", "SD", [xb_done[3]], False)

    wv_slots = [stB[:, 1024 * i:1024 * (i + 1)] for i in range(4)] + \
               [stA[:, 1024 * i:1024 * (i + 1)] for i in range(4)]
    wv_lds = []
    for c in range(NCH):
        deps = [wk_done[c - 4]] if c >= 4 else []
        wv_lds.append(S.add("scalar", lambda p, c=c: p.dma_start(
            out=wv_slots[c], in_=wvT[128 * c:128 * (c + 1), :]),
            deps=deps, sem="SD", inc=16, tag=f"wvld{c}", new_group=(c == 0)))
    def emit_wv_casts():
        for c in range(NCH):
            wv_done.append(S.add("scalar", lambda p, c=c: p.copy(
                W1[:, 1024 * c:1024 * (c + 1)], wv_slots[c]),
                deps=[wv_lds[c]], sem="SE", inc=1, tag=f"wvcast{c}"))

    for c in range(4, NCH):
        xb_done[c] = S.add("vector", lambda p, c=c: p.tensor_copy(
            A1[:, 2048 * c:2048 * (c + 1)],
            xstage[:, 2048 * (c % 4):2048 * (c % 4 + 1)]),
            deps=[xlds[c]], sem="VE", inc=1, tag=f"xbcast{c}")

    q_load("repbk", RB0[:, :], rep_bk, new_group=True,
           deps=[wk_done[6], wk_done[7]])
    q_load("repbv", RB1[:, :], rep_bv)

    gd = {}
    def cast_load(name, dst, src, deps=(), new_group=False):
        gd[name] = S.add("gpsimd", lambda p, d=dst, s=src: p.dma_start(out=d, in_=s),
                         deps=deps, sem="GD", inc=16, tag=f"cl.{name}",
                         new_group=new_group)

    for c in range(NCH):
        cast_load(f"wcn{c}", A4[:, 1024 * c:1024 * (c + 1)],
                  wc_nat[128 * c:128 * (c + 1), :], new_group=(c == 0),
                  deps=[wv_lds[-1]] if c == 0 else ())
    cast_load("win", win_b[:, :].rearrange("p (c m) -> p c m", c=NCH),
              win.rearrange("(c p) m -> p c m", p=128))
    cast_load("wout", wout_b[:, :], wout, new_group=True)
    cast_load("bdw2", bdw2_b[:, :].rearrange("p (c m) -> p c m", c=NCH),
              bdw2.rearrange("c p m -> p c m"))
    cast_load("selS", selS_b[:, :].rearrange("m (c p) -> m c p", c=NCH),
              selS.rearrange("c m p -> m c p"))
    cast_load("brows", brows_b[:, :], brows)
    cast_load("bmask", bmask_s[:, :], blockmask)
    cast_load("rsel", rsel_f[:, :].rearrange("a (r m) -> a r m", r=32),
              rsel.rearrange("r a m -> a r m"))
    ld["bmask"] = gd["bmask"]

    m_ones_r = S.add("vector", lambda p: p.memset(ones_r[:, :], 1.0), sem="VE", inc=1)
    m_ones_c = S.add("vector", lambda p: p.memset(ones_c[:, :], 1.0), sem="VE", inc=1)
    eps_col = LW2[:, 80:81]
    m_eps = S.add("vector", lambda p: p.memset(eps_col, EPS_Z), sem="VE", inc=1)
    d_negbq = S.add("vector", lambda p: p.tensor_scalar_mul(C_negbq, C_bq, -1.0),
                    deps=[ld["bq"]], sem="VE", inc=1)
    d_bq1 = S.add("vector", lambda p: p.tensor_scalar_add(C_bq1, C_bq, 1.0),
                  deps=[ld["bq"]], sem="VE", inc=1)
    d_bcc = S.add("vector", lambda p: p.tensor_copy(bcb[:, :], C_bc),
                  deps=[ld["bc"]], sem="VE", inc=1)

    def mm_group(p, outs, pairs):
        ins = None
        n = len(pairs)
        for i, (l, r) in enumerate(pairs):
            ins = p.matmul(outs, l, r, start=(i == 0), stop=(i == n - 1))
        return ins

    wk_all = wk_done
    wv_all = wv_done
    ktil = [None] * NT
    vtil = [None] * NT
    k_mm = [None] * NT
    v_mm = [None] * NT
    prev_kev = {}
    prev_vev = {}
    KAHEAD = 4
    def emit_kproj(t):
        pk = (P[0], P[1]) if t % 2 == 0 else (P[4], P[5])
        deps_k = xb_done + wk_all
        if t % 2 in prev_kev:
            deps_k = deps_k + [prev_kev[t % 2]]
        def kproj(p, t=t, pk=pk):
            ins = None
            for jh in range(2):
                ins = mm_group(p, pk[jh][:, :],
                               [(A1[:, 2048 * kc + 128 * t:2048 * kc + 128 * (t + 1)],
                                 W0[:, 1024 * kc + 512 * jh:1024 * kc + 512 * (jh + 1)])
                                for kc in range(NCH)])
            return ins
        k_mm[t] = S.add("tensor", kproj, deps=deps_k, sem="TE", inc=1, tag=f"kproj{t}")
        def k_tb(p, t=t, pk=pk):
            p.tensor_tensor(ktb[:, 0:512], pk[0][:, :], RB0[:, 0:512], OP.add)
            return p.tensor_tensor(ktb[:, 512:1024], pk[1][:, :], RB0[:, 512:1024], OP.add)
        tb_deps = [k_mm[t], ld["repbk"]]
        if t == 0:
            tb_deps += [wk_done[4], wk_done[5]]
        d_tb = S.add("vector", k_tb, deps=tb_deps, sem="VE", inc=1,
                     tag=f"ktb{t}")
        d_r = S.add("scalar", lambda p: p.activation(e2[:, :], ktb[:, :], AF.Relu,
                                                     scale=-1.0),
                    deps=[d_tb, ktil[t - 1] if t > 0 else None], sem="SE", inc=1)
        d_e = S.add("scalar", lambda p: p.activation(e2[:, :], e2[:, :], AF.Exp,
                                                     scale=-1.0),
                    deps=[d_r], sem="SE", inc=1)
        kdeps = [d_e] + ([wv_done[(t - 8) // 2]] if t >= 8 else [])
        ktil[t] = S.add("vector", lambda p, t=t: p.scalar_tensor_tensor(
            A2[:, 1024 * t:1024 * (t + 1)], ktb[:, :], 1.0, e2[:, :], OP.add, OP.max),
            deps=kdeps, sem="VE", inc=1, tag=f"ktil{t}")
        prev_kev[t % 2] = ktil[t]

    def emit_vproj(t):
        pv = (P[2], P[3]) if t % 2 == 0 else (P[6], P[7])
        deps_v = xb_done + wv_all
        if t % 2 in prev_vev:
            deps_v = deps_v + [prev_vev[t % 2]]
        def vproj(p, t=t, pv=pv):
            ins = None
            for jh in range(2):
                ins = mm_group(p, pv[jh][:, :],
                               [(A1[:, 2048 * kc + 128 * t:2048 * kc + 128 * (t + 1)],
                                 W1[:, 1024 * kc + 512 * jh:1024 * kc + 512 * (jh + 1)])
                                for kc in range(NCH)])
            return ins
        v_mm[t] = S.add("tensor", vproj, deps=deps_v, sem="TE", inc=1, tag=f"vproj{t}")
        def v_ev(p, t=t, pv=pv):
            p.tensor_tensor(A3[:, 1024 * t:1024 * t + 512], pv[0][:, :],
                            RB1[:, 0:512], OP.add)
            return p.tensor_tensor(A3[:, 1024 * t + 512:1024 * (t + 1)], pv[1][:, :],
                                   RB1[:, 512:1024], OP.add)
        vtil[t] = S.add("vector", v_ev, deps=[v_mm[t], ld["repbv"]], sem="VE", inc=1,
                        tag=f"vtil{t}")
        prev_vev[t % 2] = vtil[t]

    for t in range(6):
        emit_kproj(t)
    emit_wv_casts()
    emit_vproj(0); emit_vproj(1)
    kq, vq = 6, 2
    while kq < NT or vq < NT:
        if kq < NT:
            emit_kproj(kq); kq += 1
        if vq < NT:
            emit_vproj(vq); vq += 1

    wq_stage = A4[:, 8192:16384].bitcast(F32)
    wq_all = []
    prev_cast = None
    for b in range(2):
        lds = []
        for i in range(4):
            c = 4 * b + i
            lds.append(S.add("sync", lambda p, c=c, i=i: p.dma_start(
                out=wq_stage[:, 1024 * i:1024 * (i + 1)],
                in_=wqT[128 * c:128 * (c + 1), :]),
                deps=[prev_cast] if prev_cast else [wv_done[-1]],
                sem="LD", inc=16, tag=f"wqld{c}", new_group=(i == 0)))
        for i in range(4):
            c = 4 * b + i
            prev_cast = S.add("scalar", lambda p, c=c, i=i: p.copy(
                W0[:, 1024 * c:1024 * (c + 1)], wq_stage[:, 1024 * i:1024 * (i + 1)]),
                deps=[lds[i], k_mm[NT - 1]], sem="SE", inc=1, tag=f"wqcast{c}")
            wq_all.append(prev_cast)
    wc_all = []
    for c in range(NCH):
        wc_all.append(S.add("gpsimd", lambda p, c=c: p.dma_start(
            out=W1[:, 1024 * c:1024 * (c + 1)], in_=wcT[128 * c:128 * (c + 1), :]),
            deps=[v_mm[NT - 1]], sem="GD", inc=16, tag=f"wc{c}",
            new_group=(c == 0)))

    kv_deps_base = ktil + vtil
    kv_last = None
    for pr in range(8):
        po = P[4 + pr // 4]
        deps = list(kv_deps_base) if pr == 0 else [kv_last]
        kv_last = S.add("tensor", lambda p, pr=pr, po=po: mm_group(
            p, po[:, 128 * (pr % 4):128 * (pr % 4 + 1)],
            [(A2[:, 1024 * t + 128 * pr:1024 * t + 128 * (pr + 1)],
              A3[:, 1024 * t + 128 * pr:1024 * t + 128 * (pr + 1)])
             for t in range(NT)]),
            deps=deps, sem="TE", inc=1, tag=f"kv{pr}")
    ks_row = [None, None]
    for h in range(2):
        po = (P[6], P[7])[h]
        ks_row[h] = S.add("tensor", lambda p, h=h, po=po: mm_group(
            p, po[0:1, :],
            [(ones_c[:, :], A2[:, 1024 * t + 512 * h:1024 * t + 512 * (h + 1)])
             for t in range(NT)]),
            deps=[m_ones_c, kv_last], sem="TE", inc=1, tag=f"ksr{h}")
    M_kv = ks_row[1]
    ksrow_s = SCR[0:1, 0:1024]
    ev_ksr = []
    for h in range(2):
        po = (P[6], P[7])[h]
        ev_ksr.append(S.add("vector", lambda p, h=h, po=po: p.tensor_copy(
            ksrow_s[:, 512 * h:512 * (h + 1)], po[0:1, :]),
            deps=[ks_row[h]], sem="VE", inc=1, tag=f"ksrev{h}"))
    tr_last = None
    for c in range(NCH):
        tr_last = S.add("tensor", lambda p, c=c: p.transpose(
            P[6][:, c:c + 1], ksrow_s[:, 128 * c:128 * (c + 1)],
            ident_s[0:1, 0:1]),
            deps=[ev_ksr[0], ev_ksr[1], ld["ident"]], sem="TE", inc=1,
            tag=f"kstr{c}")

    ev_kv1 = S.add("scalar", lambda p: p.copy(exch_v[:, 0:512], P[4][:, :]),
                   deps=[M_kv], sem="SE", inc=1)
    ev_kv2 = S.add("scalar", lambda p: p.copy(exch_v[:, 512:1024], P[5][:, :]),
                   deps=[M_kv], sem="SE", inc=1)
    ev_ks = S.add("scalar", lambda p: p.copy(exks_v[:, :], P[6][:, 0:8]),
                  deps=[tr_last], sem="SE", inc=1)

    d_xo1 = S.add("gpsimd", lambda p: p.dma_start(out=cc_in[:, 0:1024],
                                                  in_=exch_v[:, :]),
                  deps=[ev_kv1, ev_kv2], sem="GD", inc=16, tag="xout1",
                  new_group=True)
    d_xo2 = S.add("gpsimd", lambda p: p.dma_start(out=cc_in[:, 1024:1032],
                                                  in_=exks_v[:, :]),
                  deps=[ev_ks], sem="GD", inc=16, tag="xout2")
    d_cc = S.add("gpsimd", lambda p: p.collective_compute(
        "AllReduce", OP.add, replica_groups=[[0, 1], [2, 3], [4, 5], [6, 7]],
        ins=[cc_in], outs=[cc_out]),
        deps=[d_xo1, d_xo2], sem="CC", inc=1, tag="allreduce")
    d_xi1 = S.add("gpsimd", lambda p: p.dma_start(out=exch_v[:, :],
                                                  in_=cc_out[:, 0:1024]),
                  deps=[d_cc], sem="GD", inc=16, tag="xin1",
                  new_group=True)
    d_xi2 = S.add("gpsimd", lambda p: p.dma_start(out=exks_v[:, :],
                                                  in_=cc_out[:, 1024:1032]),
                  deps=[d_cc], sem="GD", inc=16, tag="xin2")


    wcn_deps = [gd[f"wcn{c}"] for c in range(NCH)] + [gd["win"]]
    wcw_ev = []
    for cc2 in range(NCH):
        po = P[cc2 % 2]
        dn = S.add("tensor", lambda p, cc2=cc2, po=po: mm_group(
            p, po[:, 0:64],
            [(A4[:, 1024 * jc + 128 * cc2:1024 * jc + 128 * (cc2 + 1)],
              win_b[:, 64 * jc:64 * (jc + 1)]) for jc in range(NCH)]),
            deps=(wcn_deps + [ktil[NT - 2]] if cc2 < 2 else [wcw_ev[cc2 - 2]]),
            sem="TE", inc=1, tag=f"wcw{cc2}")
        ev = S.add("vector", lambda p, cc2=cc2, po=po: p.tensor_copy(
            wcw_b[:, 64 * cc2:64 * (cc2 + 1)], po[:, 0:64]),
            deps=[dn], sem="VE", inc=1, tag=f"wcwev{cc2}")
        wcw_ev.append(ev)
    M_wcw = wcw_ev[-1]

    d_bcw = S.add("tensor", lambda p: mm_group(
        p, P[7][0:64, 0:1],
        [(win_b[:, 64 * jc:64 * (jc + 1)], bcb[:, jc:jc + 1]) for jc in range(NCH)]),
        deps=[d_bcc, gd["win"], vtil[NT - 1], ev_ksr[1]], sem="TE", inc=1,
        tag="bcwin")
    d_cs = S.add("tensor", lambda p: mm_group(
        p, P[7][0:64, 1:2],
        [(win_b[:, 64 * jc:64 * (jc + 1)], ones_c[:, :]) for jc in range(NCH)]),
        deps=[m_ones_c], sem="TE", inc=1, tag="cs")
    ev_bcw = S.add("vector", lambda p: p.tensor_copy(M_bcw, P[7][0:64, 0:1]),
                   deps=[d_bcw], sem="VE", inc=1)
    ev_cs = S.add("vector", lambda p: p.tensor_copy(M_cs, P[7][0:64, 1:2]),
                  deps=[d_cs], sem="VE", inc=1)


    q_mm = [[None] * 4 for _ in range(NCH)]
    qtil = [[None] * 4 for _ in range(NCH)]
    prev_q = {}
    for c in range(NCH):
        for th in range(4):
            slot = (c * 4 + th) % 4
            po = P[slot]
            deps = wq_all + [ktil[NT - 2], vtil[NT - 2]]
            if slot < 2:
                deps = deps + [M_wcw]
            if slot in prev_q:
                deps = [wq_all[-1], prev_q[slot]]
            dn = S.add("tensor", lambda p, c=c, th=th, po=po: mm_group(
                p, po[:, :],
                [(W0[:, 1024 * kc + 128 * c:1024 * kc + 128 * (c + 1)],
                  A1[:, 2048 * kc + 512 * th:2048 * kc + 512 * (th + 1)])
                 for kc in range(NCH)]),
                deps=deps, sem="TE", inc=1, tag=f"q{c}.{th}")
            q_mm[c][th] = dn
            prev_qtil = qtil[c][th - 1] if th > 0 else (qtil[c - 1][3] if c > 0
                                                        else ktil[NT - 1])
            d_r = S.add("scalar", lambda p, c=c, po=po: p.activation(
                e2[:, 0:512], po[:, :], AF.Relu, bias=C_negbq[:, c:c + 1], scale=-1.0),
                deps=[dn, d_negbq, prev_qtil], sem="SE", inc=1)
            d_e = S.add("scalar", lambda p: p.activation(
                e2[:, 0:512], e2[:, 0:512], AF.Exp, scale=-1.0),
                deps=[d_r], sem="SE", inc=1)
            depq = [d_e, d_bq1]
            if c == 0 and th == 0:
                depq += [M_wcw]
            d_q = S.add("vector", lambda p, c=c, th=th, po=po: p.scalar_tensor_tensor(
                A4[:, 2048 * c + 512 * th:2048 * c + 512 * (th + 1)],
                po[:, :], C_bq1[:, c:c + 1], e2[:, 0:512], OP.add, OP.max),
                deps=depq, sem="VE", inc=1, tag=f"qtil{c}.{th}")
            qtil[c][th] = d_q
            prev_q[slot] = d_q

    if _DEBUG_TAPS == 4:
        st_total = 0
        for name, src, dt in [("d_qt", A4[:, :], BF16), ("d_kt", A2[:, :], BF16),
                              ("d_vt", A3[:, :], BF16), ("d_xb", A1[:, :], BF16)]:
            dst = nc.dram_tensor(name, list(src.shape), dt,
                                 kind="ExternalOutput").ap()
            S.add("sync", lambda p, dd=dst, ss=src: p.dma_start(out=dd, in_=ss),
                  deps=[("VE", S.counts["VE"]), ("TE", S.counts["TE"]),
                        ("SE", S.counts["SE"]), ("GD", S.counts["GD"])],
                  sem="ST", inc=16, tag=f"tap4.{name}")
            st_total += 16
        S.add("sync", lambda p: p.dma_start(out=out_tm[0:128, :], in_=RB0[:, :]),
              deps=[ld["repbk"]], sem="ST", inc=16)
        st_total += 16
        S.add("sync", lambda p: (p.wait_ge(sems["ST"], st_total),
                             p.wait_ge(sems["ST2"], st2_total)) and None,
          tag="final")
        S.finalize()
        with nc.Block() as block:
            block.sync(lambda p: S.emit("sync", p, sems))
            block.gpsimd(lambda p: S.emit("gpsimd", p, sems))
            block.tensor(lambda p: S.emit("tensor", p, sems))
            block.vector(lambda p: S.emit("vector", p, sems))
            block.scalar(lambda p: S.emit("scalar", p, sems))
        ctx.close()
        return nc

    S.add("scalar", lambda p: p.activation(zmz[0:1, 0:1], ones_r[0:1, 0:1], AF.Sqrt),
          deps=[m_ones_r], sem="SE", inc=1, tag="sqrt_preload")

    sx_ev = []
    sx_mm = []
    prev_sx = None
    for ts in range(4):
        po = P[2] if ts % 2 == 0 else P[3]
        deps = xb_done + [gd["win"], qtil[NCH - 1][2 + ts % 2]]
        if prev_sx:
            deps = deps + [prev_sx]
        dn = S.add("tensor", lambda p, ts=ts, po=po: mm_group(
            p, po[0:64, :],
            [(win_b[:, 64 * kc:64 * (kc + 1)],
              A1[:, 2048 * kc + 512 * ts:2048 * kc + 512 * (ts + 1)])
             for kc in range(NCH)]),
            deps=deps, sem="TE", inc=1, tag=f"sx{ts}")
        sx_mm.append(dn)
        ev = S.add("vector", lambda p, ts=ts, po=po: p.tensor_copy(
            s_x[:, 512 * ts:512 * (ts + 1)], po[0:64, :]),
            deps=[dn], sem="VE", inc=1, tag=f"sxev{ts}")
        prev_sx = ev
        sx_ev.append(ev)


    M_sx = sx_mm[3]
    M_q = q_mm[NCH - 1][3]

    d_kvb = S.add("vector", lambda p: p.tensor_tensor(
        kv_b[:, :].rearrange("p (a b) -> p a b", a=8),
        exch_v[:, :].rearrange("p (a b) -> p a b", a=8),
        bmask_s[:, :].unsqueeze(1).broadcast_to([128, 8, 128]), OP.mult),
        deps=[d_xi1, ld["bmask"]], sem="VE", inc=1, tag="kvb")
    lden128 = A1[:, 4096:5120]
    d_ld128 = []
    for c in range(NCH):
        d_ld128.append(S.add("vector", lambda p, c=c: p.tensor_scalar_mul(
            lden128[:, 128 * c:128 * (c + 1)], bmask_s[:, :],
            exks_v[:, c:c + 1]),
            deps=[d_xi2, ld["bmask"], M_q, M_sx], sem="VE", inc=1,
            tag=f"ld128.{c}"))

    if _DEBUG_TAPS == 3:
        st_total = 0
        dst_e = nc.dram_tensor("d_qt_early", [128, 16384], BF16,
                               kind="ExternalOutput").ap()
        S.add("sync", lambda p: p.dma_start(out=dst_e, in_=A4[:, :]),
              deps=[qtil[NCH - 1][3]], sem="ST", inc=16, tag="tapE")
        st_total += 16
        taps3 = [
            ("d_qt", A4[:, :], BF16),
            ("d_qh", A1[:, :], BF16),
            ("d_den", A2[0:16, 0:8192], BF16),
            ("d_lden", lden[:, :], BF16),
            ("d_exks", exks_v[:, :], F32),
            ("d_kvb", kv_b[:, :], BF16),
            ("d_w0", W0[:, :], BF16),
        ]
        for name, src, dt in taps3:
            dst = nc.dram_tensor(name, list(src.shape), dt,
                                 kind="ExternalOutput").ap()
            S.add("sync", lambda p, dd=dst, ss=src: p.dma_start(out=dd, in_=ss),
                  deps=[("VE", S.counts["VE"]), ("TE", S.counts["TE"]),
                        ("SE", S.counts["SE"]), ("GD", S.counts["GD"])],
                  sem="ST", inc=16, tag=f"tap3.{name}")
            st_total += 16
        S.add("sync", lambda p: p.dma_start(out=out_tm[0:128, :], in_=RB0[:, :]),
              deps=[ld["repbk"]], sem="ST", inc=16)
        st_total += 16
        S.add("sync", lambda p: p.wait_ge(sems["ST"], st_total) and None, tag="final")
        S.finalize()
        with nc.Block() as block:
            block.sync(lambda p: S.emit("sync", p, sems))
            block.gpsimd(lambda p: S.emit("gpsimd", p, sems))
            block.tensor(lambda p: S.emit("tensor", p, sems))
            block.vector(lambda p: S.emit("vector", p, sems))
            block.scalar(lambda p: S.emit("scalar", p, sems))
        ctx.close()
        return nc

    idx_list = [(c, th) for c in range(NCH) for th in range(4)]
    NI = len(idx_list)
    den_ops = [None] * NI
    zc0_ops = [None] * NI
    zcp_ops = [None] * NI
    yev = [None] * NI

    def emit_den(i):
        c, th = idx_list[i]
        bank = i % 4
        zp = P[4 + bank]
        s = i % 8
        zdeps = [d_ld128[c], qtil[c][th]]
        if i >= 4:
            zdeps += [zc0_ops[i - 4]]
        else:
            zdeps += [[ev_kv1], [ev_kv2], [ev_ks], [ev_bcw, ev_cs]][bank]
        den_ops[i] = S.add("tensor", lambda p, c=c, th=th, zp=zp: p.matmul(
            zp[:, :], lden128[:, 128 * c:128 * (c + 1)],
            A4[:, 2048 * c + 512 * th:2048 * c + 512 * (th + 1)],
            start=True, stop=True),
            deps=zdeps, sem="TE", inc=1, tag=f"den{c}.{th}")
        cpdeps = [den_ops[i], m_eps]
        if i >= 8:
            cpdeps += [yev[i - 8]]
        else:
            cpdeps += [M_q, M_sx]
        zc0_ops[i] = S.add("scalar", lambda p, s=s, zp=zp: p.activation(
            zrep[s], zp[:, :], AF.Identity, bias=eps_col),
            deps=cpdeps, sem="SE", inc=1, tag=f"zeps{i}")
        zcp_ops[i] = S.add("vector", lambda p, s=s: p.reciprocal(
            zrep[s], zrep[s]),
            deps=[zc0_ops[i]], sem="VE", inc=1, tag=f"zrec{i}")

    def emit_y(i):
        c, th = idx_list[i]
        slot = i % 4
        po = P[slot]
        s = i % 8
        ydeps = [qtil[c][th], d_kvb]
        if i >= 4:
            ydeps += [yev[i - 4]]
        else:
            ydeps += [qtil[NCH - 1][slot]]
            if slot >= 2:
                ydeps += [sx_ev[slot]]
        dn = S.add("tensor", lambda p, c=c, th=th, po=po: p.matmul(
            po[:, :], kv_b[:, 128 * c:128 * (c + 1)],
            A4[:, 2048 * c + 512 * th:2048 * c + 512 * (th + 1)],
            start=True, stop=True),
            deps=ydeps, sem="TE", inc=1, tag=f"y{c}.{th}")
        evdeps = [dn, zcp_ops[i], M_kv]
        if c == 5:
            evdeps += [d_ld128[7]]
        yev[i] = S.add("vector", lambda p, c=c, th=th, po=po, s=s:
            p.tensor_tensor(
                A2[:, 2048 * c + 512 * th:2048 * c + 512 * (th + 1)],
                po[:, :], zrep[s], OP.mult),
            deps=evdeps, sem="VE", inc=1, tag=f"yev{c}.{th}")

    for i in range(3):
        emit_den(i)
    for i in range(NI):
        if i + 3 < NI:
            emit_den(i + 3)
        emit_y(i)
    M_y = yev[NI - 1]
    last_zcp = {0: zc0_ops[30], 1: zc0_ops[31]}
    sattn_pwar = [zc0_ops[28], zc0_ops[29]]

    sattn_mm = []
    for ts in range(4):
        po = P[4 + ts // 2]
        reg = po[64 * (ts % 2):64 * (ts % 2) + 64, :]
        deps = [M_y, M_wcw]
        deps += sattn_pwar
        dn = S.add("tensor", lambda p, ts=ts, reg=reg: mm_group(
            p, reg,
            [(wcw_b[:, 64 * c:64 * (c + 1)],
              A2[:, 2048 * c + 512 * ts:2048 * c + 512 * (ts + 1)])
             for c in range(NCH)]),
            deps=deps, sem="TE", inc=1, tag=f"sattn{ts}")
        sattn_mm.append((dn, reg))

    if _DEBUG_TAPS == 2:
        xld0 = S.add("sync", lambda p: p.dma_start(out=xin[0],
                                                   in_=x_tm[0:128, :]),
                     deps=[ktil[NT - 1]], sem="LD", inc=16, tag="xld0")
        slots = (P[0], P[1])
        def attn_emit0(p):
            ins = None
            for jh in range(2):
                pairs = [(A2[:, 2048 * kc:2048 * kc + 128],
                          W1[:, 1024 * kc + 512 * jh:1024 * kc + 512 * (jh + 1)])
                         for kc in range(NCH)]
                pairs.append((ones_r[:, :], brows_b[0:1, 512 * jh:512 * (jh + 1)]))
                ins = mm_group(p, slots[jh][:, :], pairs)
            return ins
        at0 = S.add("tensor", attn_emit0,
                    deps=wc_all + [M_y, gd["brows"], m_ones_r, qhat[NCH - 1]],
                    sem="TE", inc=1, tag="attn0")
        ev0 = S.add("scalar", lambda p: p.copy(f32_tile(A4, 4)[:, 0:512],
                                               slots[0][:, :]),
                    deps=[at0], sem="SE", inc=1)
        ev1 = S.add("scalar", lambda p: p.copy(f32_tile(A4, 4)[:, 512:1024],
                                               slots[1][:, :]),
                    deps=[at0], sem="SE", inc=1)
        st_total = 0
        taps2 = [
            ("d_y", A2[:, :], BF16, [("SE", S.counts["SE"])]),
            ("d_qh", A1[:, :], BF16, None),
            ("d_qt", A4[:, 0:8192], BF16, None),
            ("d_w1", W1[:, :], BF16, None),
            ("d_attn0", f32_tile(A4, 4)[:, :], F32, [ev0, ev1]),
            ("d_xin0", xin[0][:, :], F32, [xld0]),
            ("d_kvb", kv_b[:, :], BF16, None),
            ("d_brows", brows_b[:, :], BF16, None),
        ]
        for name, src, dt, extra in taps2:
            dst = nc.dram_tensor(name, list(src.shape), dt,
                                 kind="ExternalOutput").ap()
            S.add("sync", lambda p, dd=dst, ss=src: p.dma_start(out=dd, in_=ss),
                  deps=[("VE", S.counts["VE"]), ("TE", S.counts["TE"])] +
                       (extra or []),
                  sem="ST", inc=16, tag=f"tap2.{name}")
            st_total += 16
        S.add("sync", lambda p: p.dma_start(out=out_tm[0:128, :], in_=xin[0][:, :]),
              deps=[xld0], sem="ST", inc=16)
        st_total += 16
        S.add("sync", lambda p: p.wait_ge(sems["ST"], st_total) and None, tag="final")
        S.finalize()
        with nc.Block() as block:
            block.sync(lambda p: S.emit("sync", p, sems))
            block.gpsimd(lambda p: S.emit("gpsimd", p, sems))
            block.tensor(lambda p: S.emit("tensor", p, sems))
            block.vector(lambda p: S.emit("vector", p, sems))
            block.scalar(lambda p: S.emit("scalar", p, sems))
        ctx.close()
        return nc

    xld = [None] * NT
    attn_mm = [None] * NT
    r1d = [None] * NT
    ssq1d = [None] * NT
    prev_at = {}
    prev_r1 = {}
    for t in range(NT):
        if t == 0:
            xdeps = [ktil[NT - 1]]
        elif t == 1:
            xdeps = [vtil[NT - 1]]
        else:
            xdeps = [prev_r1[t % 2]]
        xld[t] = S.add("sync", lambda p, t=t: p.dma_start(
            out=xin[t % 2], in_=x_tm[128 * t:128 * (t + 1), :]),
            deps=xdeps, sem="LD", inc=16, tag=f"xld{t}", new_group=True)
        slots = (P[0], P[1]) if t % 2 == 0 else (P[2], P[3])
        deps = wc_all + [M_y, gd["brows"], m_ones_r]
        if t % 2 in prev_at:
            deps += [prev_at[t % 2]]
        else:
            deps += [M_y]
        def attn_emit(p, t=t, slots=slots):
            ins = None
            for jh in range(2):
                pairs = [(A2[:, 2048 * kc + 128 * t:2048 * kc + 128 * (t + 1)],
                          W1[:, 1024 * kc + 512 * jh:1024 * kc + 512 * (jh + 1)])
                         for kc in range(NCH)]
                pairs.append((ones_r[:, :], brows_b[0:1, 512 * jh:512 * (jh + 1)]))
                ins = mm_group(p, slots[jh][:, :], pairs)
            return ins
        attn_mm[t] = S.add("tensor", attn_emit, deps=deps, sem="TE", inc=1,
                           tag=f"attn{t}")

        def r1_emit(p, t=t, slots=slots):
            p.scalar_tensor_tensor(r1_t(t)[:, 0:512], slots[0][:, :], 1.0,
                                   xin[t % 2][:, 0:512], OP.mult, OP.add,
                                   accum_out=C_sum1a[:, t:t + 1])
            return p.scalar_tensor_tensor(r1_t(t)[:, 512:1024], slots[1][:, :], 1.0,
                                          xin[t % 2][:, 512:1024], OP.mult, OP.add,
                                          accum_out=C_sum1b[:, t:t + 1])
        extra = [attn_mm[t], xld[t]]
        if t < 2:
            extra += [M_kv]
        r1d[t] = S.add("vector", r1_emit, deps=extra, sem="VE", inc=1, tag=f"r1.{t}")
        prev_r1[t % 2] = r1d[t]
        prev_at[t % 2] = r1d[t]
        ssq1d[t] = S.add("scalar", lambda p, t=t: p.activation(
            sqscr[:, 0:1024], r1_t(t)[:, :], AF.Square, accum_out=C_ssq1[:, t:t + 1]),
            deps=[r1d[t]], sem="SE", inc=1, tag=f"ssq1.{t}")

    d_s1 = S.add("vector", lambda p: p.tensor_add(L_sum, C_sum1a, C_sum1b),
                 deps=[r1d[NT - 1]], sem="VE", inc=1)
    d_m1 = S.add("vector", lambda p: p.tensor_scalar_mul(C_m1, L_sum, 1.0 / D),
                 deps=[d_s1], sem="VE", inc=1)
    d_msq = S.add("vector", lambda p: p.tensor_mul(L_msq, C_m1, C_m1),
                  deps=[d_m1], sem="VE", inc=1)
    d_var = S.add("vector", lambda p: p.scalar_tensor_tensor(
        L_var, C_ssq1, 1.0 / D, L_msq, OP.mult, OP.subtract),
        deps=[d_msq, ssq1d[NT - 1]], sem="VE", inc=1)
    d_vpe = S.add("vector", lambda p: p.tensor_scalar_add(L_var, L_var, EPS_LN),
                  deps=[d_var], sem="VE", inc=1)
    d_ri = S.add("vector", lambda p: p.reciprocal(L_rinv, L_var),
                 deps=[d_vpe], sem="VE", inc=1)
    d_rs1 = S.add("scalar", lambda p: p.activation(L_rstd1, L_rinv, AF.Sqrt),
                  deps=[d_ri], sem="SE", inc=1, tag="sqrt1")
    d_mrs1 = S.add("vector", lambda p: p.tensor_mul(L_mrs1, C_m1, L_rstd1),
                   deps=[d_rs1], sem="VE", inc=1)

    d_zmz1 = S.add("vector", lambda p: p.tensor_copy(zmz[:, 0:16], L_rstd1),
                   deps=[d_rs1], sem="VE", inc=1)
    d_zmz2 = S.add("vector", lambda p: p.tensor_copy(zmz[:, 16:32], L_mrs1),
                   deps=[d_mrs1], sem="VE", inc=1)
    d_ztr = S.add("tensor", lambda p: p.transpose(P[6][0:32, 0:128], zmz[:, :],
                                                  ident_s[:, :]),
                  deps=[d_zmz1, d_zmz2, ld["ident"], last_zcp[0]], sem="TE",
                  inc=1, tag="ztr")


    d_zwf = S.add("vector", lambda p: p.tensor_copy(zwf, P[6][0:32, 0:128]),
                  deps=[d_ztr, attn_mm[NT - 1]], sem="VE", inc=1, tag="zwf")
    d_zwlo = d_zwf

    def zbc_emit(p):
        ins = None
        for g in range(8):
            pslot = P[g // 2]
            half = 64 * (g % 2)
            for i in range(4):
                row = 4 * g + i if g < 4 else 4 * (g - 4) + i + 16
                reg = pslot[half:half + 64, 128 * i:128 * (i + 1)]
                ins = p.matmul(reg, rsel_f[0:32, 64 * row:64 * (row + 1)],
                               zwf[0:32, :], start=True, stop=True,
                               skip_group_check=True)
        return ins
    d_zbc = S.add("tensor", zbc_emit,
                  deps=[d_zwf, gd["rsel"], r1d[NT - 1]], sem="TE", inc=1,
                  tag="zbc")
    zev = []
    for g in range(8):
        dest = z1rep if g < 4 else mz1rep
        grp = g % 4
        zev.append(S.add("vector", lambda p, g=g, grp=grp, dest=dest: p.tensor_copy(
            dest[:, 512 * grp:512 * (grp + 1)],
            P[g // 2][64 * (g % 2):64 * (g % 2) + 64, 0:512]),
            deps=[d_zbc, q_mm[NCH - 1][3]], sem="VE", inc=1, tag=f"zbcev{g}"))
    d_z1r = zev[3]
    d_mz1r = zev[7]

    sbd = []
    f1d = []
    for ts in range(4):
        reg = sattn_mm[ts][1]
        d_f1 = S.add("vector", lambda p, ts=ts, reg=reg: p.scalar_tensor_tensor(
            t1[:, :], reg, M_bcw, s_x[:, 512 * ts:512 * (ts + 1)], OP.add, OP.add),
            deps=[sattn_mm[ts][0], ev_bcw, sx_ev[ts], r1d[NT - 1]], sem="VE", inc=1,
            tag=f"f1.{ts}")
        d_f2 = S.add("vector", lambda p, ts=ts: p.tensor_tensor(
            t2[:, :], t1[:, :], z1rep[:, 512 * ts:512 * (ts + 1)], OP.mult),
            deps=[d_f1, d_z1r], sem="VE", inc=1, tag=f"f2.{ts}")
        d_f3 = S.add("vector", lambda p, ts=ts: p.tensor_scalar_mul(
            t1[:, :], mz1rep[:, 512 * ts:512 * (ts + 1)], M_cs),
            deps=[d_f2, d_mz1r, ev_cs], sem="VE", inc=1, tag=f"f3.{ts}")
        d_f4 = S.add("vector", lambda p, ts=ts: p.scalar_tensor_tensor(
            s_b[:, 512 * ts:512 * (ts + 1)], t2[:, :], M_bin, t1[:, :],
            OP.add, OP.subtract),
            deps=[d_f3, ld["bin"], attn_mm[NT - 1]], sem="VE", inc=1,
            tag=f"f4.{ts}")
        sbd.append(d_f4)
        f1d.append(d_f1)

    x2d = [None] * NT
    for t in range(NT):
        deps = [d_mrs1, r1d[t]]
        if t < 8:
            deps += [M_q, M_sx]
        else:
            deps += [x2d[t - 8], ssq1d[t - 8]]
        x2d[t] = S.add("vector", lambda p, t=t: p.tensor_scalar(
            x2_t(t)[:, :], r1_t(t)[:, :], L_rstd1[:, t:t + 1], L_mrs1[:, t:t + 1],
            OP.mult, OP.subtract),
            deps=deps, sem="VE", inc=1, tag=f"x2.{t}")


    gdone = [None] * 4
    r2d = [None] * NT
    ssq2d = [None] * NT
    prev_f = {}

    def emit_G(qa, qb, c, hA, hB, gA, gB, pair_idx):
        depsA = [hA[c], gd["bdw2"]]
        if c == 0:
            depsA += [d_zwlo] if pair_idx == 0 else [gdone[0], gdone[1]]
        gA[c] = S.add("tensor", lambda p, c=c: p.matmul(
            P[6][0:64, :], bdw2_b[:, 64 * c:64 * (c + 1)], h_b[c % 2][:, :],
            start=(c == 0), stop=(c == NCH - 1), skip_group_check=True),
            deps=depsA, sem="TE", inc=1, tag=f"GA.{c}")
        depsB = [hB[c], gd["bdw2"]]
        if c == 0:
            depsB += [d_zwlo] if pair_idx == 0 else [gdone[0], gdone[1]]
        gB[c] = S.add("tensor", lambda p, c=c: p.matmul(
            P[6][64:128, :], bdw2_b[:, 64 * c:64 * (c + 1)], h_b[2 + c % 2][:, :],
            start=(c == 0), stop=(c == NCH - 1), skip_group_check=True),
            deps=depsB, sem="TE", inc=1, tag=f"GB.{c}")

    def emit_ffn_tile(t, deep=False):
        if deep:
            slots = (P[2 * (t % 4)], P[2 * (t % 4) + 1])
            key = t % 4
        else:
            slots = (P[0], P[1])
            key = 0
        deps = [gdone[t // 4], gd["wout"], gd["brows"], m_ones_r]
        if key in prev_f:
            deps += [prev_f[key]]
        else:
            deps += [r1d[NT - 1]]
        if deep and t % 4 >= 1:
            deps += [hA_pp[NCH - 1], hB_pp[NCH - 1], gdone[2], gdone[3]]
        def ffn_emit(p, t=t, slots=slots):
            ins = None
            for jh in range(2):
                pairs = [(g_b[:, 128 * t:128 * (t + 1)],
                          wout_b[0:64, 512 * jh:512 * (jh + 1)]),
                         (ones_r[:, :],
                          brows_b[0:1, 1024 + 512 * jh:1024 + 512 * (jh + 1)])]
                ins = mm_group(p, slots[jh][:, :], pairs)
            return ins
        d_ffn = S.add("tensor", ffn_emit, deps=deps, sem="TE", inc=1, tag=f"ffn{t}")

        def r2_emit(p, t=t, slots=slots):
            p.scalar_tensor_tensor(r2_t(t)[:, 0:512], slots[0][:, :], 1.0,
                                   x2_t(t)[:, 0:512], OP.mult, OP.add,
                                   accum_out=C_sum2a[:, t:t + 1])
            return p.scalar_tensor_tensor(r2_t(t)[:, 512:1024], slots[1][:, :], 1.0,
                                          x2_t(t)[:, 512:1024], OP.mult, OP.add,
                                          accum_out=C_sum2b[:, t:t + 1])
        extra = [d_ffn, x2d[t]]
        if t < 8:
            extra += [x2d[8 + t], ssq1d[8 + t]]
        else:
            extra += [r2d[t - 8]]
        r2d[t] = S.add("vector", r2_emit, deps=extra, sem="VE", inc=1, tag=f"r2.{t}")
        prev_f[key] = r2d[t]
        ssq2d[t] = S.add("scalar", lambda p, t=t: p.activation(
            sqscr[:, 0:1024], r2_t(t)[:, :], AF.Square, accum_out=C_ssq2[:, t:t + 1]),
            deps=[r2d[t]], sem="SE", inc=1, tag=f"ssq2.{t}")

    hA_prev_pair = hB_prev_pair = None
    for pair_idx, (qa, qb) in enumerate(((0, 1), (2, 3))):
        hA = [None] * NCH; hB = [None] * NCH
        gA = [None] * NCH; gB = [None] * NCH
        sA = [None] * NCH; sB = [None] * NCH
        prev_hA = {}; prev_hB = {}
        for c in range(NCH):
            psA = P[4 + (c % 2)]
            depsA = [sbd[qa], gd["selS"]] + ([f1d[0], f1d[1]] if c % 2 == 0
                                             else [f1d[2], f1d[3]])
            if (c % 2) in prev_hA:
                depsA += [prev_hA[c % 2]]
            elif pair_idx == 1:
                depsA += [hA_prev_pair[6 + (c % 2)]]
            sA[c] = S.add("tensor", lambda p, qa=qa, c=c, psA=psA: p.matmul(
                psA[:, :], selS_b[:, 128 * c:128 * (c + 1)],
                s_b[0:64, 512 * qa:512 * (qa + 1)], start=True, stop=True,
                skip_group_check=True),
                deps=depsA, sem="TE", inc=1, tag=f"srep{qa}.{c}")
            psB = P[7] if c % 2 == 0 else P[3]
            depsB = [sbd[qb], gd["selS"]]
            if (c % 2) in prev_hB:
                depsB += [prev_hB[c % 2]]
            elif pair_idx == 1:
                depsB += [hB_prev_pair[6 + (c % 2)]]
            if pair_idx == 0 and c % 2 == 1:
                depsB += [r1d[NT - 1]]
            sB[c] = S.add("tensor", lambda p, qb=qb, c=c, psB=psB: p.matmul(
                psB[:, :], selS_b[:, 128 * c:128 * (c + 1)],
                s_b[0:64, 512 * qb:512 * (qb + 1)], start=True, stop=True,
                skip_group_check=True),
                deps=depsB, sem="TE", inc=1, tag=f"srep{qb}.{c}")
            if pair_idx == 1:
                emit_ffn_tile(c)
            if c > 0:
                emit_G(qa, qb, c - 1, hA, hB, gA, gB, pair_idx)
            hA[c] = S.add("scalar", lambda p, c=c, psA=psA: p.activation(
                h_b[c % 2][:, :], psA[:, :], AF.Relu,
                bias=C_b1[:, c:c + 1], scale=C_w1[:, c:c + 1]),
                deps=[sA[c], ld["w1"], ld["b1"]] + ([gA[c - 2]] if c >= 2 else []) +
                     ([sbd[3]] if pair_idx == 0 and c < 2 else []),
                sem="SE", inc=1, tag=f"h{qa}.{c}")
            prev_hA[c % 2] = hA[c]
            hB[c] = S.add("scalar", lambda p, c=c, psB=psB: p.activation(
                h_b[2 + c % 2][:, :], psB[:, :], AF.Relu,
                bias=C_b1[:, c:c + 1], scale=C_w1[:, c:c + 1]),
                deps=[sB[c], ld["w1"], ld["b1"]] + ([gB[c - 2]] if c >= 2 else []) +
                     ([sbd[3]] if pair_idx == 0 and c < 2 else []),
                sem="SE", inc=1, tag=f"h{qb}.{c}")
            prev_hB[c % 2] = hB[c]
        emit_G(qa, qb, NCH - 1, hA, hB, gA, gB, pair_idx)
        gdone[qa] = S.add("vector", lambda p, qa=qa: p.tensor_scalar_add(
            g_b[:, 512 * qa:512 * (qa + 1)], P[6][0:64, :], M_b2),
            deps=[gA[NCH - 1], ld["b2"]], sem="VE", inc=1, tag=f"g{qa}")
        gdone[qb] = S.add("vector", lambda p, qb=qb: p.tensor_scalar_add(
            g_b[:, 512 * qb:512 * (qb + 1)], P[6][64:128, :], M_b2),
            deps=[gB[NCH - 1], ld["b2"]], sem="VE", inc=1, tag=f"g{qb}")
        hA_prev_pair = hA
        hB_prev_pair = hB

    hA_pp, hB_pp = hA_prev_pair, hB_prev_pair

    g_mrs2 = [None] * 4
    g_mrs2n = [None] * 4
    prev_st = {}

    def emit_ln2_group(g):
        c4 = slice(4 * g, 4 * (g + 1))
        scr = L_w2[:, 4 * g:4 * (g + 1)]
        deps = [r2d[t] for t in range(4 * g, 4 * g + 4)]
        d1 = S.add("vector", lambda p: p.tensor_add(
            scr, C_sum2a[:, c4], C_sum2b[:, c4]), deps=deps, sem="VE", inc=1,
            tag=f"ln2a{g}")
        d2 = S.add("vector", lambda p: p.tensor_scalar_mul(
            L_m2[:, c4], scr, 1.0 / D), deps=[d1], sem="VE", inc=1)
        d3 = S.add("vector", lambda p: p.tensor_mul(
            scr, L_m2[:, c4], L_m2[:, c4]), deps=[d2], sem="VE", inc=1)
        d4 = S.add("vector", lambda p: p.scalar_tensor_tensor(
            scr, C_ssq2[:, c4], 1.0 / D, scr, OP.mult, OP.subtract),
            deps=[d3] + [ssq2d[t] for t in range(4 * g, 4 * g + 4)],
            sem="VE", inc=1)
        d5 = S.add("vector", lambda p: p.tensor_scalar_add(scr, scr, EPS_LN),
                   deps=[d4], sem="VE", inc=1)
        d6 = S.add("vector", lambda p: p.reciprocal(scr, scr),
                   deps=[d5], sem="VE", inc=1)
        d7 = S.add("scalar", lambda p: p.activation(
            L_rstd2[:, c4], scr, AF.Sqrt), deps=[d6], sem="SE", inc=1,
            tag=f"sqrt2.{g}")
        d8 = S.add("vector", lambda p: p.tensor_mul(
            L_mrs2[:, c4], L_m2[:, c4], L_rstd2[:, c4]),
            deps=[d7], sem="VE", inc=1)
        d9 = S.add("vector", lambda p: p.tensor_scalar_mul(
            L_mrs2n[:, c4], L_mrs2[:, c4], -1.0), deps=[d8], sem="VE", inc=1)
        g_mrs2[g] = d8
        g_mrs2n[g] = d9

    def emit_out(t):
        g = t // 4
        deps = [g_mrs2[g], g_mrs2n[g]]
        if t % 4 in prev_st:
            deps += [prev_st[t % 4]]
        else:
            deps += [attn_mm[NT - 1], sattn_mm[3][0], d_zbc]
        if t % 2 == 0:
            d_o = S.add("vector", lambda p, t=t: p.tensor_scalar(
                outst[t % 4][:, :], r2_t(t)[:, :], L_rstd2[:, t:t + 1],
                L_mrs2[:, t:t + 1], OP.mult, OP.subtract),
                deps=deps, sem="VE", inc=1, tag=f"out.{t}")
        else:
            d_o = S.add("scalar", lambda p, t=t: p.activation(
                outst[t % 4][:, :], r2_t(t)[:, :], AF.Identity,
                bias=L_mrs2n[:, t:t + 1], scale=L_rstd2[:, t:t + 1]),
                deps=deps, sem="SE", inc=1, tag=f"out.{t}")
        eng, stsem = [("sync", "ST"), ("scalar", "SD"),
                      ("sync", "ST"), ("scalar", "SD")][t % 4]
        d_st = S.add(eng, lambda p, t=t: p.dma_start(
            out=out_tm[128 * t:128 * (t + 1), :], in_=outst[t % 4][:, :]),
            deps=[d_o], sem=stsem, inc=16, tag=f"st{t}", new_group=True)
        prev_st[t % 4] = d_st

    emit_ffn_tile(8, deep=True); emit_ffn_tile(9, deep=True)
    emit_ln2_group(0)
    emit_out(0); emit_out(1)
    emit_ffn_tile(10, deep=True); emit_ffn_tile(11, deep=True)
    emit_out(2); emit_out(3)
    emit_ln2_group(1)
    emit_ffn_tile(12, deep=True); emit_ffn_tile(13, deep=True)
    emit_out(4); emit_out(5)
    emit_ffn_tile(14, deep=True); emit_ffn_tile(15, deep=True)
    emit_out(6); emit_out(7)
    emit_ln2_group(2)
    for t in (8, 9, 10, 11):
        emit_out(t)
    emit_ln2_group(3)
    for t in (12, 13, 14, 15):
        emit_out(t)

    if _DEBUG_TAPS:
        outst[2] = outst[0]
        outst[3] = outst[1]
        tap_list = [
            ("d_sb", s_b[:, :], BF16, [gdone[3], r2d[NT - 1]]),
            ("d_gb", g_b[:, :], BF16, [gdone[3], r2d[NT - 1]]),
            ("d_r2_0", r2_t(0)[:, :], F32, [r2d[0], ("VE", S.counts["VE"])]),
            ("d_r2_8", r2_t(8)[:, :], F32, [r2d[8], ("VE", S.counts["VE"])]),
            ("d_r2_12", r2_t(12)[:, :], F32, [r2d[12], ("VE", S.counts["VE"])]),
            ("d_x2_8", x2_t(8)[:, :], F32, [("VE", S.counts["VE"])]),
            ("d_colsB", colsB[:, :], F32, [("VE", S.counts["VE"]),
                                           ("SE", S.counts["SE"])]),
            ("d_LW2", LW2[:, :], F32, [("VE", S.counts["VE"])]),
        ]
        for name, srcv, dt, deps in tap_list:
            dst = nc.dram_tensor(name, list(srcv.shape), dt,
                                 kind="ExternalOutput").ap()
            S.add("sync", lambda p, dd=dst, ss=srcv: p.dma_start(out=dd, in_=ss),
                  deps=deps, sem="ST", inc=16, tag=f"tap.{name}", new_group=True)

    def final_waits(p):
        for s in ("ST", "SD"):
            p.wait_ge(sems[s], S.counts.get(s, 0))
    S.add("sync", final_waits, tag="final")

    S.finalize()
    with nc.Block() as block:
        @block.sync
        def _(p):
            S.emit("sync", p, sems)

        @block.gpsimd
        def _(p):
            S.emit("gpsimd", p, sems)

        @block.tensor
        def _(p):
            S.emit("tensor", p, sems)

        @block.vector
        def _(p):
            S.emit("vector", p, sems)

        @block.scalar
        def _(p):
            S.emit("scalar", p, sems)

    ctx.close()
    return nc


def _consts():
    maskC = np.zeros((128, 128), dtype=np.float32)
    for c in range(NCH):
        for p in range(128):
            maskC[p, 16 * c + 2 * c + p // 64] = 1.0
    blockmask = np.zeros((128, 128), dtype=np.float32)
    for d in range(128):
        blockmask[d, (d // 64) * 64:(d // 64) * 64 + 64] = 1.0
    ident = np.eye(128, dtype=np.float32)
    selS = np.zeros((NCH, M, 128), dtype=np.float32)
    for c in range(NCH):
        for k in range(128):
            selS[c, (128 * c + k) // INNER, k] = 1.0
    zsel = np.zeros((NCH, 16, 128), dtype=np.float32)
    for c in range(NCH):
        for p in range(128):
            zsel[c, 2 * c + p // 64, p] = 1.0
    rsel = np.zeros((32, 32, M), dtype=np.float32)
    for r in range(32):
        rsel[r, r, :] = 1.0
    return maskC, blockmask, ident, selS, zsel, rsel


_NC_CACHE = {}

def kernel(L0_attn_x, L0_attn_Wq, L0_attn_bq, L0_attn_Wk, L0_attn_bk,
           L0_attn_Wv, L0_attn_bv, L0_attn_Wc, L0_attn_bc,
           L0_ffn_Win, L0_ffn_bin_, L0_ffn_W1, L0_ffn_b1, L0_ffn_W2,
           L0_ffn_b2, L0_ffn_Wout, L0_ffn_bout):
    x = np.ascontiguousarray(np.asarray(L0_attn_x, dtype=np.float32))
    Wq = np.asarray(L0_attn_Wq, np.float32); Wk = np.asarray(L0_attn_Wk, np.float32)
    Wv = np.asarray(L0_attn_Wv, np.float32); Wc = np.asarray(L0_attn_Wc, np.float32)
    bq = np.asarray(L0_attn_bq, np.float32); bk = np.asarray(L0_attn_bk, np.float32)
    bv = np.asarray(L0_attn_bv, np.float32); bc = np.asarray(L0_attn_bc, np.float32)
    Win = np.asarray(L0_ffn_Win, np.float32); bin_ = np.asarray(L0_ffn_bin_, np.float32)
    W1 = np.asarray(L0_ffn_W1, np.float32); b1 = np.asarray(L0_ffn_b1, np.float32)
    W2 = np.asarray(L0_ffn_W2, np.float32); b2 = np.asarray(L0_ffn_b2, np.float32)
    Wout = np.asarray(L0_ffn_Wout, np.float32); bout = np.asarray(L0_ffn_bout, np.float32)

    maskC, blockmask, ident, selS, zsel, rsel = _consts()

    bdw2 = np.zeros((NCH, 128, M), dtype=np.float32)
    flat = np.arange(128)
    for c in range(NCH):
        idx = 128 * c + flat
        bdw2[c, flat, idx // INNER] = W2[idx // INNER, idx % INNER]

    shared = dict(
        wqT=np.ascontiguousarray(Wq.T), wkT=np.ascontiguousarray(Wk.T),
        wvT=np.ascontiguousarray(Wv.T), wcT=np.ascontiguousarray(Wc.T),
        wc_nat=np.ascontiguousarray(Wc),
        win=np.ascontiguousarray(Win), wout=np.ascontiguousarray(Wout),
        bdw2=bdw2, selS=selS,
        rep_bk=np.ascontiguousarray(np.tile(bk[None, :], (128, 1))),
        rep_bv=np.ascontiguousarray(np.tile(bv[None, :], (128, 1))),
        bq_col=np.ascontiguousarray(bq.reshape(NCH, 128).T),
        bc_col=np.ascontiguousarray(bc.reshape(NCH, 128).T),
        brows=np.ascontiguousarray(np.concatenate([bc, bout])[None, :]),
        bin_col=np.ascontiguousarray(bin_.reshape(M, 1)),
        b2_col=np.ascontiguousarray(b2.reshape(M, 1)),
        w1_col=np.ascontiguousarray(W1.reshape(-1).reshape(NCH, 128).T),
        b1_col=np.ascontiguousarray(b1.reshape(-1).reshape(NCH, 128).T),
        blockmask=blockmask, ident=ident, rsel=rsel)

    in_maps = []
    for core in range(NCORES):
        b_i, half = core // 2, core % 2
        xs = x[b_i, half * TOK:(half + 1) * TOK, :]
        m = dict(shared)
        m["x_tm"] = np.ascontiguousarray(xs)
        m["x_fm"] = np.ascontiguousarray(xs.T)
        in_maps.append(m)

    if "nc" not in _NC_CACHE:
        _NC_CACHE["nc"] = build_nc()
    nc = _NC_CACHE["nc"]
    res = run_bass_kernel_spmd(nc, in_maps, core_ids=list(range(NCORES)))
    out = np.empty((B, T, D), dtype=np.float32)
    for core in range(NCORES):
        b_i, half = core // 2, core % 2
        out[b_i, half * TOK:(half + 1) * TOK, :] = res.results[core]["out_tm"]
    return out


# version 20
# speedup vs baseline: 1.0861x; 1.0861x over previous
import numpy as np
import concourse.bass as bass
import concourse.mybir as mybir
from concourse import dve_ops as _dve_ops
from concourse.bass_utils import run_bass_kernel_spmd
from contextlib import ExitStack

F32 = mybir.dt.float32
BF16 = mybir.dt.bfloat16
AF = mybir.ActivationFunctionType
OP = mybir.AluOpType

B, T, D = 4, 4096, 1024
NCORES = 8
TOK = 2048
NT = 16
NCH = 8
M, INNER = 64, 16
EPS_LN = 1e-5
EPS_Z = 1e-6


DMA_SEMS = {"LD", "GD", "ST", "ST2", "G0", "G1", "SD", "TD", "VD"}


class Op:
    __slots__ = ("eng", "fn", "deps", "sem", "inc", "done", "tag", "new_group",
                 "self_wait")
    def __init__(self, eng, fn, deps, sem, inc, tag="", new_group=False):
        self.eng, self.fn, self.deps, self.sem, self.inc = eng, fn, deps, sem, inc
        self.done = None
        self.tag = tag
        self.new_group = new_group
        self.self_wait = None


class Sched:
    def __init__(self):
        self.ops = []
        self.counts = {}
        self.group_end = {}
        self.last_done = {}

    def add(self, eng, fn, deps=(), sem=None, inc=0, tag="", new_group=False):
        deps = [d for d in deps if d is not None]
        if eng in ("vector", "scalar", "tensor") and eng in self.last_done:
            deps.append(self.last_done[eng])
        op = Op(eng, fn, deps, sem, inc, tag, new_group)
        self.ops.append(op)
        if sem is not None and inc:
            v = self.counts.get(sem, 0) + inc
            self.counts[sem] = v
            op.done = (sem, v)
            if eng in ("vector", "scalar", "tensor") and sem not in DMA_SEMS:
                self.last_done[eng] = op.done
        return op.done

    def finalize(self):
        for sem in DMA_SEMS:
            sem_ops = [op for op in self.ops if op.sem == sem]
            if not sem_ops:
                continue
            groups = []
            for i, op in enumerate(sem_ops):
                if i == 0 or op.new_group:
                    groups.append([])
                groups[-1].append(op)
            running = 0
            for g in groups:
                start = running
                for op in g:
                    running += op.inc
                for op in g:
                    self.group_end[(sem, op.done[1])] = running
                if start > 0:
                    g[0].self_wait = start

    def emit(self, eng_name, proxy, sems):
        if eng_name == "tensor":
            real = proxy

            class _Shim:
                def matmul(self, *a, **k):
                    TE_TRACE.append(self._tag)
                    return real.matmul(*a, **k)

                def transpose(self, *a, **k):
                    TE_TRACE.append(self._tag)
                    return real.transpose(*a, **k)

                def __getattr__(self, n):
                    return getattr(real, n)

            shim = _Shim()
        waited = {}
        for op in self.ops:
            if op.eng != eng_name:
                continue
            if eng_name == "tensor":
                shim._tag = op.tag
                proxy = shim
            if op.self_wait is not None:
                if waited.get(op.sem, -1) < op.self_wait:
                    proxy.wait_ge(sems[op.sem], op.self_wait)
                    waited[op.sem] = op.self_wait
            for (s, v) in op.deps:
                if s in DMA_SEMS:
                    v = self.group_end[(s, v)]
                if waited.get(s, -1) >= v:
                    continue
                proxy.wait_ge(sems[s], v)
                waited[s] = v
            ins = op.fn(proxy)
            if op.done is not None:
                assert ins is not None, op.tag
                ins.then_inc(sems[op.sem], op.inc)


_DEBUG_TAPS = False
TE_TRACE = []


def build_nc():
    nc = bass.Bass("TRN2", target_bir_lowering=False, debug=False,
                   dynamic_dma_scratch_size=16384)

    def din(name, shape):
        return nc.dram_tensor(name, list(shape), F32, kind="ExternalInput").ap()

    x_tm = din("x_tm", (TOK, D))
    x_fm = din("x_fm", (D, TOK))
    wqT = din("wqT", (D, D)); wkT = din("wkT", (D, D))
    wvT = din("wvT", (D, D)); wcT = din("wcT", (D, D))
    wc_nat = din("wc_nat", (D, D))
    win = din("win", (D, M)); wout = din("wout", (M, D))
    bdw2 = din("bdw2", (NCH, 128, M))
    selS = din("selS", (NCH, M, 128))
    rep_bk = din("rep_bk", (128, D)); rep_bv = din("rep_bv", (128, D))
    bq_col = din("bq_col", (128, NCH)); bc_col = din("bc_col", (128, NCH))
    brows = din("brows", (1, 2 * D))
    bin_col = din("bin_col", (M, 1)); b2_col = din("b2_col", (M, 1))
    w1_col = din("w1_col", (128, NCH)); b1_col = din("b1_col", (128, NCH))
    maskC = din("maskC", (128, 128))
    zsel = din("zsel", (NCH, 16, 128))
    blockmask = din("blockmask", (128, 128))
    ident = din("ident", (128, 128))

    out_tm = nc.dram_tensor("out_tm", [TOK, D], F32, kind="ExternalOutput").ap()

    cc_in = nc.dram_tensor("cc_in", [128, 1032], F32, kind="Internal").ap()
    cc_out = nc.dram_tensor("cc_out", [128, 1032], F32, kind="Internal").ap()

    ctx = ExitStack()
    e = ctx.enter_context

    A1 = e(nc.sbuf_tensor("A1", [128, 16384], BF16))
    A2 = e(nc.sbuf_tensor("A2", [128, 16384], BF16))
    A3 = e(nc.sbuf_tensor("A3", [128, 16384], BF16))
    A4 = e(nc.sbuf_tensor("A4", [128, 16384], BF16))
    W0 = e(nc.sbuf_tensor("W0", [128, 8192], BF16))
    W1 = e(nc.sbuf_tensor("W1", [128, 8192], BF16))
    SCR = e(nc.sbuf_tensor("SCR", [128, 2048], F32))
    RB0 = e(nc.sbuf_tensor("RB0", [128, 1024], F32))
    RB1 = e(nc.sbuf_tensor("RB1", [128, 1024], F32))
    s_x = e(nc.sbuf_tensor("s_x", [64, 2048], F32))
    kv_b = e(nc.sbuf_tensor("kv_b", [128, 1024], BF16))
    lden = e(nc.sbuf_tensor("lden", [128, 128], BF16))
    zsel_b = e(nc.sbuf_tensor("zsel_b", [16, 1024], BF16))
    bcb = e(nc.sbuf_tensor("bcb", [128, 8], BF16))
    wout_b = e(nc.sbuf_tensor("wout_b", [64, 1024], BF16))
    win_b = e(nc.sbuf_tensor("win_b", [128, 512], BF16))
    bdw2_b = e(nc.sbuf_tensor("bdw2_b", [128, 512], BF16))
    wcw_b = e(nc.sbuf_tensor("wcw_b", [128, 512], BF16))
    selS_b = e(nc.sbuf_tensor("selS_b", [64, 1024], BF16))
    maskC_s = e(nc.sbuf_tensor("maskC_s", [128, 128], BF16))
    bmask_s = e(nc.sbuf_tensor("bmask_s", [128, 128], BF16))
    ident_s = e(nc.sbuf_tensor("ident_s", [128, 128], F32))
    brows_b = e(nc.sbuf_tensor("brows_b", [1, 2048], BF16))
    ones_r = e(nc.sbuf_tensor("ones_r", [1, 128], BF16))
    ones_c = e(nc.sbuf_tensor("ones_c", [128, 1], BF16))
    cols = e(nc.sbuf_tensor("cols", [128, 64], F32))
    colsB = e(nc.sbuf_tensor("colsB", [128, 96], F32))
    LW = e(nc.sbuf_tensor("LW", [128, 96], F32))
    LW2 = e(nc.sbuf_tensor("LW2", [128, 96], F32))
    mcols = e(nc.sbuf_tensor("mcols", [64, 8], F32))
    zmz = e(nc.sbuf_tensor("zmz", [128, 32], F32))

    P = [e(nc.psum_tensor(f"P{i}", [128, 512], F32)) for i in range(8)]

    sems = {}
    for s in ["LD", "GD", "TE", "VE", "SE", "CC", "ST", "ST2", "G0", "G1",
              "SD", "TD", "VD"]:
        sems[s] = e(nc.semaphore(s))

    S = Sched()

    def f32_tile(arena, slot):
        return arena[:, 2048 * slot:2048 * (slot + 1)].bitcast(F32)

    def r1_t(t):  return f32_tile(A3, t) if t < 8 else f32_tile(A4, t - 8)
    def x2_t(t):  return f32_tile(A1, t) if t < 8 else f32_tile(A3, t - 8)
    def r2_t(t):  return f32_tile(A4, t) if t < 8 else f32_tile(A1, t - 8)

    den_v = A2[0:16, 0:4096].bitcast(F32)
    zrec_v = A2[0:16, 4096:8192].bitcast(F32)
    exch_v = A2[:, 8192:10240].bitcast(F32)
    exks_v = A2[:, 10240:10256].bitcast(F32)
    z1rep = W0[0:64, 0:4096].bitcast(F32)
    mz1rep = W0[0:64, 4096:8192].bitcast(F32)
    xin = [RB0[:, :], RB1[:, :]]
    ktb = SCR[:, 0:1024]
    e2 = SCR[:, 1024:2048]
    sqscr = SCR[:, 0:1024].bitcast(BF16)
    t1 = SCR[0:64, 1024:1536]
    t2 = SCR[0:64, 1536:2048]
    h_b = [SCR[:, 1024 + 256 * i:1024 + 256 * (i + 1)].bitcast(BF16) for i in range(4)]
    outst = [A2[:, 4096:6144].bitcast(F32), A2[:, 6144:8192].bitcast(F32),
             A2[:, 12288:14336].bitcast(F32), A2[:, 14336:16384].bitcast(F32)]
    s_b = A2[0:64, 0:2048]
    g_b = A2[0:64, 2048:4096]
    zwh_b = A2[0:32, 4096:4224]
    zwlo_b = A2[0:32, 4224:4352]
    zwf = A2[0:32, 4352:4608].bitcast(F32)
    A1F = A1[:, :].bitcast(F32)
    zrh_b = A1[0:16, 8192:10240]
    zrl_b = A1[0:16, 10240:12288]
    zhi_f = A1F[0:16, 2048:4096]
    zrep = [A1F[:, 6144 + 512 * s:6144 + 512 * (s + 1)] for s in range(4)]

    C_bq = cols[:, 0:8]; C_negbq = cols[:, 8:16]; C_bq1 = cols[:, 16:24]
    C_bc = cols[:, 24:32]; C_w1 = cols[:, 32:40]; C_b1 = cols[:, 40:48]
    C_sum2b = cols[:, 48:64]
    C_sum1a = colsB[:, 0:16]; C_sum1b = colsB[:, 16:32]; C_ssq1 = colsB[:, 32:48]
    C_m1 = colsB[:, 48:64]; C_sum2a = colsB[:, 64:80]; C_ssq2 = colsB[:, 80:96]
    L_msq = LW[:, 0:16]; L_var = LW[:, 16:32]; L_rinv = LW[:, 32:48]
    L_rstd1 = LW[:, 48:64]; L_mrs1 = LW[:, 64:80]; L_sum = LW[:, 80:96]
    L_rstd2 = LW2[:, 0:16]; L_mrs2 = LW2[:, 16:32]; L_m2 = LW2[:, 32:48]
    L_w2 = LW2[:, 48:64]; L_mrs2n = LW2[:, 64:80]
    M_bin = mcols[:, 0:1]; M_b2 = mcols[:, 1:2]; M_bcw = mcols[:, 2:3]; M_cs = mcols[:, 3:4]

    ld = {}
    def q_load(name, dst, src, eng="sync", sem="LD", new_group=False, deps=()):
        ld[name] = S.add(eng, lambda p, d=dst, s=src: p.dma_start(out=d, in_=s),
                         deps=deps, sem=sem, inc=16, tag=f"ld.{name}",
                         new_group=new_group)

    stA = A4[:, 8192:16384].bitcast(F32)
    stB = A2[:, 8192:16384].bitcast(F32)
    xstage = A3[:, :].bitcast(F32)
    scr_f = SCR[:, :]
    xb_done = []
    wk_done = []
    wv_done = []

    q_load("ident", ident_s[:, :], ident, eng="gpsimd", sem="GD")
    q_load("bq", C_bq, bq_col, eng="gpsimd", sem="GD")
    q_load("bc", C_bc, bc_col, eng="gpsimd", sem="GD")
    q_load("w1", C_w1, w1_col, eng="gpsimd", sem="GD")
    q_load("b1", C_b1, b1_col, eng="gpsimd", sem="GD")
    q_load("bin", M_bin, bin_col, eng="gpsimd", sem="GD")
    q_load("b2", M_b2, b2_col, eng="gpsimd", sem="GD")

    xlds = [None] * NCH
    xb_done = [None] * NCH

    def emit_xld(c, eng, sem, deps, new_group):
        j = c % 4
        xlds[c] = S.add(eng, lambda p, c=c, j=j: p.dma_start(
            out=xstage[:, 2048 * j:2048 * (j + 1)],
            in_=x_fm[128 * c:128 * (c + 1), :]),
            deps=deps, sem=sem, inc=16, tag=f"xbld{c}", new_group=new_group)

    for c in range(4):
        emit_xld(c, "sync", "LD", [], c % 2 == 0)
    for c in range(4):
        xb_done[c] = S.add("vector", lambda p, c=c: p.tensor_copy(
            A1[:, 2048 * c:2048 * (c + 1)],
            xstage[:, 2048 * c:2048 * (c + 1)]),
            deps=[xlds[c]], sem="VE", inc=1, tag=f"xbcast{c}")
    emit_xld(4, "sync", "LD", [xb_done[0]], True)
    emit_xld(5, "sync", "LD", [xb_done[1]], False)

    wk_slots = [stA[:, 1024 * i:1024 * (i + 1)] for i in range(4)] + \
               [scr_f[:, 0:1024], scr_f[:, 1024:2048], RB0[:, :], RB1[:, :]]
    wk_lds = []
    for c in range(NCH):
        wk_lds.append(S.add("scalar", lambda p, c=c: p.dma_start(
            out=wk_slots[c], in_=wkT[128 * c:128 * (c + 1), :]),
            sem="SD", inc=16, tag=f"wkld{c}", new_group=(c == 0)))
    for c in range(NCH):
        wk_done.append(S.add("scalar", lambda p, c=c: p.copy(
            W0[:, 1024 * c:1024 * (c + 1)], wk_slots[c]),
            deps=[wk_lds[c]], sem="SE", inc=1, tag=f"wkcast{c}"))

    emit_xld(6, "scalar", "SD", [xb_done[2]], True)
    emit_xld(7, "scalar", "SD", [xb_done[3]], False)

    wv_slots = [stB[:, 1024 * i:1024 * (i + 1)] for i in range(4)] + \
               [stA[:, 1024 * i:1024 * (i + 1)] for i in range(4)]
    wv_lds = []
    for c in range(NCH):
        deps = [wk_done[c - 4]] if c >= 4 else []
        wv_lds.append(S.add("scalar", lambda p, c=c: p.dma_start(
            out=wv_slots[c], in_=wvT[128 * c:128 * (c + 1), :]),
            deps=deps, sem="SD", inc=16, tag=f"wvld{c}", new_group=(c == 0)))
    def emit_wv_casts():
        for c in range(NCH):
            wv_done.append(S.add("scalar", lambda p, c=c: p.copy(
                W1[:, 1024 * c:1024 * (c + 1)], wv_slots[c]),
                deps=[wv_lds[c]], sem="SE", inc=1, tag=f"wvcast{c}"))

    for c in range(4, NCH):
        xb_done[c] = S.add("vector", lambda p, c=c: p.tensor_copy(
            A1[:, 2048 * c:2048 * (c + 1)],
            xstage[:, 2048 * (c % 4):2048 * (c % 4 + 1)]),
            deps=[xlds[c]], sem="VE", inc=1, tag=f"xbcast{c}")

    q_load("repbk", RB0[:, :], rep_bk, new_group=True,
           deps=[wk_done[6], wk_done[7]])
    q_load("repbv", RB1[:, :], rep_bv)

    gd = {}
    def cast_load(name, dst, src, deps=(), new_group=False):
        gd[name] = S.add("gpsimd", lambda p, d=dst, s=src: p.dma_start(out=d, in_=s),
                         deps=deps, sem="GD", inc=16, tag=f"cl.{name}",
                         new_group=new_group)

    for c in range(NCH):
        cast_load(f"wcn{c}", A4[:, 1024 * c:1024 * (c + 1)],
                  wc_nat[128 * c:128 * (c + 1), :], new_group=(c == 0),
                  deps=[wv_lds[-1]] if c == 0 else ())
    cast_load("win", win_b[:, :].rearrange("p (c m) -> p c m", c=NCH),
              win.rearrange("(c p) m -> p c m", p=128))
    cast_load("wout", wout_b[:, :], wout, new_group=True)
    cast_load("bdw2", bdw2_b[:, :].rearrange("p (c m) -> p c m", c=NCH),
              bdw2.rearrange("c p m -> p c m"))
    cast_load("selS", selS_b[:, :].rearrange("m (c p) -> m c p", c=NCH),
              selS.rearrange("c m p -> m c p"))
    cast_load("brows", brows_b[:, :], brows)
    cast_load("maskC", maskC_s[:, :], maskC)
    cast_load("bmask", bmask_s[:, :], blockmask)
    cast_load("zsel", zsel_b[:, :].rearrange("h (c p) -> h c p", c=NCH),
              zsel.rearrange("c h p -> h c p"))
    ld["maskC"] = gd["maskC"]
    ld["bmask"] = gd["bmask"]

    m_ones_r = S.add("vector", lambda p: p.memset(ones_r[:, :], 1.0), sem="VE", inc=1)
    m_ones_c = S.add("vector", lambda p: p.memset(ones_c[:, :], 1.0), sem="VE", inc=1)
    d_negbq = S.add("vector", lambda p: p.tensor_scalar_mul(C_negbq, C_bq, -1.0),
                    deps=[ld["bq"]], sem="VE", inc=1)
    d_bq1 = S.add("vector", lambda p: p.tensor_scalar_add(C_bq1, C_bq, 1.0),
                  deps=[ld["bq"]], sem="VE", inc=1)
    d_bcc = S.add("vector", lambda p: p.tensor_copy(bcb[:, :], C_bc),
                  deps=[ld["bc"]], sem="VE", inc=1)

    def mm_group(p, outs, pairs):
        ins = None
        n = len(pairs)
        for i, (l, r) in enumerate(pairs):
            ins = p.matmul(outs, l, r, start=(i == 0), stop=(i == n - 1))
        return ins

    wk_all = wk_done
    wv_all = wv_done
    ktil = [None] * NT
    vtil = [None] * NT
    k_mm = [None] * NT
    v_mm = [None] * NT
    prev_kev = {}
    prev_vev = {}
    KAHEAD = 4
    def emit_kproj(t):
        pk = (P[0], P[1]) if t % 2 == 0 else (P[4], P[5])
        deps_k = xb_done + wk_all
        if t % 2 in prev_kev:
            deps_k = deps_k + [prev_kev[t % 2]]
        def kproj(p, t=t, pk=pk):
            ins = None
            for jh in range(2):
                ins = mm_group(p, pk[jh][:, :],
                               [(A1[:, 2048 * kc + 128 * t:2048 * kc + 128 * (t + 1)],
                                 W0[:, 1024 * kc + 512 * jh:1024 * kc + 512 * (jh + 1)])
                                for kc in range(NCH)])
            return ins
        k_mm[t] = S.add("tensor", kproj, deps=deps_k, sem="TE", inc=1, tag=f"kproj{t}")
        def k_tb(p, t=t, pk=pk):
            p.tensor_tensor(ktb[:, 0:512], pk[0][:, :], RB0[:, 0:512], OP.add)
            return p.tensor_tensor(ktb[:, 512:1024], pk[1][:, :], RB0[:, 512:1024], OP.add)
        tb_deps = [k_mm[t], ld["repbk"]]
        if t == 0:
            tb_deps += [wk_done[4], wk_done[5]]
        d_tb = S.add("vector", k_tb, deps=tb_deps, sem="VE", inc=1,
                     tag=f"ktb{t}")
        d_r = S.add("scalar", lambda p: p.activation(e2[:, :], ktb[:, :], AF.Relu,
                                                     scale=-1.0),
                    deps=[d_tb, ktil[t - 1] if t > 0 else None], sem="SE", inc=1)
        d_e = S.add("scalar", lambda p: p.activation(e2[:, :], e2[:, :], AF.Exp,
                                                     scale=-1.0),
                    deps=[d_r], sem="SE", inc=1)
        kdeps = [d_e] + ([wv_done[(t - 8) // 2]] if t >= 8 else [])
        ktil[t] = S.add("vector", lambda p, t=t: p.scalar_tensor_tensor(
            A2[:, 1024 * t:1024 * (t + 1)], ktb[:, :], 1.0, e2[:, :], OP.add, OP.max),
            deps=kdeps, sem="VE", inc=1, tag=f"ktil{t}")
        prev_kev[t % 2] = ktil[t]

    def emit_vproj(t):
        pv = (P[2], P[3]) if t % 2 == 0 else (P[6], P[7])
        deps_v = xb_done + wv_all
        if t % 2 in prev_vev:
            deps_v = deps_v + [prev_vev[t % 2]]
        def vproj(p, t=t, pv=pv):
            ins = None
            for jh in range(2):
                ins = mm_group(p, pv[jh][:, :],
                               [(A1[:, 2048 * kc + 128 * t:2048 * kc + 128 * (t + 1)],
                                 W1[:, 1024 * kc + 512 * jh:1024 * kc + 512 * (jh + 1)])
                                for kc in range(NCH)])
            return ins
        v_mm[t] = S.add("tensor", vproj, deps=deps_v, sem="TE", inc=1, tag=f"vproj{t}")
        def v_ev(p, t=t, pv=pv):
            p.tensor_tensor(A3[:, 1024 * t:1024 * t + 512], pv[0][:, :],
                            RB1[:, 0:512], OP.add)
            return p.tensor_tensor(A3[:, 1024 * t + 512:1024 * (t + 1)], pv[1][:, :],
                                   RB1[:, 512:1024], OP.add)
        vtil[t] = S.add("vector", v_ev, deps=[v_mm[t], ld["repbv"]], sem="VE", inc=1,
                        tag=f"vtil{t}")
        prev_vev[t % 2] = vtil[t]

    for t in range(6):
        emit_kproj(t)
    emit_wv_casts()
    emit_vproj(0); emit_vproj(1)
    kq, vq = 6, 2
    while kq < NT or vq < NT:
        if kq < NT:
            emit_kproj(kq); kq += 1
        if vq < NT:
            emit_vproj(vq); vq += 1

    wq_stage = A4[:, 8192:16384].bitcast(F32)
    wq_all = []
    prev_cast = None
    for b in range(2):
        lds = []
        for i in range(4):
            c = 4 * b + i
            lds.append(S.add("sync", lambda p, c=c, i=i: p.dma_start(
                out=wq_stage[:, 1024 * i:1024 * (i + 1)],
                in_=wqT[128 * c:128 * (c + 1), :]),
                deps=[prev_cast] if prev_cast else [wv_done[-1]],
                sem="LD", inc=16, tag=f"wqld{c}", new_group=(i == 0)))
        for i in range(4):
            c = 4 * b + i
            prev_cast = S.add("scalar", lambda p, c=c, i=i: p.copy(
                W0[:, 1024 * c:1024 * (c + 1)], wq_stage[:, 1024 * i:1024 * (i + 1)]),
                deps=[lds[i], k_mm[NT - 1]], sem="SE", inc=1, tag=f"wqcast{c}")
            wq_all.append(prev_cast)
    wc_all = []
    for c in range(NCH):
        wc_all.append(S.add("gpsimd", lambda p, c=c: p.dma_start(
            out=W1[:, 1024 * c:1024 * (c + 1)], in_=wcT[128 * c:128 * (c + 1), :]),
            deps=[v_mm[NT - 1]], sem="GD", inc=16, tag=f"wc{c}",
            new_group=(c == 0)))

    kv_deps_base = ktil + vtil
    kv_last = None
    for pr in range(8):
        po = P[4 + pr // 4]
        deps = list(kv_deps_base) if pr == 0 else [kv_last]
        kv_last = S.add("tensor", lambda p, pr=pr, po=po: mm_group(
            p, po[:, 128 * (pr % 4):128 * (pr % 4 + 1)],
            [(A2[:, 1024 * t + 128 * pr:1024 * t + 128 * (pr + 1)],
              A3[:, 1024 * t + 128 * pr:1024 * t + 128 * (pr + 1)])
             for t in range(NT)]),
            deps=deps, sem="TE", inc=1, tag=f"kv{pr}")
    ks_row = [None, None]
    for h in range(2):
        po = (P[6], P[7])[h]
        ks_row[h] = S.add("tensor", lambda p, h=h, po=po: mm_group(
            p, po[0:1, :],
            [(ones_c[:, :], A2[:, 1024 * t + 512 * h:1024 * t + 512 * (h + 1)])
             for t in range(NT)]),
            deps=[m_ones_c, kv_last], sem="TE", inc=1, tag=f"ksr{h}")
    M_kv = ks_row[1]
    ksrow_s = SCR[0:1, 0:1024]
    ev_ksr = []
    for h in range(2):
        po = (P[6], P[7])[h]
        ev_ksr.append(S.add("vector", lambda p, h=h, po=po: p.tensor_copy(
            ksrow_s[:, 512 * h:512 * (h + 1)], po[0:1, :]),
            deps=[ks_row[h]], sem="VE", inc=1, tag=f"ksrev{h}"))
    tr_last = None
    for c in range(NCH):
        tr_last = S.add("tensor", lambda p, c=c: p.transpose(
            P[6][:, c:c + 1], ksrow_s[:, 128 * c:128 * (c + 1)],
            ident_s[0:1, 0:1]),
            deps=[ev_ksr[0], ev_ksr[1], ld["ident"]], sem="TE", inc=1,
            tag=f"kstr{c}")

    ev_kv1 = S.add("scalar", lambda p: p.copy(exch_v[:, 0:512], P[4][:, :]),
                   deps=[M_kv], sem="SE", inc=1)
    ev_kv2 = S.add("scalar", lambda p: p.copy(exch_v[:, 512:1024], P[5][:, :]),
                   deps=[M_kv], sem="SE", inc=1)
    ev_ks = S.add("scalar", lambda p: p.copy(exks_v[:, :], P[6][:, 0:8]),
                  deps=[tr_last], sem="SE", inc=1)

    d_xo1 = S.add("gpsimd", lambda p: p.dma_start(out=cc_in[:, 0:1024],
                                                  in_=exch_v[:, :]),
                  deps=[ev_kv1, ev_kv2], sem="GD", inc=16, tag="xout1",
                  new_group=True)
    d_xo2 = S.add("gpsimd", lambda p: p.dma_start(out=cc_in[:, 1024:1032],
                                                  in_=exks_v[:, :]),
                  deps=[ev_ks], sem="GD", inc=16, tag="xout2")
    d_cc = S.add("gpsimd", lambda p: p.collective_compute(
        "AllReduce", OP.add, replica_groups=[[0, 1], [2, 3], [4, 5], [6, 7]],
        ins=[cc_in], outs=[cc_out]),
        deps=[d_xo1, d_xo2], sem="CC", inc=1, tag="allreduce")
    d_xi1 = S.add("gpsimd", lambda p: p.dma_start(out=exch_v[:, :],
                                                  in_=cc_out[:, 0:1024]),
                  deps=[d_cc], sem="GD", inc=16, tag="xin1",
                  new_group=True)
    d_xi2 = S.add("gpsimd", lambda p: p.dma_start(out=exks_v[:, :],
                                                  in_=cc_out[:, 1024:1032]),
                  deps=[d_cc], sem="GD", inc=16, tag="xin2")


    wcn_deps = [gd[f"wcn{c}"] for c in range(NCH)] + [gd["win"]]
    wcw_ev = []
    for cc2 in range(NCH):
        po = P[cc2 % 2]
        dn = S.add("tensor", lambda p, cc2=cc2, po=po: mm_group(
            p, po[:, 0:64],
            [(A4[:, 1024 * jc + 128 * cc2:1024 * jc + 128 * (cc2 + 1)],
              win_b[:, 64 * jc:64 * (jc + 1)]) for jc in range(NCH)]),
            deps=(wcn_deps + [ktil[NT - 2]] if cc2 < 2 else [wcw_ev[cc2 - 2]]),
            sem="TE", inc=1, tag=f"wcw{cc2}")
        ev = S.add("vector", lambda p, cc2=cc2, po=po: p.tensor_copy(
            wcw_b[:, 64 * cc2:64 * (cc2 + 1)], po[:, 0:64]),
            deps=[dn], sem="VE", inc=1, tag=f"wcwev{cc2}")
        wcw_ev.append(ev)
    M_wcw = wcw_ev[-1]

    d_bcw = S.add("tensor", lambda p: mm_group(
        p, P[7][0:64, 0:1],
        [(win_b[:, 64 * jc:64 * (jc + 1)], bcb[:, jc:jc + 1]) for jc in range(NCH)]),
        deps=[d_bcc, gd["win"], vtil[NT - 1], ev_ksr[1]], sem="TE", inc=1,
        tag="bcwin")
    d_cs = S.add("tensor", lambda p: mm_group(
        p, P[7][0:64, 1:2],
        [(win_b[:, 64 * jc:64 * (jc + 1)], ones_c[:, :]) for jc in range(NCH)]),
        deps=[m_ones_c], sem="TE", inc=1, tag="cs")
    ev_bcw = S.add("vector", lambda p: p.tensor_copy(M_bcw, P[7][0:64, 0:1]),
                   deps=[d_bcw], sem="VE", inc=1)
    ev_cs = S.add("vector", lambda p: p.tensor_copy(M_cs, P[7][0:64, 1:2]),
                  deps=[d_cs], sem="VE", inc=1)


    q_mm = [[None] * 4 for _ in range(NCH)]
    qtil = [[None] * 4 for _ in range(NCH)]
    prev_q = {}
    for c in range(NCH):
        for th in range(4):
            slot = (c * 4 + th) % 4
            po = P[slot]
            deps = wq_all + [ktil[NT - 2], vtil[NT - 2]]
            if slot < 2:
                deps = deps + [M_wcw]
            if slot in prev_q:
                deps = [wq_all[-1], prev_q[slot]]
            dn = S.add("tensor", lambda p, c=c, th=th, po=po: mm_group(
                p, po[:, :],
                [(W0[:, 1024 * kc + 128 * c:1024 * kc + 128 * (c + 1)],
                  A1[:, 2048 * kc + 512 * th:2048 * kc + 512 * (th + 1)])
                 for kc in range(NCH)]),
                deps=deps, sem="TE", inc=1, tag=f"q{c}.{th}")
            q_mm[c][th] = dn
            prev_qtil = qtil[c][th - 1] if th > 0 else (qtil[c - 1][3] if c > 0
                                                        else ktil[NT - 1])
            d_r = S.add("scalar", lambda p, c=c, po=po: p.activation(
                e2[:, 0:512], po[:, :], AF.Relu, bias=C_negbq[:, c:c + 1], scale=-1.0),
                deps=[dn, d_negbq, prev_qtil], sem="SE", inc=1)
            d_e = S.add("scalar", lambda p: p.activation(
                e2[:, 0:512], e2[:, 0:512], AF.Exp, scale=-1.0),
                deps=[d_r], sem="SE", inc=1)
            depq = [d_e, d_bq1]
            if c == 0 and th == 0:
                depq += [M_wcw]
            d_q = S.add("vector", lambda p, c=c, th=th, po=po: p.scalar_tensor_tensor(
                A4[:, 2048 * c + 512 * th:2048 * c + 512 * (th + 1)],
                po[:, :], C_bq1[:, c:c + 1], e2[:, 0:512], OP.add, OP.max),
                deps=depq, sem="VE", inc=1, tag=f"qtil{c}.{th}")
            qtil[c][th] = d_q
            prev_q[slot] = d_q

    if _DEBUG_TAPS == 4:
        st_total = 0
        for name, src, dt in [("d_qt", A4[:, :], BF16), ("d_kt", A2[:, :], BF16),
                              ("d_vt", A3[:, :], BF16), ("d_xb", A1[:, :], BF16)]:
            dst = nc.dram_tensor(name, list(src.shape), dt,
                                 kind="ExternalOutput").ap()
            S.add("sync", lambda p, dd=dst, ss=src: p.dma_start(out=dd, in_=ss),
                  deps=[("VE", S.counts["VE"]), ("TE", S.counts["TE"]),
                        ("SE", S.counts["SE"]), ("GD", S.counts["GD"])],
                  sem="ST", inc=16, tag=f"tap4.{name}")
            st_total += 16
        S.add("sync", lambda p: p.dma_start(out=out_tm[0:128, :], in_=RB0[:, :]),
              deps=[ld["repbk"]], sem="ST", inc=16)
        st_total += 16
        S.add("sync", lambda p: (p.wait_ge(sems["ST"], st_total),
                             p.wait_ge(sems["ST2"], st2_total)) and None,
          tag="final")
        S.finalize()
        with nc.Block() as block:
            block.sync(lambda p: S.emit("sync", p, sems))
            block.gpsimd(lambda p: S.emit("gpsimd", p, sems))
            block.tensor(lambda p: S.emit("tensor", p, sems))
            block.vector(lambda p: S.emit("vector", p, sems))
            block.scalar(lambda p: S.emit("scalar", p, sems))
        ctx.close()
        return nc

    S.add("scalar", lambda p: p.activation(zmz[0:1, 0:1], ones_r[0:1, 0:1], AF.Sqrt),
          deps=[m_ones_r], sem="SE", inc=1, tag="sqrt_preload")


    d_kvb = S.add("vector", lambda p: p.tensor_tensor(
        kv_b[:, :].rearrange("p (a b) -> p a b", a=8),
        exch_v[:, :].rearrange("p (a b) -> p a b", a=8),
        bmask_s[:, :].unsqueeze(1).broadcast_to([128, 8, 128]), OP.mult),
        deps=[d_xi1, ld["bmask"]], sem="VE", inc=1, tag="kvb")
    d_lden = []
    for c in range(NCH):
        d_lden.append(S.add("vector", lambda p, c=c: p.tensor_scalar_mul(
            lden[:, 16 * c:16 * (c + 1)], maskC_s[:, 16 * c:16 * (c + 1)],
            exks_v[:, c:c + 1]),
            deps=[d_xi2, ld["maskC"]], sem="VE", inc=1, tag=f"lden{c}"))

    den = []
    for th in range(4):
        po = P[4 + th % 2]
        deps = [qtil[c][th] for c in range(NCH)] + d_lden
        deps += [ev_kv1, ev_kv2] if th < 2 else [den[th - 2][1]]
        dn = S.add("tensor", lambda p, th=th, po=po: mm_group(
            p, po[0:16, :],
            [(lden[:, 16 * c:16 * (c + 1)],
              A4[:, 2048 * c + 512 * th:2048 * c + 512 * (th + 1)])
             for c in range(NCH)]),
            deps=deps, sem="TE", inc=1, tag=f"den{th}")
        ev = S.add("vector", lambda p, th=th, po=po: p.tensor_scalar_add(
            den_v[:, 512 * th:512 * (th + 1)], po[0:16, :], EPS_Z),
            deps=[dn, M_kv], sem="VE", inc=1, tag=f"denev{th}")
        den.append((dn, ev))
    sx_ev = []
    sx_mm = []
    prev_sx = None
    for ts in range(4):
        po = P[2] if ts % 2 == 0 else P[3]
        deps = xb_done + [gd["win"], qtil[NCH - 1][2 + ts % 2]]
        if prev_sx:
            deps = deps + [prev_sx]
        dn = S.add("tensor", lambda p, ts=ts, po=po: mm_group(
            p, po[0:64, :],
            [(win_b[:, 64 * kc:64 * (kc + 1)],
              A1[:, 2048 * kc + 512 * ts:2048 * kc + 512 * (ts + 1)])
             for kc in range(NCH)]),
            deps=deps, sem="TE", inc=1, tag=f"sx{ts}")
        sx_mm.append(dn)
        ev = S.add("vector", lambda p, ts=ts, po=po: p.tensor_copy(
            s_x[:, 512 * ts:512 * (ts + 1)], po[0:64, :]),
            deps=[dn], sem="VE", inc=1, tag=f"sxev{ts}")
        prev_sx = ev
        sx_ev.append(ev)

    d_zlo4 = [None] * 4
    for th in range(4):
        dq = slice(512 * th, 512 * (th + 1))
        d_ra = S.add("vector", lambda p, dq=dq: p.reciprocal(
            zrec_v[:, dq], den_v[:, dq]),
            deps=[den[th][1]], sem="VE", inc=1, tag=f"recip{th}")
        d_zh = S.add("vector", lambda p, dq=dq: p.tensor_copy(zrh_b[:, dq],
                                                              zrec_v[:, dq]),
                     deps=[d_ra, sx_mm[3]], sem="VE", inc=1)
        d_zhf = S.add("vector", lambda p, dq=dq: p.tensor_copy(zhi_f[:, dq],
                                                               zrh_b[:, dq]),
                      deps=[d_zh], sem="VE", inc=1)
        d_zlo4[th] = S.add("vector", lambda p, dq=dq: p.tensor_tensor(
            zrl_b[:, dq], zrec_v[:, dq], zhi_f[:, dq], OP.subtract),
            deps=[d_zhf], sem="VE", inc=1, tag=f"zlo{th}")
    d_rec = d_zlo4[3]
    d_zlo = d_zlo4[3]

    M_sx = sx_mm[3]
    M_q = q_mm[NCH - 1][3]

    if _DEBUG_TAPS == 3:
        st_total = 0
        dst_e = nc.dram_tensor("d_qt_early", [128, 16384], BF16,
                               kind="ExternalOutput").ap()
        S.add("sync", lambda p: p.dma_start(out=dst_e, in_=A4[:, :]),
              deps=[qtil[NCH - 1][3]], sem="ST", inc=16, tag="tapE")
        st_total += 16
        taps3 = [
            ("d_qt", A4[:, :], BF16),
            ("d_qh", A1[:, :], BF16),
            ("d_den", A2[0:16, 0:8192], BF16),
            ("d_lden", lden[:, :], BF16),
            ("d_exks", exks_v[:, :], F32),
            ("d_kvb", kv_b[:, :], BF16),
            ("d_w0", W0[:, :], BF16),
        ]
        for name, src, dt in taps3:
            dst = nc.dram_tensor(name, list(src.shape), dt,
                                 kind="ExternalOutput").ap()
            S.add("sync", lambda p, dd=dst, ss=src: p.dma_start(out=dd, in_=ss),
                  deps=[("VE", S.counts["VE"]), ("TE", S.counts["TE"]),
                        ("SE", S.counts["SE"]), ("GD", S.counts["GD"])],
                  sem="ST", inc=16, tag=f"tap3.{name}")
            st_total += 16
        S.add("sync", lambda p: p.dma_start(out=out_tm[0:128, :], in_=RB0[:, :]),
              deps=[ld["repbk"]], sem="ST", inc=16)
        st_total += 16
        S.add("sync", lambda p: p.wait_ge(sems["ST"], st_total) and None, tag="final")
        S.finalize()
        with nc.Block() as block:
            block.sync(lambda p: S.emit("sync", p, sems))
            block.gpsimd(lambda p: S.emit("gpsimd", p, sems))
            block.tensor(lambda p: S.emit("tensor", p, sems))
            block.vector(lambda p: S.emit("vector", p, sems))
            block.scalar(lambda p: S.emit("scalar", p, sems))
        ctx.close()
        return nc

    y_ev = [[None] * 4 for _ in range(NCH)]
    prev_y = {}
    prev_z = {}
    prev_zcp = {}
    last_zcp = {}
    for c in range(NCH):
        for th in range(4):
            idx = c * 4 + th
            slot = idx % 4
            po = P[slot]
            zpar = idx % 2
            zp = P[6] if zpar == 0 else P[7]
            zdeps = [d_zlo4[th], gd["zsel"]]
            if zpar in prev_z:
                zdeps += [prev_z[zpar]]
            elif zpar == 0:
                zdeps += [ev_ks]
            else:
                zdeps += [ev_bcw, ev_cs]
            d_zmm = S.add("tensor", lambda p, c=c, th=th, zp=zp: (
                p.matmul(zp[:, :], zsel_b[0:16, 128 * c:128 * (c + 1)],
                         zrh_b[0:16, 512 * th:512 * (th + 1)],
                         start=True, stop=False),
                p.matmul(zp[:, :], zsel_b[0:16, 128 * c:128 * (c + 1)],
                         zrl_b[0:16, 512 * th:512 * (th + 1)],
                         start=False, stop=True))[-1],
                deps=zdeps, sem="TE", inc=1, tag=f"zmm{c}.{th}")
            cpdeps = [d_zmm]
            if slot in prev_zcp:
                cpdeps += [prev_zcp[slot]]
            else:
                cpdeps += [M_q, M_sx]
            d_zcp = S.add("scalar", lambda p, slot=slot, zp=zp: p.copy(
                zrep[slot], zp[:, :]), deps=cpdeps, sem="SE", inc=1,
                tag=f"zcp{c}.{th}")
            prev_z[zpar] = d_zcp
            last_zcp[zpar] = d_zcp
            ydeps = [qtil[c][th], d_kvb]
            if slot in prev_y:
                ydeps += [prev_y[slot]]
            else:
                ydeps += [qtil[NCH - 1][slot]]
                if slot >= 2:
                    ydeps += [sx_ev[slot]]
            dn = S.add("tensor", lambda p, c=c, th=th, po=po: p.matmul(
                po[:, :], kv_b[:, 128 * c:128 * (c + 1)],
                A4[:, 2048 * c + 512 * th:2048 * c + 512 * (th + 1)],
                start=True, stop=True),
                deps=ydeps, sem="TE", inc=1, tag=f"y{c}.{th}")
            evdeps = [dn, d_zcp, M_kv]
            if c in (0, 2):
                evdeps += [d_zlo4[th // 2]]
            elif c in (1, 3):
                evdeps += [d_zlo4[2 + th // 2]]
            elif c == 5:
                evdeps += [d_lden[7]]
            ev = S.add("vector", lambda p, c=c, th=th, po=po, slot=slot:
                p.tensor_tensor(
                    A2[:, 2048 * c + 512 * th:2048 * c + 512 * (th + 1)],
                    po[:, :], zrep[slot], OP.mult),
                deps=evdeps, sem="VE", inc=1, tag=f"yev{c}.{th}")
            y_ev[c][th] = ev
            prev_y[slot] = ev
            prev_zcp[slot] = ev
    M_y = y_ev[NCH - 1][3]

    sattn_mm = []
    for ts in range(4):
        po = P[4 + ts // 2]
        reg = po[64 * (ts % 2):64 * (ts % 2) + 64, :]
        deps = [M_y, M_wcw]
        deps += [den[2 + (ts // 2)][1]]
        dn = S.add("tensor", lambda p, ts=ts, reg=reg: mm_group(
            p, reg,
            [(wcw_b[:, 64 * c:64 * (c + 1)],
              A2[:, 2048 * c + 512 * ts:2048 * c + 512 * (ts + 1)])
             for c in range(NCH)]),
            deps=deps, sem="TE", inc=1, tag=f"sattn{ts}")
        sattn_mm.append((dn, reg))

    if _DEBUG_TAPS == 2:
        xld0 = S.add("sync", lambda p: p.dma_start(out=xin[0],
                                                   in_=x_tm[0:128, :]),
                     deps=[ktil[NT - 1]], sem="LD", inc=16, tag="xld0")
        slots = (P[0], P[1])
        def attn_emit0(p):
            ins = None
            for jh in range(2):
                pairs = [(A2[:, 2048 * kc:2048 * kc + 128],
                          W1[:, 1024 * kc + 512 * jh:1024 * kc + 512 * (jh + 1)])
                         for kc in range(NCH)]
                pairs.append((ones_r[:, :], brows_b[0:1, 512 * jh:512 * (jh + 1)]))
                ins = mm_group(p, slots[jh][:, :], pairs)
            return ins
        at0 = S.add("tensor", attn_emit0,
                    deps=wc_all + [M_y, gd["brows"], m_ones_r, qhat[NCH - 1]],
                    sem="TE", inc=1, tag="attn0")
        ev0 = S.add("scalar", lambda p: p.copy(f32_tile(A4, 4)[:, 0:512],
                                               slots[0][:, :]),
                    deps=[at0], sem="SE", inc=1)
        ev1 = S.add("scalar", lambda p: p.copy(f32_tile(A4, 4)[:, 512:1024],
                                               slots[1][:, :]),
                    deps=[at0], sem="SE", inc=1)
        st_total = 0
        taps2 = [
            ("d_y", A2[:, :], BF16, [("SE", S.counts["SE"])]),
            ("d_qh", A1[:, :], BF16, None),
            ("d_qt", A4[:, 0:8192], BF16, None),
            ("d_w1", W1[:, :], BF16, None),
            ("d_attn0", f32_tile(A4, 4)[:, :], F32, [ev0, ev1]),
            ("d_xin0", xin[0][:, :], F32, [xld0]),
            ("d_kvb", kv_b[:, :], BF16, None),
            ("d_brows", brows_b[:, :], BF16, None),
        ]
        for name, src, dt, extra in taps2:
            dst = nc.dram_tensor(name, list(src.shape), dt,
                                 kind="ExternalOutput").ap()
            S.add("sync", lambda p, dd=dst, ss=src: p.dma_start(out=dd, in_=ss),
                  deps=[("VE", S.counts["VE"]), ("TE", S.counts["TE"])] +
                       (extra or []),
                  sem="ST", inc=16, tag=f"tap2.{name}")
            st_total += 16
        S.add("sync", lambda p: p.dma_start(out=out_tm[0:128, :], in_=xin[0][:, :]),
              deps=[xld0], sem="ST", inc=16)
        st_total += 16
        S.add("sync", lambda p: p.wait_ge(sems["ST"], st_total) and None, tag="final")
        S.finalize()
        with nc.Block() as block:
            block.sync(lambda p: S.emit("sync", p, sems))
            block.gpsimd(lambda p: S.emit("gpsimd", p, sems))
            block.tensor(lambda p: S.emit("tensor", p, sems))
            block.vector(lambda p: S.emit("vector", p, sems))
            block.scalar(lambda p: S.emit("scalar", p, sems))
        ctx.close()
        return nc

    xld = [None] * NT
    attn_mm = [None] * NT
    r1d = [None] * NT
    ssq1d = [None] * NT
    prev_at = {}
    prev_r1 = {}
    for t in range(NT):
        if t == 0:
            xdeps = [ktil[NT - 1]]
        elif t == 1:
            xdeps = [vtil[NT - 1]]
        else:
            xdeps = [prev_r1[t % 2]]
        xld[t] = S.add("sync", lambda p, t=t: p.dma_start(
            out=xin[t % 2], in_=x_tm[128 * t:128 * (t + 1), :]),
            deps=xdeps, sem="LD", inc=16, tag=f"xld{t}", new_group=True)
        slots = (P[0], P[1]) if t % 2 == 0 else (P[2], P[3])
        deps = wc_all + [M_y, gd["brows"], m_ones_r]
        if t % 2 in prev_at:
            deps += [prev_at[t % 2]]
        else:
            deps += [M_y]
        def attn_emit(p, t=t, slots=slots):
            ins = None
            for jh in range(2):
                pairs = [(A2[:, 2048 * kc + 128 * t:2048 * kc + 128 * (t + 1)],
                          W1[:, 1024 * kc + 512 * jh:1024 * kc + 512 * (jh + 1)])
                         for kc in range(NCH)]
                pairs.append((ones_r[:, :], brows_b[0:1, 512 * jh:512 * (jh + 1)]))
                ins = mm_group(p, slots[jh][:, :], pairs)
            return ins
        attn_mm[t] = S.add("tensor", attn_emit, deps=deps, sem="TE", inc=1,
                           tag=f"attn{t}")

        def r1_emit(p, t=t, slots=slots):
            p.scalar_tensor_tensor(r1_t(t)[:, 0:512], slots[0][:, :], 1.0,
                                   xin[t % 2][:, 0:512], OP.mult, OP.add,
                                   accum_out=C_sum1a[:, t:t + 1])
            return p.scalar_tensor_tensor(r1_t(t)[:, 512:1024], slots[1][:, :], 1.0,
                                          xin[t % 2][:, 512:1024], OP.mult, OP.add,
                                          accum_out=C_sum1b[:, t:t + 1])
        extra = [attn_mm[t], xld[t]]
        if t < 2:
            extra += [M_kv]
        r1d[t] = S.add("vector", r1_emit, deps=extra, sem="VE", inc=1, tag=f"r1.{t}")
        prev_r1[t % 2] = r1d[t]
        prev_at[t % 2] = r1d[t]
        ssq1d[t] = S.add("scalar", lambda p, t=t: p.activation(
            sqscr[:, 0:1024], r1_t(t)[:, :], AF.Square, accum_out=C_ssq1[:, t:t + 1]),
            deps=[r1d[t]], sem="SE", inc=1, tag=f"ssq1.{t}")

    d_s1 = S.add("vector", lambda p: p.tensor_add(L_sum, C_sum1a, C_sum1b),
                 deps=[r1d[NT - 1]], sem="VE", inc=1)
    d_m1 = S.add("vector", lambda p: p.tensor_scalar_mul(C_m1, L_sum, 1.0 / D),
                 deps=[d_s1], sem="VE", inc=1)
    d_msq = S.add("vector", lambda p: p.tensor_mul(L_msq, C_m1, C_m1),
                  deps=[d_m1], sem="VE", inc=1)
    d_var = S.add("vector", lambda p: p.scalar_tensor_tensor(
        L_var, C_ssq1, 1.0 / D, L_msq, OP.mult, OP.subtract),
        deps=[d_msq, ssq1d[NT - 1]], sem="VE", inc=1)
    d_vpe = S.add("vector", lambda p: p.tensor_scalar_add(L_var, L_var, EPS_LN),
                  deps=[d_var], sem="VE", inc=1)
    d_ri = S.add("vector", lambda p: p.reciprocal(L_rinv, L_var),
                 deps=[d_vpe], sem="VE", inc=1)
    d_rs1 = S.add("scalar", lambda p: p.activation(L_rstd1, L_rinv, AF.Sqrt),
                  deps=[d_ri], sem="SE", inc=1, tag="sqrt1")
    d_mrs1 = S.add("vector", lambda p: p.tensor_mul(L_mrs1, C_m1, L_rstd1),
                   deps=[d_rs1], sem="VE", inc=1)

    d_zmz1 = S.add("vector", lambda p: p.tensor_copy(zmz[:, 0:16], L_rstd1),
                   deps=[d_rs1], sem="VE", inc=1)
    d_zmz2 = S.add("vector", lambda p: p.tensor_copy(zmz[:, 16:32], L_mrs1),
                   deps=[d_mrs1], sem="VE", inc=1)
    d_ztr = S.add("tensor", lambda p: p.transpose(P[6][0:32, 0:128], zmz[:, :],
                                                  ident_s[:, :]),
                  deps=[d_zmz1, d_zmz2, ld["ident"], last_zcp[0]], sem="TE",
                  inc=1, tag="ztr")


    d_zwf = S.add("vector", lambda p: p.tensor_copy(zwf, P[6][0:32, 0:128]),
                  deps=[d_ztr, attn_mm[NT - 1]], sem="VE", inc=1, tag="zwf")
    d_zwlo = d_zwf

    def zbc_emit(p):
        ins = None
        for g in range(8):
            pslot = P[g // 2]
            half = 64 * (g % 2)
            for i in range(4):
                row = 4 * g + i if g < 4 else 4 * (g - 4) + i + 16
                reg = pslot[half:half + 64, 128 * i:128 * (i + 1)]
                ins = p.matmul(
                    reg,
                    ident_s[0:32, row:row + 1].broadcast_to([32, 64]),
                    zwf[0:32, :], start=True, stop=True,
                    skip_group_check=True)
        return ins
    d_zbc = S.add("tensor", zbc_emit,
                  deps=[d_zwf, ld["ident"], r1d[NT - 1]], sem="TE", inc=1,
                  tag="zbc")
    zev = []
    for g in range(8):
        dest = z1rep if g < 4 else mz1rep
        grp = g % 4
        zev.append(S.add("vector", lambda p, g=g, grp=grp, dest=dest: p.tensor_copy(
            dest[:, 512 * grp:512 * (grp + 1)],
            P[g // 2][64 * (g % 2):64 * (g % 2) + 64, 0:512]),
            deps=[d_zbc, q_mm[NCH - 1][3]], sem="VE", inc=1, tag=f"zbcev{g}"))
    d_z1r = zev[3]
    d_mz1r = zev[7]

    sbd = []
    f1d = []
    for ts in range(4):
        reg = sattn_mm[ts][1]
        d_f1 = S.add("vector", lambda p, ts=ts, reg=reg: p.scalar_tensor_tensor(
            t1[:, :], reg, M_bcw, s_x[:, 512 * ts:512 * (ts + 1)], OP.add, OP.add),
            deps=[sattn_mm[ts][0], ev_bcw, sx_ev[ts], r1d[NT - 1]], sem="VE", inc=1,
            tag=f"f1.{ts}")
        d_f2 = S.add("vector", lambda p, ts=ts: p.tensor_tensor(
            t2[:, :], t1[:, :], z1rep[:, 512 * ts:512 * (ts + 1)], OP.mult),
            deps=[d_f1, d_z1r], sem="VE", inc=1, tag=f"f2.{ts}")
        d_f3 = S.add("vector", lambda p, ts=ts: p.tensor_scalar_mul(
            t1[:, :], mz1rep[:, 512 * ts:512 * (ts + 1)], M_cs),
            deps=[d_f2, d_mz1r, ev_cs], sem="VE", inc=1, tag=f"f3.{ts}")
        d_f4 = S.add("vector", lambda p, ts=ts: p.scalar_tensor_tensor(
            s_b[:, 512 * ts:512 * (ts + 1)], t2[:, :], M_bin, t1[:, :],
            OP.add, OP.subtract),
            deps=[d_f3, ld["bin"], d_rec, attn_mm[NT - 1]], sem="VE", inc=1,
            tag=f"f4.{ts}")
        sbd.append(d_f4)
        f1d.append(d_f1)

    x2d = [None] * NT
    for t in range(NT):
        deps = [d_mrs1, r1d[t]]
        if t < 8:
            deps += [M_q, M_sx]
        else:
            deps += [x2d[t - 8], ssq1d[t - 8]]
        x2d[t] = S.add("vector", lambda p, t=t: p.tensor_scalar(
            x2_t(t)[:, :], r1_t(t)[:, :], L_rstd1[:, t:t + 1], L_mrs1[:, t:t + 1],
            OP.mult, OP.subtract),
            deps=deps, sem="VE", inc=1, tag=f"x2.{t}")


    gdone = [None] * 4
    r2d = [None] * NT
    ssq2d = [None] * NT
    prev_f = {}

    def emit_G(qa, qb, c, hA, hB, gA, gB, pair_idx):
        depsA = [hA[c], gd["bdw2"]]
        if c == 0:
            depsA += [d_zwlo] if pair_idx == 0 else [gdone[0], gdone[1]]
        gA[c] = S.add("tensor", lambda p, c=c: p.matmul(
            P[6][0:64, :], bdw2_b[:, 64 * c:64 * (c + 1)], h_b[c % 2][:, :],
            start=(c == 0), stop=(c == NCH - 1), skip_group_check=True),
            deps=depsA, sem="TE", inc=1, tag=f"GA.{c}")
        depsB = [hB[c], gd["bdw2"]]
        if c == 0:
            depsB += [d_zwlo] if pair_idx == 0 else [gdone[0], gdone[1]]
        gB[c] = S.add("tensor", lambda p, c=c: p.matmul(
            P[6][64:128, :], bdw2_b[:, 64 * c:64 * (c + 1)], h_b[2 + c % 2][:, :],
            start=(c == 0), stop=(c == NCH - 1), skip_group_check=True),
            deps=depsB, sem="TE", inc=1, tag=f"GB.{c}")

    def emit_ffn_tile(t, deep=False):
        if deep:
            slots = (P[2 * (t % 4)], P[2 * (t % 4) + 1])
            key = t % 4
        else:
            slots = (P[0], P[1])
            key = 0
        deps = [gdone[t // 4], gd["wout"], gd["brows"], m_ones_r]
        if key in prev_f:
            deps += [prev_f[key]]
        else:
            deps += [r1d[NT - 1]]
        if deep and t % 4 >= 1:
            deps += [hA_pp[NCH - 1], hB_pp[NCH - 1], gdone[2], gdone[3]]
        def ffn_emit(p, t=t, slots=slots):
            ins = None
            for jh in range(2):
                pairs = [(g_b[:, 128 * t:128 * (t + 1)],
                          wout_b[0:64, 512 * jh:512 * (jh + 1)]),
                         (ones_r[:, :],
                          brows_b[0:1, 1024 + 512 * jh:1024 + 512 * (jh + 1)])]
                ins = mm_group(p, slots[jh][:, :], pairs)
            return ins
        d_ffn = S.add("tensor", ffn_emit, deps=deps, sem="TE", inc=1, tag=f"ffn{t}")

        def r2_emit(p, t=t, slots=slots):
            p.scalar_tensor_tensor(r2_t(t)[:, 0:512], slots[0][:, :], 1.0,
                                   x2_t(t)[:, 0:512], OP.mult, OP.add,
                                   accum_out=C_sum2a[:, t:t + 1])
            return p.scalar_tensor_tensor(r2_t(t)[:, 512:1024], slots[1][:, :], 1.0,
                                          x2_t(t)[:, 512:1024], OP.mult, OP.add,
                                          accum_out=C_sum2b[:, t:t + 1])
        extra = [d_ffn, x2d[t]]
        if t < 8:
            extra += [x2d[8 + t], ssq1d[8 + t]]
        else:
            extra += [r2d[t - 8]]
        r2d[t] = S.add("vector", r2_emit, deps=extra, sem="VE", inc=1, tag=f"r2.{t}")
        prev_f[key] = r2d[t]
        ssq2d[t] = S.add("scalar", lambda p, t=t: p.activation(
            sqscr[:, 0:1024], r2_t(t)[:, :], AF.Square, accum_out=C_ssq2[:, t:t + 1]),
            deps=[r2d[t]], sem="SE", inc=1, tag=f"ssq2.{t}")

    hA_prev_pair = hB_prev_pair = None
    for pair_idx, (qa, qb) in enumerate(((0, 1), (2, 3))):
        hA = [None] * NCH; hB = [None] * NCH
        gA = [None] * NCH; gB = [None] * NCH
        sA = [None] * NCH; sB = [None] * NCH
        prev_hA = {}; prev_hB = {}
        for c in range(NCH):
            psA = P[4 + (c % 2)]
            depsA = [sbd[qa], gd["selS"]] + ([f1d[0], f1d[1]] if c % 2 == 0
                                             else [f1d[2], f1d[3]])
            if (c % 2) in prev_hA:
                depsA += [prev_hA[c % 2]]
            elif pair_idx == 1:
                depsA += [hA_prev_pair[6 + (c % 2)]]
            sA[c] = S.add("tensor", lambda p, qa=qa, c=c, psA=psA: p.matmul(
                psA[:, :], selS_b[:, 128 * c:128 * (c + 1)],
                s_b[0:64, 512 * qa:512 * (qa + 1)], start=True, stop=True,
                skip_group_check=True),
                deps=depsA, sem="TE", inc=1, tag=f"srep{qa}.{c}")
            psB = P[7] if c % 2 == 0 else P[3]
            depsB = [sbd[qb], gd["selS"]]
            if (c % 2) in prev_hB:
                depsB += [prev_hB[c % 2]]
            elif pair_idx == 1:
                depsB += [hB_prev_pair[6 + (c % 2)]]
            if pair_idx == 0 and c % 2 == 1:
                depsB += [r1d[NT - 1]]
            sB[c] = S.add("tensor", lambda p, qb=qb, c=c, psB=psB: p.matmul(
                psB[:, :], selS_b[:, 128 * c:128 * (c + 1)],
                s_b[0:64, 512 * qb:512 * (qb + 1)], start=True, stop=True,
                skip_group_check=True),
                deps=depsB, sem="TE", inc=1, tag=f"srep{qb}.{c}")
            if pair_idx == 1:
                emit_ffn_tile(c)
            if c > 0:
                emit_G(qa, qb, c - 1, hA, hB, gA, gB, pair_idx)
            hA[c] = S.add("scalar", lambda p, c=c, psA=psA: p.activation(
                h_b[c % 2][:, :], psA[:, :], AF.Relu,
                bias=C_b1[:, c:c + 1], scale=C_w1[:, c:c + 1]),
                deps=[sA[c], ld["w1"], ld["b1"]] + ([gA[c - 2]] if c >= 2 else []) +
                     ([sbd[3]] if pair_idx == 0 and c < 2 else []),
                sem="SE", inc=1, tag=f"h{qa}.{c}")
            prev_hA[c % 2] = hA[c]
            hB[c] = S.add("scalar", lambda p, c=c, psB=psB: p.activation(
                h_b[2 + c % 2][:, :], psB[:, :], AF.Relu,
                bias=C_b1[:, c:c + 1], scale=C_w1[:, c:c + 1]),
                deps=[sB[c], ld["w1"], ld["b1"]] + ([gB[c - 2]] if c >= 2 else []) +
                     ([sbd[3]] if pair_idx == 0 and c < 2 else []),
                sem="SE", inc=1, tag=f"h{qb}.{c}")
            prev_hB[c % 2] = hB[c]
        emit_G(qa, qb, NCH - 1, hA, hB, gA, gB, pair_idx)
        gdone[qa] = S.add("vector", lambda p, qa=qa: p.tensor_scalar_add(
            g_b[:, 512 * qa:512 * (qa + 1)], P[6][0:64, :], M_b2),
            deps=[gA[NCH - 1], ld["b2"]], sem="VE", inc=1, tag=f"g{qa}")
        gdone[qb] = S.add("vector", lambda p, qb=qb: p.tensor_scalar_add(
            g_b[:, 512 * qb:512 * (qb + 1)], P[6][64:128, :], M_b2),
            deps=[gB[NCH - 1], ld["b2"]], sem="VE", inc=1, tag=f"g{qb}")
        hA_prev_pair = hA
        hB_prev_pair = hB

    hA_pp, hB_pp = hA_prev_pair, hB_prev_pair

    g_mrs2 = [None] * 4
    g_mrs2n = [None] * 4
    prev_st = {}

    def emit_ln2_group(g):
        c4 = slice(4 * g, 4 * (g + 1))
        scr = L_w2[:, 4 * g:4 * (g + 1)]
        deps = [r2d[t] for t in range(4 * g, 4 * g + 4)]
        d1 = S.add("vector", lambda p: p.tensor_add(
            scr, C_sum2a[:, c4], C_sum2b[:, c4]), deps=deps, sem="VE", inc=1,
            tag=f"ln2a{g}")
        d2 = S.add("vector", lambda p: p.tensor_scalar_mul(
            L_m2[:, c4], scr, 1.0 / D), deps=[d1], sem="VE", inc=1)
        d3 = S.add("vector", lambda p: p.tensor_mul(
            scr, L_m2[:, c4], L_m2[:, c4]), deps=[d2], sem="VE", inc=1)
        d4 = S.add("vector", lambda p: p.scalar_tensor_tensor(
            scr, C_ssq2[:, c4], 1.0 / D, scr, OP.mult, OP.subtract),
            deps=[d3] + [ssq2d[t] for t in range(4 * g, 4 * g + 4)],
            sem="VE", inc=1)
        d5 = S.add("vector", lambda p: p.tensor_scalar_add(scr, scr, EPS_LN),
                   deps=[d4], sem="VE", inc=1)
        d6 = S.add("vector", lambda p: p.reciprocal(scr, scr),
                   deps=[d5], sem="VE", inc=1)
        d7 = S.add("scalar", lambda p: p.activation(
            L_rstd2[:, c4], scr, AF.Sqrt), deps=[d6], sem="SE", inc=1,
            tag=f"sqrt2.{g}")
        d8 = S.add("vector", lambda p: p.tensor_mul(
            L_mrs2[:, c4], L_m2[:, c4], L_rstd2[:, c4]),
            deps=[d7], sem="VE", inc=1)
        d9 = S.add("vector", lambda p: p.tensor_scalar_mul(
            L_mrs2n[:, c4], L_mrs2[:, c4], -1.0), deps=[d8], sem="VE", inc=1)
        g_mrs2[g] = d8
        g_mrs2n[g] = d9

    def emit_out(t):
        g = t // 4
        deps = [g_mrs2[g], g_mrs2n[g]]
        if t % 4 in prev_st:
            deps += [prev_st[t % 4]]
        else:
            deps += [attn_mm[NT - 1], sattn_mm[3][0], d_zbc]
        if t % 2 == 0:
            d_o = S.add("vector", lambda p, t=t: p.tensor_scalar(
                outst[t % 4][:, :], r2_t(t)[:, :], L_rstd2[:, t:t + 1],
                L_mrs2[:, t:t + 1], OP.mult, OP.subtract),
                deps=deps, sem="VE", inc=1, tag=f"out.{t}")
        else:
            d_o = S.add("scalar", lambda p, t=t: p.activation(
                outst[t % 4][:, :], r2_t(t)[:, :], AF.Identity,
                bias=L_mrs2n[:, t:t + 1], scale=L_rstd2[:, t:t + 1]),
                deps=deps, sem="SE", inc=1, tag=f"out.{t}")
        eng, stsem = [("sync", "ST"), ("scalar", "SD"),
                      ("sync", "ST"), ("scalar", "SD")][t % 4]
        d_st = S.add(eng, lambda p, t=t: p.dma_start(
            out=out_tm[128 * t:128 * (t + 1), :], in_=outst[t % 4][:, :]),
            deps=[d_o], sem=stsem, inc=16, tag=f"st{t}", new_group=True)
        prev_st[t % 4] = d_st

    emit_ffn_tile(8, deep=True); emit_ffn_tile(9, deep=True)
    emit_ln2_group(0)
    emit_out(0); emit_out(1)
    emit_ffn_tile(10, deep=True); emit_ffn_tile(11, deep=True)
    emit_out(2); emit_out(3)
    emit_ln2_group(1)
    emit_ffn_tile(12, deep=True); emit_ffn_tile(13, deep=True)
    emit_out(4); emit_out(5)
    emit_ffn_tile(14, deep=True); emit_ffn_tile(15, deep=True)
    emit_out(6); emit_out(7)
    emit_ln2_group(2)
    for t in (8, 9, 10, 11):
        emit_out(t)
    emit_ln2_group(3)
    for t in (12, 13, 14, 15):
        emit_out(t)

    if _DEBUG_TAPS:
        outst[2] = outst[0]
        outst[3] = outst[1]
        tap_list = [
            ("d_sb", s_b[:, :], BF16, [gdone[3], r2d[NT - 1]]),
            ("d_gb", g_b[:, :], BF16, [gdone[3], r2d[NT - 1]]),
            ("d_r2_0", r2_t(0)[:, :], F32, [r2d[0], ("VE", S.counts["VE"])]),
            ("d_r2_8", r2_t(8)[:, :], F32, [r2d[8], ("VE", S.counts["VE"])]),
            ("d_r2_12", r2_t(12)[:, :], F32, [r2d[12], ("VE", S.counts["VE"])]),
            ("d_x2_8", x2_t(8)[:, :], F32, [("VE", S.counts["VE"])]),
            ("d_colsB", colsB[:, :], F32, [("VE", S.counts["VE"]),
                                           ("SE", S.counts["SE"])]),
            ("d_LW2", LW2[:, :], F32, [("VE", S.counts["VE"])]),
        ]
        for name, srcv, dt, deps in tap_list:
            dst = nc.dram_tensor(name, list(srcv.shape), dt,
                                 kind="ExternalOutput").ap()
            S.add("sync", lambda p, dd=dst, ss=srcv: p.dma_start(out=dd, in_=ss),
                  deps=deps, sem="ST", inc=16, tag=f"tap.{name}", new_group=True)

    def final_waits(p):
        for s in ("ST", "SD"):
            p.wait_ge(sems[s], S.counts.get(s, 0))
    S.add("sync", final_waits, tag="final")

    S.finalize()
    with nc.Block() as block:
        @block.sync
        def _(p):
            S.emit("sync", p, sems)

        @block.gpsimd
        def _(p):
            S.emit("gpsimd", p, sems)

        @block.tensor
        def _(p):
            S.emit("tensor", p, sems)

        @block.vector
        def _(p):
            S.emit("vector", p, sems)

        @block.scalar
        def _(p):
            S.emit("scalar", p, sems)

    ctx.close()
    return nc


def _consts():
    maskC = np.zeros((128, 128), dtype=np.float32)
    for c in range(NCH):
        for p in range(128):
            maskC[p, 16 * c + 2 * c + p // 64] = 1.0
    blockmask = np.zeros((128, 128), dtype=np.float32)
    for d in range(128):
        blockmask[d, (d // 64) * 64:(d // 64) * 64 + 64] = 1.0
    ident = np.eye(128, dtype=np.float32)
    selS = np.zeros((NCH, M, 128), dtype=np.float32)
    for c in range(NCH):
        for k in range(128):
            selS[c, (128 * c + k) // INNER, k] = 1.0
    zsel = np.zeros((NCH, 16, 128), dtype=np.float32)
    for c in range(NCH):
        for p in range(128):
            zsel[c, 2 * c + p // 64, p] = 1.0
    rsel = np.zeros((32, 32, M), dtype=np.float32)
    for r in range(32):
        rsel[r, r, :] = 1.0
    return maskC, blockmask, ident, selS, zsel, rsel


_NC_CACHE = {}

def kernel(L0_attn_x, L0_attn_Wq, L0_attn_bq, L0_attn_Wk, L0_attn_bk,
           L0_attn_Wv, L0_attn_bv, L0_attn_Wc, L0_attn_bc,
           L0_ffn_Win, L0_ffn_bin_, L0_ffn_W1, L0_ffn_b1, L0_ffn_W2,
           L0_ffn_b2, L0_ffn_Wout, L0_ffn_bout):
    x = np.ascontiguousarray(np.asarray(L0_attn_x, dtype=np.float32))
    Wq = np.asarray(L0_attn_Wq, np.float32); Wk = np.asarray(L0_attn_Wk, np.float32)
    Wv = np.asarray(L0_attn_Wv, np.float32); Wc = np.asarray(L0_attn_Wc, np.float32)
    bq = np.asarray(L0_attn_bq, np.float32); bk = np.asarray(L0_attn_bk, np.float32)
    bv = np.asarray(L0_attn_bv, np.float32); bc = np.asarray(L0_attn_bc, np.float32)
    Win = np.asarray(L0_ffn_Win, np.float32); bin_ = np.asarray(L0_ffn_bin_, np.float32)
    W1 = np.asarray(L0_ffn_W1, np.float32); b1 = np.asarray(L0_ffn_b1, np.float32)
    W2 = np.asarray(L0_ffn_W2, np.float32); b2 = np.asarray(L0_ffn_b2, np.float32)
    Wout = np.asarray(L0_ffn_Wout, np.float32); bout = np.asarray(L0_ffn_bout, np.float32)

    maskC, blockmask, ident, selS, zsel, rsel = _consts()

    bdw2 = np.zeros((NCH, 128, M), dtype=np.float32)
    flat = np.arange(128)
    for c in range(NCH):
        idx = 128 * c + flat
        bdw2[c, flat, idx // INNER] = W2[idx // INNER, idx % INNER]

    shared = dict(
        wqT=np.ascontiguousarray(Wq.T), wkT=np.ascontiguousarray(Wk.T),
        wvT=np.ascontiguousarray(Wv.T), wcT=np.ascontiguousarray(Wc.T),
        wc_nat=np.ascontiguousarray(Wc),
        win=np.ascontiguousarray(Win), wout=np.ascontiguousarray(Wout),
        bdw2=bdw2, selS=selS,
        rep_bk=np.ascontiguousarray(np.tile(bk[None, :], (128, 1))),
        rep_bv=np.ascontiguousarray(np.tile(bv[None, :], (128, 1))),
        bq_col=np.ascontiguousarray(bq.reshape(NCH, 128).T),
        bc_col=np.ascontiguousarray(bc.reshape(NCH, 128).T),
        brows=np.ascontiguousarray(np.concatenate([bc, bout])[None, :]),
        bin_col=np.ascontiguousarray(bin_.reshape(M, 1)),
        b2_col=np.ascontiguousarray(b2.reshape(M, 1)),
        w1_col=np.ascontiguousarray(W1.reshape(-1).reshape(NCH, 128).T),
        b1_col=np.ascontiguousarray(b1.reshape(-1).reshape(NCH, 128).T),
        maskC=maskC, blockmask=blockmask, ident=ident, zsel=zsel,
)

    in_maps = []
    for core in range(NCORES):
        b_i, half = core // 2, core % 2
        xs = x[b_i, half * TOK:(half + 1) * TOK, :]
        m = dict(shared)
        m["x_tm"] = np.ascontiguousarray(xs)
        m["x_fm"] = np.ascontiguousarray(xs.T)
        in_maps.append(m)

    if "nc" not in _NC_CACHE:
        _NC_CACHE["nc"] = build_nc()
    nc = _NC_CACHE["nc"]
    res = run_bass_kernel_spmd(nc, in_maps, core_ids=list(range(NCORES)))
    out = np.empty((B, T, D), dtype=np.float32)
    for core in range(NCORES):
        b_i, half = core // 2, core % 2
        out[b_i, half * TOK:(half + 1) * TOK, :] = res.results[core]["out_tm"]
    return out


# version 21
# speedup vs baseline: 1.0961x; 1.0092x over previous
import numpy as np
import concourse.bass as bass
import concourse.mybir as mybir
from concourse import dve_ops as _dve_ops
from concourse.bass_utils import run_bass_kernel_spmd
from contextlib import ExitStack

F32 = mybir.dt.float32
BF16 = mybir.dt.bfloat16
AF = mybir.ActivationFunctionType
OP = mybir.AluOpType

B, T, D = 4, 4096, 1024
NCORES = 8
TOK = 2048
NT = 16
NCH = 8
M, INNER = 64, 16
EPS_LN = 1e-5
EPS_Z = 1e-6


DMA_SEMS = {"LD", "GD", "ST", "ST2", "G0", "G1", "SD", "TD", "VD"}


class Op:
    __slots__ = ("eng", "fn", "deps", "sem", "inc", "done", "tag", "new_group",
                 "self_wait")
    def __init__(self, eng, fn, deps, sem, inc, tag="", new_group=False):
        self.eng, self.fn, self.deps, self.sem, self.inc = eng, fn, deps, sem, inc
        self.done = None
        self.tag = tag
        self.new_group = new_group
        self.self_wait = None


class Sched:
    def __init__(self):
        self.ops = []
        self.counts = {}
        self.group_end = {}
        self.last_done = {}

    def add(self, eng, fn, deps=(), sem=None, inc=0, tag="", new_group=False):
        deps = [d for d in deps if d is not None]
        if eng in ("vector", "scalar", "tensor") and eng in self.last_done:
            deps.append(self.last_done[eng])
        op = Op(eng, fn, deps, sem, inc, tag, new_group)
        self.ops.append(op)
        if sem is not None and inc:
            v = self.counts.get(sem, 0) + inc
            self.counts[sem] = v
            op.done = (sem, v)
            if eng in ("vector", "scalar", "tensor") and sem not in DMA_SEMS:
                self.last_done[eng] = op.done
        return op.done

    def finalize(self):
        for sem in DMA_SEMS:
            sem_ops = [op for op in self.ops if op.sem == sem]
            if not sem_ops:
                continue
            groups = []
            for i, op in enumerate(sem_ops):
                if i == 0 or op.new_group:
                    groups.append([])
                groups[-1].append(op)
            running = 0
            for g in groups:
                start = running
                for op in g:
                    running += op.inc
                for op in g:
                    self.group_end[(sem, op.done[1])] = running
                if start > 0:
                    g[0].self_wait = start

    def emit(self, eng_name, proxy, sems):
        if eng_name == "tensor":
            real = proxy

            class _Shim:
                def matmul(self, *a, **k):
                    TE_TRACE.append(self._tag)
                    return real.matmul(*a, **k)

                def transpose(self, *a, **k):
                    TE_TRACE.append(self._tag)
                    return real.transpose(*a, **k)

                def __getattr__(self, n):
                    return getattr(real, n)

            shim = _Shim()
        waited = {}
        for op in self.ops:
            if op.eng != eng_name:
                continue
            if eng_name == "tensor":
                shim._tag = op.tag
                proxy = shim
            if op.self_wait is not None:
                if waited.get(op.sem, -1) < op.self_wait:
                    proxy.wait_ge(sems[op.sem], op.self_wait)
                    waited[op.sem] = op.self_wait
            for (s, v) in op.deps:
                if s in DMA_SEMS:
                    v = self.group_end[(s, v)]
                if waited.get(s, -1) >= v:
                    continue
                proxy.wait_ge(sems[s], v)
                waited[s] = v
            ins = op.fn(proxy)
            if op.done is not None:
                assert ins is not None, op.tag
                ins.then_inc(sems[op.sem], op.inc)


_DEBUG_TAPS = False
TE_TRACE = []


def build_nc():
    nc = bass.Bass("TRN2", target_bir_lowering=False, debug=False,
                   dynamic_dma_scratch_size=16384)

    def din(name, shape):
        return nc.dram_tensor(name, list(shape), F32, kind="ExternalInput").ap()

    x_tm = din("x_tm", (TOK, D))
    x_fm = din("x_fm", (D, TOK))
    wqT = din("wqT", (D, D)); wkT = din("wkT", (D, D))
    wvT = din("wvT", (D, D)); wcT = din("wcT", (D, D))
    wc_nat = din("wc_nat", (D, D))
    win = din("win", (D, M)); wout = din("wout", (M, D))
    bdw2 = din("bdw2", (NCH, 128, M))
    selS = din("selS", (NCH, M, 128))
    rep_bk = din("rep_bk", (128, D)); rep_bv = din("rep_bv", (128, D))
    bq_col = din("bq_col", (128, NCH)); bc_col = din("bc_col", (128, NCH))
    brows = din("brows", (1, 2 * D))
    bin_col = din("bin_col", (M, 1)); b2_col = din("b2_col", (M, 1))
    w1_col = din("w1_col", (128, NCH)); b1_col = din("b1_col", (128, NCH))
    maskC = din("maskC", (128, 128))
    zsel = din("zsel", (NCH, 128, 128))
    blockmask = din("blockmask", (128, 128))
    ident = din("ident", (128, 128))

    out_tm = nc.dram_tensor("out_tm", [TOK, D], F32, kind="ExternalOutput").ap()

    cc_in = nc.dram_tensor("cc_in", [128, 1032], F32, kind="Internal").ap()
    cc_out = nc.dram_tensor("cc_out", [128, 1032], F32, kind="Internal").ap()

    ctx = ExitStack()
    e = ctx.enter_context

    A1 = e(nc.sbuf_tensor("A1", [128, 16384], BF16))
    A2 = e(nc.sbuf_tensor("A2", [128, 16384], BF16))
    A3 = e(nc.sbuf_tensor("A3", [128, 16384], BF16))
    A4 = e(nc.sbuf_tensor("A4", [128, 16384], BF16))
    W0 = e(nc.sbuf_tensor("W0", [128, 8192], BF16))
    W1 = e(nc.sbuf_tensor("W1", [128, 8192], BF16))
    SCR = e(nc.sbuf_tensor("SCR", [128, 2048], F32))
    RB0 = e(nc.sbuf_tensor("RB0", [128, 1024], F32))
    RB1 = e(nc.sbuf_tensor("RB1", [128, 1024], F32))
    s_x = e(nc.sbuf_tensor("s_x", [64, 2048], F32))
    kv_b = e(nc.sbuf_tensor("kv_b", [128, 1024], BF16))
    lden = e(nc.sbuf_tensor("lden", [128, 128], BF16))
    zsel_b = e(nc.sbuf_tensor("zsel_b", [128, 1024], BF16))
    bcb = e(nc.sbuf_tensor("bcb", [128, 8], BF16))
    wout_b = e(nc.sbuf_tensor("wout_b", [64, 1024], BF16))
    win_b = e(nc.sbuf_tensor("win_b", [128, 512], BF16))
    bdw2_b = e(nc.sbuf_tensor("bdw2_b", [128, 512], BF16))
    wcw_b = e(nc.sbuf_tensor("wcw_b", [128, 512], BF16))
    selS_b = e(nc.sbuf_tensor("selS_b", [64, 1024], BF16))
    maskC_s = e(nc.sbuf_tensor("maskC_s", [128, 128], BF16))
    bmask_s = e(nc.sbuf_tensor("bmask_s", [128, 128], BF16))
    ident_s = e(nc.sbuf_tensor("ident_s", [128, 128], F32))
    brows_b = e(nc.sbuf_tensor("brows_b", [1, 2048], BF16))
    ones_r = e(nc.sbuf_tensor("ones_r", [1, 128], BF16))
    ones_c = e(nc.sbuf_tensor("ones_c", [128, 1], BF16))
    cols = e(nc.sbuf_tensor("cols", [128, 64], F32))
    colsB = e(nc.sbuf_tensor("colsB", [128, 96], F32))
    LW = e(nc.sbuf_tensor("LW", [128, 96], F32))
    LW2 = e(nc.sbuf_tensor("LW2", [128, 96], F32))
    mcols = e(nc.sbuf_tensor("mcols", [64, 8], F32))
    zmz = e(nc.sbuf_tensor("zmz", [128, 32], F32))

    P = [e(nc.psum_tensor(f"P{i}", [128, 512], F32)) for i in range(8)]

    sems = {}
    for s in ["LD", "GD", "TE", "VE", "SE", "CC", "ST", "ST2", "G0", "G1",
              "SD", "TD", "VD"]:
        sems[s] = e(nc.semaphore(s))

    S = Sched()

    def f32_tile(arena, slot):
        return arena[:, 2048 * slot:2048 * (slot + 1)].bitcast(F32)

    def r1_t(t):  return f32_tile(A3, t) if t < 8 else f32_tile(A4, t - 8)
    def x2_t(t):  return f32_tile(A1, t) if t < 8 else f32_tile(A3, t - 8)
    def r2_t(t):  return f32_tile(A4, t) if t < 8 else f32_tile(A1, t - 8)

    den_v = A2[0:16, 0:4096].bitcast(F32)
    zrec_v = A2[0:16, 4096:8192].bitcast(F32)
    exch_v = A2[:, 8192:10240].bitcast(F32)
    exks_v = A2[:, 10240:10256].bitcast(F32)
    z1rep = W0[0:64, 0:4096].bitcast(F32)
    mz1rep = W0[0:64, 4096:8192].bitcast(F32)
    xin = [RB0[:, :], RB1[:, :]]
    ktb = SCR[:, 0:1024]
    e2 = SCR[:, 1024:2048]
    sqscr = SCR[:, 0:1024].bitcast(BF16)
    t1 = SCR[0:64, 1024:1536]
    t2 = SCR[0:64, 1536:2048]
    h_b = [SCR[:, 1024 + 256 * i:1024 + 256 * (i + 1)].bitcast(BF16) for i in range(4)]
    outst = [A2[:, 4096:6144].bitcast(F32), A2[:, 6144:8192].bitcast(F32),
             A2[:, 12288:14336].bitcast(F32), A2[:, 14336:16384].bitcast(F32)]
    s_b = A2[0:64, 0:2048]
    g_b = A2[0:64, 2048:4096]
    zwh_b = A2[0:32, 4096:4224]
    zwlo_b = A2[0:32, 4224:4352]
    zwf = A2[0:32, 4352:4608].bitcast(F32)
    A1F = A1[:, :].bitcast(F32)
    zrh_b = A1[0:16, 8192:10240]
    zrl_b = A1[0:16, 10240:12288]
    zhi_f = A1F[0:16, 2048:4096]
    zrep = [A1F[:, 6144 + 512 * s:6144 + 512 * (s + 1)] for s in range(4)]

    C_bq = cols[:, 0:8]; C_negbq = cols[:, 8:16]; C_bq1 = cols[:, 16:24]
    C_bc = cols[:, 24:32]; C_w1 = cols[:, 32:40]; C_b1 = cols[:, 40:48]
    C_sum2b = cols[:, 48:64]
    C_sum1a = colsB[:, 0:16]; C_sum1b = colsB[:, 16:32]; C_ssq1 = colsB[:, 32:48]
    C_m1 = colsB[:, 48:64]; C_sum2a = colsB[:, 64:80]; C_ssq2 = colsB[:, 80:96]
    L_msq = LW[:, 0:16]; L_var = LW[:, 16:32]; L_rinv = LW[:, 32:48]
    L_rstd1 = LW[:, 48:64]; L_mrs1 = LW[:, 64:80]; L_sum = LW[:, 80:96]
    L_rstd2 = LW2[:, 0:16]; L_mrs2 = LW2[:, 16:32]; L_m2 = LW2[:, 32:48]
    L_w2 = LW2[:, 48:64]; L_mrs2n = LW2[:, 64:80]
    M_bin = mcols[:, 0:1]; M_b2 = mcols[:, 1:2]; M_bcw = mcols[:, 2:3]; M_cs = mcols[:, 3:4]

    ld = {}
    def q_load(name, dst, src, eng="sync", sem="LD", new_group=False, deps=()):
        ld[name] = S.add(eng, lambda p, d=dst, s=src: p.dma_start(out=d, in_=s),
                         deps=deps, sem=sem, inc=16, tag=f"ld.{name}",
                         new_group=new_group)

    stA = A4[:, 8192:16384].bitcast(F32)
    stB = A2[:, 8192:16384].bitcast(F32)
    xstage = A3[:, :].bitcast(F32)
    scr_f = SCR[:, :]
    xb_done = []
    wk_done = []
    wv_done = []

    q_load("ident", ident_s[:, :], ident, eng="gpsimd", sem="GD")
    q_load("bq", C_bq, bq_col, eng="gpsimd", sem="GD")
    q_load("bc", C_bc, bc_col, eng="gpsimd", sem="GD")
    q_load("w1", C_w1, w1_col, eng="gpsimd", sem="GD")
    q_load("b1", C_b1, b1_col, eng="gpsimd", sem="GD")
    q_load("bin", M_bin, bin_col, eng="gpsimd", sem="GD")
    q_load("b2", M_b2, b2_col, eng="gpsimd", sem="GD")

    xlds = [None] * NCH
    xb_done = [None] * NCH

    def emit_xld(c, eng, sem, deps, new_group):
        j = c % 4
        xlds[c] = S.add(eng, lambda p, c=c, j=j: p.dma_start(
            out=xstage[:, 2048 * j:2048 * (j + 1)],
            in_=x_fm[128 * c:128 * (c + 1), :]),
            deps=deps, sem=sem, inc=16, tag=f"xbld{c}", new_group=new_group)

    for c in range(4):
        emit_xld(c, "sync", "LD", [], c % 2 == 0)
    for c in range(4):
        xb_done[c] = S.add("vector", lambda p, c=c: p.tensor_copy(
            A1[:, 2048 * c:2048 * (c + 1)],
            xstage[:, 2048 * c:2048 * (c + 1)]),
            deps=[xlds[c]], sem="VE", inc=1, tag=f"xbcast{c}")
    emit_xld(4, "sync", "LD", [xb_done[0]], True)
    emit_xld(5, "sync", "LD", [xb_done[1]], False)

    wk_slots = [stA[:, 1024 * i:1024 * (i + 1)] for i in range(4)] + \
               [scr_f[:, 0:1024], scr_f[:, 1024:2048], RB0[:, :], RB1[:, :]]
    wk_lds = []
    for c in range(NCH):
        wk_lds.append(S.add("scalar", lambda p, c=c: p.dma_start(
            out=wk_slots[c], in_=wkT[128 * c:128 * (c + 1), :]),
            sem="SD", inc=16, tag=f"wkld{c}", new_group=(c == 0)))
    for c in range(NCH):
        wk_done.append(S.add("scalar", lambda p, c=c: p.copy(
            W0[:, 1024 * c:1024 * (c + 1)], wk_slots[c]),
            deps=[wk_lds[c]], sem="SE", inc=1, tag=f"wkcast{c}"))

    emit_xld(6, "scalar", "SD", [xb_done[2]], True)
    emit_xld(7, "scalar", "SD", [xb_done[3]], False)

    wv_slots = [stB[:, 1024 * i:1024 * (i + 1)] for i in range(4)] + \
               [stA[:, 1024 * i:1024 * (i + 1)] for i in range(4)]
    wv_lds = []
    for c in range(NCH):
        deps = [wk_done[c - 4]] if c >= 4 else []
        wv_lds.append(S.add("scalar", lambda p, c=c: p.dma_start(
            out=wv_slots[c], in_=wvT[128 * c:128 * (c + 1), :]),
            deps=deps, sem="SD", inc=16, tag=f"wvld{c}", new_group=(c == 0)))
    def emit_wv_casts():
        for c in range(NCH):
            wv_done.append(S.add("scalar", lambda p, c=c: p.copy(
                W1[:, 1024 * c:1024 * (c + 1)], wv_slots[c]),
                deps=[wv_lds[c]], sem="SE", inc=1, tag=f"wvcast{c}"))

    for c in range(4, NCH):
        xb_done[c] = S.add("vector", lambda p, c=c: p.tensor_copy(
            A1[:, 2048 * c:2048 * (c + 1)],
            xstage[:, 2048 * (c % 4):2048 * (c % 4 + 1)]),
            deps=[xlds[c]], sem="VE", inc=1, tag=f"xbcast{c}")

    q_load("repbk", RB0[:, :], rep_bk, new_group=True,
           deps=[wk_done[6], wk_done[7]])
    q_load("repbv", RB1[:, :], rep_bv)

    gd = {}
    def cast_load(name, dst, src, deps=(), new_group=False):
        gd[name] = S.add("gpsimd", lambda p, d=dst, s=src: p.dma_start(out=d, in_=s),
                         deps=deps, sem="GD", inc=16, tag=f"cl.{name}",
                         new_group=new_group)

    for c in range(NCH):
        cast_load(f"wcn{c}", A4[:, 1024 * c:1024 * (c + 1)],
                  wc_nat[128 * c:128 * (c + 1), :], new_group=(c == 0),
                  deps=[wv_lds[-1]] if c == 0 else ())
    cast_load("win", win_b[:, :].rearrange("p (c m) -> p c m", c=NCH),
              win.rearrange("(c p) m -> p c m", p=128))
    cast_load("wout", wout_b[:, :], wout, new_group=True)
    cast_load("bdw2", bdw2_b[:, :].rearrange("p (c m) -> p c m", c=NCH),
              bdw2.rearrange("c p m -> p c m"))
    cast_load("selS", selS_b[:, :].rearrange("m (c p) -> m c p", c=NCH),
              selS.rearrange("c m p -> m c p"))
    cast_load("brows", brows_b[:, :], brows)
    cast_load("maskC", maskC_s[:, :], maskC)
    cast_load("bmask", bmask_s[:, :], blockmask)
    cast_load("zsel", zsel_b[:, :].rearrange("h (c p) -> h c p", c=NCH),
              zsel.rearrange("c h p -> h c p"))
    ld["maskC"] = gd["maskC"]
    ld["bmask"] = gd["bmask"]

    m_ones_r = S.add("vector", lambda p: p.memset(ones_r[:, :], 1.0), sem="VE", inc=1)
    m_ones_c = S.add("vector", lambda p: p.memset(ones_c[:, :], 1.0), sem="VE", inc=1)
    d_negbq = S.add("vector", lambda p: p.tensor_scalar_mul(C_negbq, C_bq, -1.0),
                    deps=[ld["bq"]], sem="VE", inc=1)
    d_bq1 = S.add("vector", lambda p: p.tensor_scalar_add(C_bq1, C_bq, 1.0),
                  deps=[ld["bq"]], sem="VE", inc=1)
    d_bcc = S.add("vector", lambda p: p.tensor_copy(bcb[:, :], C_bc),
                  deps=[ld["bc"]], sem="VE", inc=1)

    def mm_group(p, outs, pairs):
        ins = None
        n = len(pairs)
        for i, (l, r) in enumerate(pairs):
            ins = p.matmul(outs, l, r, start=(i == 0), stop=(i == n - 1))
        return ins

    wk_all = wk_done
    wv_all = wv_done
    ktil = [None] * NT
    vtil = [None] * NT
    k_mm = [None] * NT
    v_mm = [None] * NT
    prev_kev = {}
    prev_vev = {}
    KAHEAD = 4
    def emit_kproj(t):
        pk = (P[0], P[1]) if t % 2 == 0 else (P[4], P[5])
        deps_k = xb_done + wk_all
        if t % 2 in prev_kev:
            deps_k = deps_k + [prev_kev[t % 2]]
        def kproj(p, t=t, pk=pk):
            ins = None
            for jh in range(2):
                ins = mm_group(p, pk[jh][:, :],
                               [(A1[:, 2048 * kc + 128 * t:2048 * kc + 128 * (t + 1)],
                                 W0[:, 1024 * kc + 512 * jh:1024 * kc + 512 * (jh + 1)])
                                for kc in range(NCH)])
            return ins
        k_mm[t] = S.add("tensor", kproj, deps=deps_k, sem="TE", inc=1, tag=f"kproj{t}")
        def k_tb(p, t=t, pk=pk):
            p.tensor_tensor(ktb[:, 0:512], pk[0][:, :], RB0[:, 0:512], OP.add)
            return p.tensor_tensor(ktb[:, 512:1024], pk[1][:, :], RB0[:, 512:1024], OP.add)
        tb_deps = [k_mm[t], ld["repbk"]]
        if t == 0:
            tb_deps += [wk_done[4], wk_done[5]]
        d_tb = S.add("vector", k_tb, deps=tb_deps, sem="VE", inc=1,
                     tag=f"ktb{t}")
        d_r = S.add("scalar", lambda p: p.activation(e2[:, :], ktb[:, :], AF.Relu,
                                                     scale=-1.0),
                    deps=[d_tb, ktil[t - 1] if t > 0 else None], sem="SE", inc=1)
        d_e = S.add("scalar", lambda p: p.activation(e2[:, :], e2[:, :], AF.Exp,
                                                     scale=-1.0),
                    deps=[d_r], sem="SE", inc=1)
        kdeps = [d_e] + ([wv_done[(t - 8) // 2]] if t >= 8 else [])
        ktil[t] = S.add("vector", lambda p, t=t: p.scalar_tensor_tensor(
            A2[:, 1024 * t:1024 * (t + 1)], ktb[:, :], 1.0, e2[:, :], OP.add, OP.max),
            deps=kdeps, sem="VE", inc=1, tag=f"ktil{t}")
        prev_kev[t % 2] = ktil[t]

    def emit_vproj(t):
        pv = (P[2], P[3]) if t % 2 == 0 else (P[6], P[7])
        deps_v = xb_done + wv_all
        if t % 2 in prev_vev:
            deps_v = deps_v + [prev_vev[t % 2]]
        def vproj(p, t=t, pv=pv):
            ins = None
            for jh in range(2):
                ins = mm_group(p, pv[jh][:, :],
                               [(A1[:, 2048 * kc + 128 * t:2048 * kc + 128 * (t + 1)],
                                 W1[:, 1024 * kc + 512 * jh:1024 * kc + 512 * (jh + 1)])
                                for kc in range(NCH)])
            return ins
        v_mm[t] = S.add("tensor", vproj, deps=deps_v, sem="TE", inc=1, tag=f"vproj{t}")
        def v_ev(p, t=t, pv=pv):
            p.tensor_tensor(A3[:, 1024 * t:1024 * t + 512], pv[0][:, :],
                            RB1[:, 0:512], OP.add)
            return p.tensor_tensor(A3[:, 1024 * t + 512:1024 * (t + 1)], pv[1][:, :],
                                   RB1[:, 512:1024], OP.add)
        vtil[t] = S.add("vector", v_ev, deps=[v_mm[t], ld["repbv"]], sem="VE", inc=1,
                        tag=f"vtil{t}")
        prev_vev[t % 2] = vtil[t]

    for t in range(6):
        emit_kproj(t)
    emit_wv_casts()
    emit_vproj(0); emit_vproj(1)
    kq, vq = 6, 2
    while kq < NT or vq < NT:
        if kq < NT:
            emit_kproj(kq); kq += 1
        if vq < NT:
            emit_vproj(vq); vq += 1

    wq_stage = A4[:, 8192:16384].bitcast(F32)
    wq_all = []
    prev_cast = None
    for b in range(2):
        lds = []
        for i in range(4):
            c = 4 * b + i
            lds.append(S.add("sync", lambda p, c=c, i=i: p.dma_start(
                out=wq_stage[:, 1024 * i:1024 * (i + 1)],
                in_=wqT[128 * c:128 * (c + 1), :]),
                deps=[prev_cast] if prev_cast else [wv_done[-1]],
                sem="LD", inc=16, tag=f"wqld{c}", new_group=(i == 0)))
        for i in range(4):
            c = 4 * b + i
            prev_cast = S.add("scalar", lambda p, c=c, i=i: p.copy(
                W0[:, 1024 * c:1024 * (c + 1)], wq_stage[:, 1024 * i:1024 * (i + 1)]),
                deps=[lds[i], k_mm[NT - 1]], sem="SE", inc=1, tag=f"wqcast{c}")
            wq_all.append(prev_cast)
    wc_all = []
    for c in range(NCH):
        wc_all.append(S.add("gpsimd", lambda p, c=c: p.dma_start(
            out=W1[:, 1024 * c:1024 * (c + 1)], in_=wcT[128 * c:128 * (c + 1), :]),
            deps=[v_mm[NT - 1]], sem="GD", inc=16, tag=f"wc{c}",
            new_group=(c == 0)))

    kv_deps_base = ktil + vtil
    kv_last = None
    for pr in range(8):
        po = P[4 + pr // 4]
        deps = list(kv_deps_base) if pr == 0 else [kv_last]
        kv_last = S.add("tensor", lambda p, pr=pr, po=po: mm_group(
            p, po[:, 128 * (pr % 4):128 * (pr % 4 + 1)],
            [(A2[:, 1024 * t + 128 * pr:1024 * t + 128 * (pr + 1)],
              A3[:, 1024 * t + 128 * pr:1024 * t + 128 * (pr + 1)])
             for t in range(NT)]),
            deps=deps, sem="TE", inc=1, tag=f"kv{pr}")
    ks_row = [None, None]
    for h in range(2):
        po = (P[6], P[7])[h]
        ks_row[h] = S.add("tensor", lambda p, h=h, po=po: mm_group(
            p, po[0:1, :],
            [(ones_c[:, :], A2[:, 1024 * t + 512 * h:1024 * t + 512 * (h + 1)])
             for t in range(NT)]),
            deps=[m_ones_c, kv_last], sem="TE", inc=1, tag=f"ksr{h}")
    M_kv = ks_row[1]
    ksrow_s = SCR[0:1, 0:1024]
    ev_ksr = []
    for h in range(2):
        po = (P[6], P[7])[h]
        ev_ksr.append(S.add("vector", lambda p, h=h, po=po: p.tensor_copy(
            ksrow_s[:, 512 * h:512 * (h + 1)], po[0:1, :]),
            deps=[ks_row[h]], sem="VE", inc=1, tag=f"ksrev{h}"))
    tr_last = None
    for c in range(NCH):
        tr_last = S.add("tensor", lambda p, c=c: p.transpose(
            P[6][:, c:c + 1], ksrow_s[:, 128 * c:128 * (c + 1)],
            ident_s[0:1, 0:1]),
            deps=[ev_ksr[0], ev_ksr[1], ld["ident"]], sem="TE", inc=1,
            tag=f"kstr{c}")

    ev_kv1 = S.add("scalar", lambda p: p.copy(exch_v[:, 0:512], P[4][:, :]),
                   deps=[M_kv], sem="SE", inc=1)
    ev_kv2 = S.add("scalar", lambda p: p.copy(exch_v[:, 512:1024], P[5][:, :]),
                   deps=[M_kv], sem="SE", inc=1)
    ev_ks = S.add("scalar", lambda p: p.copy(exks_v[:, :], P[6][:, 0:8]),
                  deps=[tr_last], sem="SE", inc=1)

    d_xo1 = S.add("gpsimd", lambda p: p.dma_start(out=cc_in[:, 0:1024],
                                                  in_=exch_v[:, :]),
                  deps=[ev_kv1, ev_kv2], sem="GD", inc=16, tag="xout1",
                  new_group=True)
    d_xo2 = S.add("gpsimd", lambda p: p.dma_start(out=cc_in[:, 1024:1032],
                                                  in_=exks_v[:, :]),
                  deps=[ev_ks], sem="GD", inc=16, tag="xout2")
    d_cc = S.add("gpsimd", lambda p: p.collective_compute(
        "AllReduce", OP.add, replica_groups=[[0, 1], [2, 3], [4, 5], [6, 7]],
        ins=[cc_in], outs=[cc_out]),
        deps=[d_xo1, d_xo2], sem="CC", inc=1, tag="allreduce")
    d_xi1 = S.add("gpsimd", lambda p: p.dma_start(out=exch_v[:, :],
                                                  in_=cc_out[:, 0:1024]),
                  deps=[d_cc], sem="GD", inc=16, tag="xin1",
                  new_group=True)
    d_xi2 = S.add("gpsimd", lambda p: p.dma_start(out=exks_v[:, :],
                                                  in_=cc_out[:, 1024:1032]),
                  deps=[d_cc], sem="GD", inc=16, tag="xin2")


    wcn_deps = [gd[f"wcn{c}"] for c in range(NCH)] + [gd["win"]]
    wcw_ev = []
    for cc2 in range(NCH):
        po = P[cc2 % 2]
        dn = S.add("tensor", lambda p, cc2=cc2, po=po: mm_group(
            p, po[:, 0:64],
            [(A4[:, 1024 * jc + 128 * cc2:1024 * jc + 128 * (cc2 + 1)],
              win_b[:, 64 * jc:64 * (jc + 1)]) for jc in range(NCH)]),
            deps=(wcn_deps + [ktil[NT - 2]] if cc2 < 2 else [wcw_ev[cc2 - 2]]),
            sem="TE", inc=1, tag=f"wcw{cc2}")
        ev = S.add("vector", lambda p, cc2=cc2, po=po: p.tensor_copy(
            wcw_b[:, 64 * cc2:64 * (cc2 + 1)], po[:, 0:64]),
            deps=[dn], sem="VE", inc=1, tag=f"wcwev{cc2}")
        wcw_ev.append(ev)
    M_wcw = wcw_ev[-1]

    d_bcw = S.add("tensor", lambda p: mm_group(
        p, P[7][0:64, 0:1],
        [(win_b[:, 64 * jc:64 * (jc + 1)], bcb[:, jc:jc + 1]) for jc in range(NCH)]),
        deps=[d_bcc, gd["win"], vtil[NT - 1], ev_ksr[1]], sem="TE", inc=1,
        tag="bcwin")
    d_cs = S.add("tensor", lambda p: mm_group(
        p, P[7][0:64, 1:2],
        [(win_b[:, 64 * jc:64 * (jc + 1)], ones_c[:, :]) for jc in range(NCH)]),
        deps=[m_ones_c], sem="TE", inc=1, tag="cs")
    ev_bcw = S.add("vector", lambda p: p.tensor_copy(M_bcw, P[7][0:64, 0:1]),
                   deps=[d_bcw], sem="VE", inc=1)
    ev_cs = S.add("vector", lambda p: p.tensor_copy(M_cs, P[7][0:64, 1:2]),
                  deps=[d_cs], sem="VE", inc=1)


    q_mm = [[None] * 4 for _ in range(NCH)]
    qtil = [[None] * 4 for _ in range(NCH)]
    prev_q = {}
    for c in range(NCH):
        for th in range(4):
            slot = (c * 4 + th) % 4
            po = P[slot]
            deps = wq_all + [ktil[NT - 2], vtil[NT - 2]]
            if slot < 2:
                deps = deps + [M_wcw]
            if slot in prev_q:
                deps = [wq_all[-1], prev_q[slot]]
            dn = S.add("tensor", lambda p, c=c, th=th, po=po: mm_group(
                p, po[:, :],
                [(W0[:, 1024 * kc + 128 * c:1024 * kc + 128 * (c + 1)],
                  A1[:, 2048 * kc + 512 * th:2048 * kc + 512 * (th + 1)])
                 for kc in range(NCH)]),
                deps=deps, sem="TE", inc=1, tag=f"q{c}.{th}")
            q_mm[c][th] = dn
            prev_qtil = qtil[c][th - 1] if th > 0 else (qtil[c - 1][3] if c > 0
                                                        else ktil[NT - 1])
            d_r = S.add("scalar", lambda p, c=c, po=po: p.activation(
                e2[:, 0:512], po[:, :], AF.Relu, bias=C_negbq[:, c:c + 1], scale=-1.0),
                deps=[dn, d_negbq, prev_qtil], sem="SE", inc=1)
            d_e = S.add("scalar", lambda p: p.activation(
                e2[:, 0:512], e2[:, 0:512], AF.Exp, scale=-1.0),
                deps=[d_r], sem="SE", inc=1)
            depq = [d_e, d_bq1]
            if c == 0 and th == 0:
                depq += [M_wcw]
            d_q = S.add("vector", lambda p, c=c, th=th, po=po: p.scalar_tensor_tensor(
                A4[:, 2048 * c + 512 * th:2048 * c + 512 * (th + 1)],
                po[:, :], C_bq1[:, c:c + 1], e2[:, 0:512], OP.add, OP.max),
                deps=depq, sem="VE", inc=1, tag=f"qtil{c}.{th}")
            qtil[c][th] = d_q
            prev_q[slot] = d_q

    if _DEBUG_TAPS == 4:
        st_total = 0
        for name, src, dt in [("d_qt", A4[:, :], BF16), ("d_kt", A2[:, :], BF16),
                              ("d_vt", A3[:, :], BF16), ("d_xb", A1[:, :], BF16)]:
            dst = nc.dram_tensor(name, list(src.shape), dt,
                                 kind="ExternalOutput").ap()
            S.add("sync", lambda p, dd=dst, ss=src: p.dma_start(out=dd, in_=ss),
                  deps=[("VE", S.counts["VE"]), ("TE", S.counts["TE"]),
                        ("SE", S.counts["SE"]), ("GD", S.counts["GD"])],
                  sem="ST", inc=16, tag=f"tap4.{name}")
            st_total += 16
        S.add("sync", lambda p: p.dma_start(out=out_tm[0:128, :], in_=RB0[:, :]),
              deps=[ld["repbk"]], sem="ST", inc=16)
        st_total += 16
        S.add("sync", lambda p: (p.wait_ge(sems["ST"], st_total),
                             p.wait_ge(sems["ST2"], st2_total)) and None,
          tag="final")
        S.finalize()
        with nc.Block() as block:
            block.sync(lambda p: S.emit("sync", p, sems))
            block.gpsimd(lambda p: S.emit("gpsimd", p, sems))
            block.tensor(lambda p: S.emit("tensor", p, sems))
            block.vector(lambda p: S.emit("vector", p, sems))
            block.scalar(lambda p: S.emit("scalar", p, sems))
        ctx.close()
        return nc

    S.add("scalar", lambda p: p.activation(zmz[0:1, 0:1], ones_r[0:1, 0:1], AF.Sqrt),
          deps=[m_ones_r], sem="SE", inc=1, tag="sqrt_preload")


    d_kvb = S.add("vector", lambda p: p.tensor_tensor(
        kv_b[:, :].rearrange("p (a b) -> p a b", a=8),
        exch_v[:, :].rearrange("p (a b) -> p a b", a=8),
        bmask_s[:, :].unsqueeze(1).broadcast_to([128, 8, 128]), OP.mult),
        deps=[d_xi1, ld["bmask"]], sem="VE", inc=1, tag="kvb")
    d_lden = []
    for c in range(NCH):
        d_lden.append(S.add("vector", lambda p, c=c: p.tensor_scalar_mul(
            lden[:, 16 * c:16 * (c + 1)], maskC_s[:, 16 * c:16 * (c + 1)],
            exks_v[:, c:c + 1]),
            deps=[d_xi2, ld["maskC"]], sem="VE", inc=1, tag=f"lden{c}"))

    den = []
    for th in range(4):
        po = P[4 + th % 2]
        deps = [qtil[c][th] for c in range(NCH)] + d_lden
        deps += [ev_kv1, ev_kv2] if th < 2 else [den[th - 2][1]]
        dn = S.add("tensor", lambda p, th=th, po=po: mm_group(
            p, po[0:16, :],
            [(lden[:, 16 * c:16 * (c + 1)],
              A4[:, 2048 * c + 512 * th:2048 * c + 512 * (th + 1)])
             for c in range(NCH)]),
            deps=deps, sem="TE", inc=1, tag=f"den{th}")
        ev = S.add("vector", lambda p, th=th, po=po: p.tensor_scalar_add(
            den_v[:, 512 * th:512 * (th + 1)], po[0:16, :], EPS_Z),
            deps=[dn, M_kv], sem="VE", inc=1, tag=f"denev{th}")
        den.append((dn, ev))
    sx_ev = []
    sx_mm = []
    for ts in range(4):
        po = P[2] if ts % 2 == 0 else P[3]
        half = 64 * (ts // 2)
        reg = po[half:half + 64, :]
        deps = xb_done + [gd["win"], qtil[NCH - 1][2 + ts % 2]]
        dn = S.add("tensor", lambda p, ts=ts, reg=reg: mm_group(
            p, reg,
            [(win_b[:, 64 * kc:64 * (kc + 1)],
              A1[:, 2048 * kc + 512 * ts:2048 * kc + 512 * (ts + 1)])
             for kc in range(NCH)]),
            deps=deps, sem="TE", inc=1, tag=f"sx{ts}")
        sx_mm.append(dn)
        ev = S.add("vector", lambda p, ts=ts, reg=reg: p.tensor_copy(
            s_x[:, 512 * ts:512 * (ts + 1)], reg),
            deps=[dn], sem="VE", inc=1, tag=f"sxev{ts}")
        sx_ev.append(ev)

    d_zlo4 = [None] * 4
    for th in range(4):
        dq = slice(512 * th, 512 * (th + 1))
        d_ra = S.add("vector", lambda p, dq=dq: p.reciprocal(
            zrec_v[:, dq], den_v[:, dq]),
            deps=[den[th][1]], sem="VE", inc=1, tag=f"recip{th}")
        d_zh = S.add("vector", lambda p, dq=dq: p.tensor_copy(zrh_b[:, dq],
                                                              zrec_v[:, dq]),
                     deps=[d_ra, sx_mm[3]], sem="VE", inc=1)
        d_zhf = S.add("vector", lambda p, dq=dq: p.tensor_copy(zhi_f[:, dq],
                                                               zrh_b[:, dq]),
                      deps=[d_zh], sem="VE", inc=1)
        d_zlo4[th] = S.add("vector", lambda p, dq=dq: p.tensor_tensor(
            zrl_b[:, dq], zrec_v[:, dq], zhi_f[:, dq], OP.subtract),
            deps=[d_zhf], sem="VE", inc=1, tag=f"zlo{th}")
    d_rec = d_zlo4[3]
    d_zlo = d_zlo4[3]

    M_sx = sx_mm[3]
    M_q = q_mm[NCH - 1][3]

    if _DEBUG_TAPS == 3:
        st_total = 0
        dst_e = nc.dram_tensor("d_qt_early", [128, 16384], BF16,
                               kind="ExternalOutput").ap()
        S.add("sync", lambda p: p.dma_start(out=dst_e, in_=A4[:, :]),
              deps=[qtil[NCH - 1][3]], sem="ST", inc=16, tag="tapE")
        st_total += 16
        taps3 = [
            ("d_qt", A4[:, :], BF16),
            ("d_qh", A1[:, :], BF16),
            ("d_den", A2[0:16, 0:8192], BF16),
            ("d_lden", lden[:, :], BF16),
            ("d_exks", exks_v[:, :], F32),
            ("d_kvb", kv_b[:, :], BF16),
            ("d_w0", W0[:, :], BF16),
        ]
        for name, src, dt in taps3:
            dst = nc.dram_tensor(name, list(src.shape), dt,
                                 kind="ExternalOutput").ap()
            S.add("sync", lambda p, dd=dst, ss=src: p.dma_start(out=dd, in_=ss),
                  deps=[("VE", S.counts["VE"]), ("TE", S.counts["TE"]),
                        ("SE", S.counts["SE"]), ("GD", S.counts["GD"])],
                  sem="ST", inc=16, tag=f"tap3.{name}")
            st_total += 16
        S.add("sync", lambda p: p.dma_start(out=out_tm[0:128, :], in_=RB0[:, :]),
              deps=[ld["repbk"]], sem="ST", inc=16)
        st_total += 16
        S.add("sync", lambda p: p.wait_ge(sems["ST"], st_total) and None, tag="final")
        S.finalize()
        with nc.Block() as block:
            block.sync(lambda p: S.emit("sync", p, sems))
            block.gpsimd(lambda p: S.emit("gpsimd", p, sems))
            block.tensor(lambda p: S.emit("tensor", p, sems))
            block.vector(lambda p: S.emit("vector", p, sems))
            block.scalar(lambda p: S.emit("scalar", p, sems))
        ctx.close()
        return nc

    y_ev = [[None] * 4 for _ in range(NCH)]
    prev_y = {}
    prev_z = {}
    prev_zcp = {}
    last_zcp = {}
    for c in range(NCH):
        for th in range(4):
            idx = c * 4 + th
            slot = idx % 4
            po = P[slot]
            zpar = idx % 2
            zp = P[6] if zpar == 0 else P[7]
            zdeps = [d_zlo4[th], gd["zsel"]]
            if zpar in prev_z:
                zdeps += [prev_z[zpar]]
            elif zpar == 0:
                zdeps += [ev_ks]
            else:
                zdeps += [ev_bcw, ev_cs]
            d_zmm = S.add("tensor", lambda p, c=c, th=th, zp=zp: (
                p.matmul(zp[:, :], zsel_b[0:128, 128 * c:128 * (c + 1)],
                         A1[0:128, 8192 + 512 * th:8192 + 512 * (th + 1)],
                         start=True, stop=False),
                p.matmul(zp[:, :], zsel_b[0:128, 128 * c:128 * (c + 1)],
                         A1[0:128, 10240 + 512 * th:10240 + 512 * (th + 1)],
                         start=False, stop=True))[-1],
                deps=zdeps, sem="TE", inc=1, tag=f"zmm{c}.{th}")
            cpdeps = [d_zmm]
            if slot in prev_zcp:
                cpdeps += [prev_zcp[slot]]
            else:
                cpdeps += [M_q, M_sx]
            d_zcp = S.add("scalar", lambda p, slot=slot, zp=zp: p.copy(
                zrep[slot], zp[:, :]), deps=cpdeps, sem="SE", inc=1,
                tag=f"zcp{c}.{th}")
            prev_z[zpar] = d_zcp
            last_zcp[zpar] = d_zcp
            ydeps = [qtil[c][th], d_kvb]
            if slot in prev_y:
                ydeps += [prev_y[slot]]
            else:
                ydeps += [qtil[NCH - 1][slot]]
                if slot >= 2:
                    ydeps += [sx_ev[slot - 2], sx_ev[slot]]
            dn = S.add("tensor", lambda p, c=c, th=th, po=po: p.matmul(
                po[:, :], kv_b[:, 128 * c:128 * (c + 1)],
                A4[:, 2048 * c + 512 * th:2048 * c + 512 * (th + 1)],
                start=True, stop=True),
                deps=ydeps, sem="TE", inc=1, tag=f"y{c}.{th}")
            evdeps = [dn, d_zcp, M_kv]
            if c in (0, 2):
                evdeps += [d_zlo4[th // 2]]
            elif c in (1, 3):
                evdeps += [d_zlo4[2 + th // 2]]
            elif c == 5:
                evdeps += [d_lden[7]]
            ev = S.add("vector", lambda p, c=c, th=th, po=po, slot=slot:
                p.tensor_tensor(
                    A2[:, 2048 * c + 512 * th:2048 * c + 512 * (th + 1)],
                    po[:, :], zrep[slot], OP.mult),
                deps=evdeps, sem="VE", inc=1, tag=f"yev{c}.{th}")
            y_ev[c][th] = ev
            prev_y[slot] = ev
            prev_zcp[slot] = ev
    M_y = y_ev[NCH - 1][3]

    sattn_mm = []
    for ts in range(4):
        po = P[4 + ts // 2]
        reg = po[64 * (ts % 2):64 * (ts % 2) + 64, :]
        deps = [M_y, M_wcw]
        deps += [den[2 + (ts // 2)][1]]
        dn = S.add("tensor", lambda p, ts=ts, reg=reg: mm_group(
            p, reg,
            [(wcw_b[:, 64 * c:64 * (c + 1)],
              A2[:, 2048 * c + 512 * ts:2048 * c + 512 * (ts + 1)])
             for c in range(NCH)]),
            deps=deps, sem="TE", inc=1, tag=f"sattn{ts}")
        sattn_mm.append((dn, reg))

    if _DEBUG_TAPS == 2:
        xld0 = S.add("sync", lambda p: p.dma_start(out=xin[0],
                                                   in_=x_tm[0:128, :]),
                     deps=[ktil[NT - 1]], sem="LD", inc=16, tag="xld0")
        slots = (P[0], P[1])
        def attn_emit0(p):
            ins = None
            for jh in range(2):
                pairs = [(A2[:, 2048 * kc:2048 * kc + 128],
                          W1[:, 1024 * kc + 512 * jh:1024 * kc + 512 * (jh + 1)])
                         for kc in range(NCH)]
                pairs.append((ones_r[:, :], brows_b[0:1, 512 * jh:512 * (jh + 1)]))
                ins = mm_group(p, slots[jh][:, :], pairs)
            return ins
        at0 = S.add("tensor", attn_emit0,
                    deps=wc_all + [M_y, gd["brows"], m_ones_r, qhat[NCH - 1]],
                    sem="TE", inc=1, tag="attn0")
        ev0 = S.add("scalar", lambda p: p.copy(f32_tile(A4, 4)[:, 0:512],
                                               slots[0][:, :]),
                    deps=[at0], sem="SE", inc=1)
        ev1 = S.add("scalar", lambda p: p.copy(f32_tile(A4, 4)[:, 512:1024],
                                               slots[1][:, :]),
                    deps=[at0], sem="SE", inc=1)
        st_total = 0
        taps2 = [
            ("d_y", A2[:, :], BF16, [("SE", S.counts["SE"])]),
            ("d_qh", A1[:, :], BF16, None),
            ("d_qt", A4[:, 0:8192], BF16, None),
            ("d_w1", W1[:, :], BF16, None),
            ("d_attn0", f32_tile(A4, 4)[:, :], F32, [ev0, ev1]),
            ("d_xin0", xin[0][:, :], F32, [xld0]),
            ("d_kvb", kv_b[:, :], BF16, None),
            ("d_brows", brows_b[:, :], BF16, None),
        ]
        for name, src, dt, extra in taps2:
            dst = nc.dram_tensor(name, list(src.shape), dt,
                                 kind="ExternalOutput").ap()
            S.add("sync", lambda p, dd=dst, ss=src: p.dma_start(out=dd, in_=ss),
                  deps=[("VE", S.counts["VE"]), ("TE", S.counts["TE"])] +
                       (extra or []),
                  sem="ST", inc=16, tag=f"tap2.{name}")
            st_total += 16
        S.add("sync", lambda p: p.dma_start(out=out_tm[0:128, :], in_=xin[0][:, :]),
              deps=[xld0], sem="ST", inc=16)
        st_total += 16
        S.add("sync", lambda p: p.wait_ge(sems["ST"], st_total) and None, tag="final")
        S.finalize()
        with nc.Block() as block:
            block.sync(lambda p: S.emit("sync", p, sems))
            block.gpsimd(lambda p: S.emit("gpsimd", p, sems))
            block.tensor(lambda p: S.emit("tensor", p, sems))
            block.vector(lambda p: S.emit("vector", p, sems))
            block.scalar(lambda p: S.emit("scalar", p, sems))
        ctx.close()
        return nc

    xld = [None] * NT
    attn_mm = [None] * NT
    r1d = [None] * NT
    ssq1d = [None] * NT
    prev_at = {}
    prev_r1 = {}
    for t in range(NT):
        if t == 0:
            xdeps = [ktil[NT - 1]]
        elif t == 1:
            xdeps = [vtil[NT - 1]]
        else:
            xdeps = [prev_r1[t % 2]]
        xld[t] = S.add("sync", lambda p, t=t: p.dma_start(
            out=xin[t % 2], in_=x_tm[128 * t:128 * (t + 1), :]),
            deps=xdeps, sem="LD", inc=16, tag=f"xld{t}", new_group=True)
        slots = (P[0], P[1]) if t % 2 == 0 else (P[2], P[3])
        deps = wc_all + [M_y, gd["brows"], m_ones_r]
        if t % 2 in prev_at:
            deps += [prev_at[t % 2]]
        else:
            deps += [M_y]
        def attn_emit(p, t=t, slots=slots):
            ins = None
            for jh in range(2):
                pairs = [(A2[:, 2048 * kc + 128 * t:2048 * kc + 128 * (t + 1)],
                          W1[:, 1024 * kc + 512 * jh:1024 * kc + 512 * (jh + 1)])
                         for kc in range(NCH)]
                pairs.append((ones_r[:, :], brows_b[0:1, 512 * jh:512 * (jh + 1)]))
                ins = mm_group(p, slots[jh][:, :], pairs)
            return ins
        attn_mm[t] = S.add("tensor", attn_emit, deps=deps, sem="TE", inc=1,
                           tag=f"attn{t}")

        def r1_emit(p, t=t, slots=slots):
            p.scalar_tensor_tensor(r1_t(t)[:, 0:512], slots[0][:, :], 1.0,
                                   xin[t % 2][:, 0:512], OP.mult, OP.add,
                                   accum_out=C_sum1a[:, t:t + 1])
            return p.scalar_tensor_tensor(r1_t(t)[:, 512:1024], slots[1][:, :], 1.0,
                                          xin[t % 2][:, 512:1024], OP.mult, OP.add,
                                          accum_out=C_sum1b[:, t:t + 1])
        extra = [attn_mm[t], xld[t]]
        if t < 2:
            extra += [M_kv]
        r1d[t] = S.add("vector", r1_emit, deps=extra, sem="VE", inc=1, tag=f"r1.{t}")
        prev_r1[t % 2] = r1d[t]
        prev_at[t % 2] = r1d[t]
        ssq1d[t] = S.add("scalar", lambda p, t=t: p.activation(
            sqscr[:, 0:1024], r1_t(t)[:, :], AF.Square, accum_out=C_ssq1[:, t:t + 1]),
            deps=[r1d[t]], sem="SE", inc=1, tag=f"ssq1.{t}")

    d_s1 = S.add("vector", lambda p: p.tensor_add(L_sum, C_sum1a, C_sum1b),
                 deps=[r1d[NT - 1]], sem="VE", inc=1)
    d_m1 = S.add("vector", lambda p: p.tensor_scalar_mul(C_m1, L_sum, 1.0 / D),
                 deps=[d_s1], sem="VE", inc=1)
    d_msq = S.add("vector", lambda p: p.tensor_mul(L_msq, C_m1, C_m1),
                  deps=[d_m1], sem="VE", inc=1)
    d_var = S.add("vector", lambda p: p.scalar_tensor_tensor(
        L_var, C_ssq1, 1.0 / D, L_msq, OP.mult, OP.subtract),
        deps=[d_msq, ssq1d[NT - 1]], sem="VE", inc=1)
    d_vpe = S.add("vector", lambda p: p.tensor_scalar_add(L_var, L_var, EPS_LN),
                  deps=[d_var], sem="VE", inc=1)
    d_ri = S.add("vector", lambda p: p.reciprocal(L_rinv, L_var),
                 deps=[d_vpe], sem="VE", inc=1)
    d_rs1 = S.add("scalar", lambda p: p.activation(L_rstd1, L_rinv, AF.Sqrt),
                  deps=[d_ri], sem="SE", inc=1, tag="sqrt1")
    d_mrs1 = S.add("vector", lambda p: p.tensor_mul(L_mrs1, C_m1, L_rstd1),
                   deps=[d_rs1], sem="VE", inc=1)

    d_zmz1 = S.add("vector", lambda p: p.tensor_copy(zmz[:, 0:16], L_rstd1),
                   deps=[d_rs1], sem="VE", inc=1)
    d_zmz2 = S.add("vector", lambda p: p.tensor_copy(zmz[:, 16:32], L_mrs1),
                   deps=[d_mrs1], sem="VE", inc=1)
    d_ztr = S.add("tensor", lambda p: p.transpose(P[6][0:32, 0:128], zmz[:, :],
                                                  ident_s[:, :]),
                  deps=[d_zmz1, d_zmz2, ld["ident"], last_zcp[0]], sem="TE",
                  inc=1, tag="ztr")


    d_zwf = S.add("vector", lambda p: p.tensor_copy(zwf, P[6][0:32, 0:128]),
                  deps=[d_ztr, attn_mm[NT - 1]], sem="VE", inc=1, tag="zwf")
    d_zwlo = d_zwf

    def zbc_emit(p):
        ins = None
        for g in range(8):
            pslot = P[g // 2]
            half = 64 * (g % 2)
            for i in range(4):
                row = 4 * g + i if g < 4 else 4 * (g - 4) + i + 16
                reg = pslot[half:half + 64, 128 * i:128 * (i + 1)]
                ins = p.matmul(
                    reg,
                    ident_s[0:32, row:row + 1].broadcast_to([32, 64]),
                    zwf[0:32, :], start=True, stop=True,
                    skip_group_check=True)
        return ins
    d_zbc = S.add("tensor", zbc_emit,
                  deps=[d_zwf, ld["ident"], r1d[NT - 1]], sem="TE", inc=1,
                  tag="zbc")
    zev = []
    for g in range(8):
        dest = z1rep if g < 4 else mz1rep
        grp = g % 4
        zev.append(S.add("vector", lambda p, g=g, grp=grp, dest=dest: p.tensor_copy(
            dest[:, 512 * grp:512 * (grp + 1)],
            P[g // 2][64 * (g % 2):64 * (g % 2) + 64, 0:512]),
            deps=[d_zbc, q_mm[NCH - 1][3]], sem="VE", inc=1, tag=f"zbcev{g}"))
    d_z1r = zev[3]
    d_mz1r = zev[7]

    sbd = []
    f1d = []
    for ts in range(4):
        reg = sattn_mm[ts][1]
        d_f1 = S.add("vector", lambda p, ts=ts, reg=reg: p.scalar_tensor_tensor(
            t1[:, :], reg, M_bcw, s_x[:, 512 * ts:512 * (ts + 1)], OP.add, OP.add),
            deps=[sattn_mm[ts][0], ev_bcw, sx_ev[ts], r1d[NT - 1]], sem="VE", inc=1,
            tag=f"f1.{ts}")
        d_f2 = S.add("vector", lambda p, ts=ts: p.tensor_tensor(
            t2[:, :], t1[:, :], z1rep[:, 512 * ts:512 * (ts + 1)], OP.mult),
            deps=[d_f1, d_z1r], sem="VE", inc=1, tag=f"f2.{ts}")
        d_f3 = S.add("vector", lambda p, ts=ts: p.tensor_scalar_mul(
            t1[:, :], mz1rep[:, 512 * ts:512 * (ts + 1)], M_cs),
            deps=[d_f2, d_mz1r, ev_cs], sem="VE", inc=1, tag=f"f3.{ts}")
        d_f4 = S.add("vector", lambda p, ts=ts: p.scalar_tensor_tensor(
            s_b[:, 512 * ts:512 * (ts + 1)], t2[:, :], M_bin, t1[:, :],
            OP.add, OP.subtract),
            deps=[d_f3, ld["bin"], d_rec, attn_mm[NT - 1]], sem="VE", inc=1,
            tag=f"f4.{ts}")
        sbd.append(d_f4)
        f1d.append(d_f1)

    x2d = [None] * NT
    for t in range(NT):
        deps = [d_mrs1, r1d[t]]
        if t < 8:
            deps += [M_q, M_sx]
        else:
            deps += [x2d[t - 8], ssq1d[t - 8]]
        x2d[t] = S.add("vector", lambda p, t=t: p.tensor_scalar(
            x2_t(t)[:, :], r1_t(t)[:, :], L_rstd1[:, t:t + 1], L_mrs1[:, t:t + 1],
            OP.mult, OP.subtract),
            deps=deps, sem="VE", inc=1, tag=f"x2.{t}")


    gdone = [None] * 4
    r2d = [None] * NT
    ssq2d = [None] * NT
    prev_f = {}

    def emit_G(qa, qb, c, hA, hB, gA, gB, pair_idx):
        depsA = [hA[c], gd["bdw2"]]
        if c == 0:
            depsA += [d_zwlo] if pair_idx == 0 else [gdone[0], gdone[1]]
        gA[c] = S.add("tensor", lambda p, c=c: p.matmul(
            P[6][0:64, :], bdw2_b[:, 64 * c:64 * (c + 1)], h_b[c % 2][:, :],
            start=(c == 0), stop=(c == NCH - 1), skip_group_check=True),
            deps=depsA, sem="TE", inc=1, tag=f"GA.{c}")
        depsB = [hB[c], gd["bdw2"]]
        if c == 0:
            depsB += [d_zwlo] if pair_idx == 0 else [gdone[0], gdone[1]]
        gB[c] = S.add("tensor", lambda p, c=c: p.matmul(
            P[6][64:128, :], bdw2_b[:, 64 * c:64 * (c + 1)], h_b[2 + c % 2][:, :],
            start=(c == 0), stop=(c == NCH - 1), skip_group_check=True),
            deps=depsB, sem="TE", inc=1, tag=f"GB.{c}")

    def emit_ffn_tile(t, deep=False):
        if deep:
            slots = (P[2 * (t % 4)], P[2 * (t % 4) + 1])
            key = t % 4
        else:
            slots = (P[0], P[1])
            key = 0
        deps = [gdone[t // 4], gd["wout"], gd["brows"], m_ones_r]
        if key in prev_f:
            deps += [prev_f[key]]
        else:
            deps += [r1d[NT - 1]]
        if deep and t % 4 >= 1:
            deps += [hA_pp[NCH - 1], hB_pp[NCH - 1], gdone[2], gdone[3]]
        def ffn_emit(p, t=t, slots=slots):
            ins = None
            for jh in range(2):
                pairs = [(g_b[:, 128 * t:128 * (t + 1)],
                          wout_b[0:64, 512 * jh:512 * (jh + 1)]),
                         (ones_r[:, :],
                          brows_b[0:1, 1024 + 512 * jh:1024 + 512 * (jh + 1)])]
                ins = mm_group(p, slots[jh][:, :], pairs)
            return ins
        d_ffn = S.add("tensor", ffn_emit, deps=deps, sem="TE", inc=1, tag=f"ffn{t}")

        def r2_emit(p, t=t, slots=slots):
            p.scalar_tensor_tensor(r2_t(t)[:, 0:512], slots[0][:, :], 1.0,
                                   x2_t(t)[:, 0:512], OP.mult, OP.add,
                                   accum_out=C_sum2a[:, t:t + 1])
            return p.scalar_tensor_tensor(r2_t(t)[:, 512:1024], slots[1][:, :], 1.0,
                                          x2_t(t)[:, 512:1024], OP.mult, OP.add,
                                          accum_out=C_sum2b[:, t:t + 1])
        extra = [d_ffn, x2d[t]]
        if t < 8:
            extra += [x2d[8 + t], ssq1d[8 + t]]
        else:
            extra += [r2d[t - 8]]
        r2d[t] = S.add("vector", r2_emit, deps=extra, sem="VE", inc=1, tag=f"r2.{t}")
        prev_f[key] = r2d[t]
        ssq2d[t] = S.add("scalar", lambda p, t=t: p.activation(
            sqscr[:, 0:1024], r2_t(t)[:, :], AF.Square, accum_out=C_ssq2[:, t:t + 1]),
            deps=[r2d[t]], sem="SE", inc=1, tag=f"ssq2.{t}")

    hA_prev_pair = hB_prev_pair = None
    for pair_idx, (qa, qb) in enumerate(((0, 1), (2, 3))):
        hA = [None] * NCH; hB = [None] * NCH
        gA = [None] * NCH; gB = [None] * NCH
        sA = [None] * NCH; sB = [None] * NCH
        prev_hA = {}; prev_hB = {}
        for c in range(NCH):
            psA = P[4 + (c % 2)]
            depsA = [sbd[qa], gd["selS"]] + ([f1d[0], f1d[1]] if c % 2 == 0
                                             else [f1d[2], f1d[3]])
            if (c % 2) in prev_hA:
                depsA += [prev_hA[c % 2]]
            elif pair_idx == 1:
                depsA += [hA_prev_pair[6 + (c % 2)]]
            sA[c] = S.add("tensor", lambda p, qa=qa, c=c, psA=psA: p.matmul(
                psA[:, :], selS_b[:, 128 * c:128 * (c + 1)],
                s_b[0:64, 512 * qa:512 * (qa + 1)], start=True, stop=True,
                skip_group_check=True),
                deps=depsA, sem="TE", inc=1, tag=f"srep{qa}.{c}")
            psB = P[7] if c % 2 == 0 else P[3]
            depsB = [sbd[qb], gd["selS"]]
            if (c % 2) in prev_hB:
                depsB += [prev_hB[c % 2]]
            elif pair_idx == 1:
                depsB += [hB_prev_pair[6 + (c % 2)]]
            if pair_idx == 0 and c % 2 == 1:
                depsB += [r1d[NT - 1]]
            sB[c] = S.add("tensor", lambda p, qb=qb, c=c, psB=psB: p.matmul(
                psB[:, :], selS_b[:, 128 * c:128 * (c + 1)],
                s_b[0:64, 512 * qb:512 * (qb + 1)], start=True, stop=True,
                skip_group_check=True),
                deps=depsB, sem="TE", inc=1, tag=f"srep{qb}.{c}")
            if pair_idx == 1:
                emit_ffn_tile(c)
            if c > 0:
                emit_G(qa, qb, c - 1, hA, hB, gA, gB, pair_idx)
            hA[c] = S.add("scalar", lambda p, c=c, psA=psA: p.activation(
                h_b[c % 2][:, :], psA[:, :], AF.Relu,
                bias=C_b1[:, c:c + 1], scale=C_w1[:, c:c + 1]),
                deps=[sA[c], ld["w1"], ld["b1"]] + ([gA[c - 2]] if c >= 2 else []) +
                     ([sbd[3]] if pair_idx == 0 and c < 2 else []),
                sem="SE", inc=1, tag=f"h{qa}.{c}")
            prev_hA[c % 2] = hA[c]
            hB[c] = S.add("scalar", lambda p, c=c, psB=psB: p.activation(
                h_b[2 + c % 2][:, :], psB[:, :], AF.Relu,
                bias=C_b1[:, c:c + 1], scale=C_w1[:, c:c + 1]),
                deps=[sB[c], ld["w1"], ld["b1"]] + ([gB[c - 2]] if c >= 2 else []) +
                     ([sbd[3]] if pair_idx == 0 and c < 2 else []),
                sem="SE", inc=1, tag=f"h{qb}.{c}")
            prev_hB[c % 2] = hB[c]
        emit_G(qa, qb, NCH - 1, hA, hB, gA, gB, pair_idx)
        gdone[qa] = S.add("vector", lambda p, qa=qa: p.tensor_scalar_add(
            g_b[:, 512 * qa:512 * (qa + 1)], P[6][0:64, :], M_b2),
            deps=[gA[NCH - 1], ld["b2"]], sem="VE", inc=1, tag=f"g{qa}")
        gdone[qb] = S.add("vector", lambda p, qb=qb: p.tensor_scalar_add(
            g_b[:, 512 * qb:512 * (qb + 1)], P[6][64:128, :], M_b2),
            deps=[gB[NCH - 1], ld["b2"]], sem="VE", inc=1, tag=f"g{qb}")
        hA_prev_pair = hA
        hB_prev_pair = hB

    hA_pp, hB_pp = hA_prev_pair, hB_prev_pair

    g_mrs2 = [None] * 4
    g_mrs2n = [None] * 4
    prev_st = {}

    def emit_ln2_group(g):
        c4 = slice(4 * g, 4 * (g + 1))
        scr = L_w2[:, 4 * g:4 * (g + 1)]
        deps = [r2d[t] for t in range(4 * g, 4 * g + 4)]
        d1 = S.add("vector", lambda p: p.tensor_add(
            scr, C_sum2a[:, c4], C_sum2b[:, c4]), deps=deps, sem="VE", inc=1,
            tag=f"ln2a{g}")
        d2 = S.add("vector", lambda p: p.tensor_scalar_mul(
            L_m2[:, c4], scr, 1.0 / D), deps=[d1], sem="VE", inc=1)
        d3 = S.add("vector", lambda p: p.tensor_mul(
            scr, L_m2[:, c4], L_m2[:, c4]), deps=[d2], sem="VE", inc=1)
        d4 = S.add("vector", lambda p: p.scalar_tensor_tensor(
            scr, C_ssq2[:, c4], 1.0 / D, scr, OP.mult, OP.subtract),
            deps=[d3] + [ssq2d[t] for t in range(4 * g, 4 * g + 4)],
            sem="VE", inc=1)
        d5 = S.add("vector", lambda p: p.tensor_scalar_add(scr, scr, EPS_LN),
                   deps=[d4], sem="VE", inc=1)
        d6 = S.add("vector", lambda p: p.reciprocal(scr, scr),
                   deps=[d5], sem="VE", inc=1)
        d7 = S.add("scalar", lambda p: p.activation(
            L_rstd2[:, c4], scr, AF.Sqrt), deps=[d6], sem="SE", inc=1,
            tag=f"sqrt2.{g}")
        d8 = S.add("vector", lambda p: p.tensor_mul(
            L_mrs2[:, c4], L_m2[:, c4], L_rstd2[:, c4]),
            deps=[d7], sem="VE", inc=1)
        d9 = S.add("vector", lambda p: p.tensor_scalar_mul(
            L_mrs2n[:, c4], L_mrs2[:, c4], -1.0), deps=[d8], sem="VE", inc=1)
        g_mrs2[g] = d8
        g_mrs2n[g] = d9

    def emit_out(t):
        g = t // 4
        deps = [g_mrs2[g], g_mrs2n[g]]
        if t % 4 in prev_st:
            deps += [prev_st[t % 4]]
        else:
            deps += [attn_mm[NT - 1], sattn_mm[3][0], d_zbc]
        if t % 2 == 0:
            d_o = S.add("vector", lambda p, t=t: p.tensor_scalar(
                outst[t % 4][:, :], r2_t(t)[:, :], L_rstd2[:, t:t + 1],
                L_mrs2[:, t:t + 1], OP.mult, OP.subtract),
                deps=deps, sem="VE", inc=1, tag=f"out.{t}")
        else:
            d_o = S.add("scalar", lambda p, t=t: p.activation(
                outst[t % 4][:, :], r2_t(t)[:, :], AF.Identity,
                bias=L_mrs2n[:, t:t + 1], scale=L_rstd2[:, t:t + 1]),
                deps=deps, sem="SE", inc=1, tag=f"out.{t}")
        eng, stsem = [("sync", "ST"), ("scalar", "SD"),
                      ("sync", "ST"), ("scalar", "SD")][t % 4]
        d_st = S.add(eng, lambda p, t=t: p.dma_start(
            out=out_tm[128 * t:128 * (t + 1), :], in_=outst[t % 4][:, :]),
            deps=[d_o], sem=stsem, inc=16, tag=f"st{t}", new_group=True)
        prev_st[t % 4] = d_st

    emit_ffn_tile(8, deep=True); emit_ffn_tile(9, deep=True)
    emit_ln2_group(0)
    emit_out(0); emit_out(1)
    emit_ffn_tile(10, deep=True); emit_ffn_tile(11, deep=True)
    emit_out(2); emit_out(3)
    emit_ln2_group(1)
    emit_ffn_tile(12, deep=True); emit_ffn_tile(13, deep=True)
    emit_out(4); emit_out(5)
    emit_ffn_tile(14, deep=True); emit_ffn_tile(15, deep=True)
    emit_out(6); emit_out(7)
    emit_ln2_group(2)
    for t in (8, 9, 10, 11):
        emit_out(t)
    emit_ln2_group(3)
    for t in (12, 13, 14, 15):
        emit_out(t)

    if _DEBUG_TAPS:
        outst[2] = outst[0]
        outst[3] = outst[1]
        tap_list = [
            ("d_sb", s_b[:, :], BF16, [gdone[3], r2d[NT - 1]]),
            ("d_gb", g_b[:, :], BF16, [gdone[3], r2d[NT - 1]]),
            ("d_r2_0", r2_t(0)[:, :], F32, [r2d[0], ("VE", S.counts["VE"])]),
            ("d_r2_8", r2_t(8)[:, :], F32, [r2d[8], ("VE", S.counts["VE"])]),
            ("d_r2_12", r2_t(12)[:, :], F32, [r2d[12], ("VE", S.counts["VE"])]),
            ("d_x2_8", x2_t(8)[:, :], F32, [("VE", S.counts["VE"])]),
            ("d_colsB", colsB[:, :], F32, [("VE", S.counts["VE"]),
                                           ("SE", S.counts["SE"])]),
            ("d_LW2", LW2[:, :], F32, [("VE", S.counts["VE"])]),
        ]
        for name, srcv, dt, deps in tap_list:
            dst = nc.dram_tensor(name, list(srcv.shape), dt,
                                 kind="ExternalOutput").ap()
            S.add("sync", lambda p, dd=dst, ss=srcv: p.dma_start(out=dd, in_=ss),
                  deps=deps, sem="ST", inc=16, tag=f"tap.{name}", new_group=True)

    def final_waits(p):
        for s in ("ST", "SD"):
            p.wait_ge(sems[s], S.counts.get(s, 0))
    S.add("sync", final_waits, tag="final")

    S.finalize()
    with nc.Block() as block:
        @block.sync
        def _(p):
            S.emit("sync", p, sems)

        @block.gpsimd
        def _(p):
            S.emit("gpsimd", p, sems)

        @block.tensor
        def _(p):
            S.emit("tensor", p, sems)

        @block.vector
        def _(p):
            S.emit("vector", p, sems)

        @block.scalar
        def _(p):
            S.emit("scalar", p, sems)

    ctx.close()
    return nc


def _consts():
    maskC = np.zeros((128, 128), dtype=np.float32)
    for c in range(NCH):
        for p in range(128):
            maskC[p, 16 * c + 2 * c + p // 64] = 1.0
    blockmask = np.zeros((128, 128), dtype=np.float32)
    for d in range(128):
        blockmask[d, (d // 64) * 64:(d // 64) * 64 + 64] = 1.0
    ident = np.eye(128, dtype=np.float32)
    selS = np.zeros((NCH, M, 128), dtype=np.float32)
    for c in range(NCH):
        for k in range(128):
            selS[c, (128 * c + k) // INNER, k] = 1.0
    zsel = np.zeros((NCH, 128, 128), dtype=np.float32)
    for c in range(NCH):
        for p in range(128):
            zsel[c, 2 * c + p // 64, p] = 1.0
    rsel = np.zeros((32, 32, M), dtype=np.float32)
    for r in range(32):
        rsel[r, r, :] = 1.0
    return maskC, blockmask, ident, selS, zsel, rsel


_NC_CACHE = {}

def kernel(L0_attn_x, L0_attn_Wq, L0_attn_bq, L0_attn_Wk, L0_attn_bk,
           L0_attn_Wv, L0_attn_bv, L0_attn_Wc, L0_attn_bc,
           L0_ffn_Win, L0_ffn_bin_, L0_ffn_W1, L0_ffn_b1, L0_ffn_W2,
           L0_ffn_b2, L0_ffn_Wout, L0_ffn_bout):
    x = np.ascontiguousarray(np.asarray(L0_attn_x, dtype=np.float32))
    Wq = np.asarray(L0_attn_Wq, np.float32); Wk = np.asarray(L0_attn_Wk, np.float32)
    Wv = np.asarray(L0_attn_Wv, np.float32); Wc = np.asarray(L0_attn_Wc, np.float32)
    bq = np.asarray(L0_attn_bq, np.float32); bk = np.asarray(L0_attn_bk, np.float32)
    bv = np.asarray(L0_attn_bv, np.float32); bc = np.asarray(L0_attn_bc, np.float32)
    Win = np.asarray(L0_ffn_Win, np.float32); bin_ = np.asarray(L0_ffn_bin_, np.float32)
    W1 = np.asarray(L0_ffn_W1, np.float32); b1 = np.asarray(L0_ffn_b1, np.float32)
    W2 = np.asarray(L0_ffn_W2, np.float32); b2 = np.asarray(L0_ffn_b2, np.float32)
    Wout = np.asarray(L0_ffn_Wout, np.float32); bout = np.asarray(L0_ffn_bout, np.float32)

    maskC, blockmask, ident, selS, zsel, rsel = _consts()

    bdw2 = np.zeros((NCH, 128, M), dtype=np.float32)
    flat = np.arange(128)
    for c in range(NCH):
        idx = 128 * c + flat
        bdw2[c, flat, idx // INNER] = W2[idx // INNER, idx % INNER]

    shared = dict(
        wqT=np.ascontiguousarray(Wq.T), wkT=np.ascontiguousarray(Wk.T),
        wvT=np.ascontiguousarray(Wv.T), wcT=np.ascontiguousarray(Wc.T),
        wc_nat=np.ascontiguousarray(Wc),
        win=np.ascontiguousarray(Win), wout=np.ascontiguousarray(Wout),
        bdw2=bdw2, selS=selS,
        rep_bk=np.ascontiguousarray(np.tile(bk[None, :], (128, 1))),
        rep_bv=np.ascontiguousarray(np.tile(bv[None, :], (128, 1))),
        bq_col=np.ascontiguousarray(bq.reshape(NCH, 128).T),
        bc_col=np.ascontiguousarray(bc.reshape(NCH, 128).T),
        brows=np.ascontiguousarray(np.concatenate([bc, bout])[None, :]),
        bin_col=np.ascontiguousarray(bin_.reshape(M, 1)),
        b2_col=np.ascontiguousarray(b2.reshape(M, 1)),
        w1_col=np.ascontiguousarray(W1.reshape(-1).reshape(NCH, 128).T),
        b1_col=np.ascontiguousarray(b1.reshape(-1).reshape(NCH, 128).T),
        maskC=maskC, blockmask=blockmask, ident=ident, zsel=zsel,
)

    in_maps = []
    for core in range(NCORES):
        b_i, half = core // 2, core % 2
        xs = x[b_i, half * TOK:(half + 1) * TOK, :]
        m = dict(shared)
        m["x_tm"] = np.ascontiguousarray(xs)
        m["x_fm"] = np.ascontiguousarray(xs.T)
        in_maps.append(m)

    if "nc" not in _NC_CACHE:
        _NC_CACHE["nc"] = build_nc()
    nc = _NC_CACHE["nc"]
    res = run_bass_kernel_spmd(nc, in_maps, core_ids=list(range(NCORES)))
    out = np.empty((B, T, D), dtype=np.float32)
    for core in range(NCORES):
        b_i, half = core // 2, core % 2
        out[b_i, half * TOK:(half + 1) * TOK, :] = res.results[core]["out_tm"]
    return out
